# version 14
# baseline (speedup 1.0000x reference)
import numpy as np
import ml_dtypes
import concourse.bass as bass
import concourse.mybir as mybir
from concourse.bass_utils import run_bass_kernel_spmd

F32 = mybir.dt.float32
BF16 = mybir.dt.bfloat16
I32 = mybir.dt.int32
AF = mybir.ActivationFunctionType
ALU = mybir.AluOpType

B, S, D = 16, 2048, 1024
NCORES = 8
SPC = B // NCORES
NH, DK, NPAIR = 16, 64, 8
NT = S // 128
NQ = S // 512
NKT = S // 128
NPAIR_G = (S // 512) * (S // 128)
DFF = 4096
EXP_SHIFT = -14.0


def _codywaite():
    v = np.float64(2 * np.pi)

    def trunc12(x):
        i = np.frombuffer(np.float32(x).tobytes(), dtype=np.uint32)[0]
        i &= np.uint32(0xFFFFF000)
        return float(np.frombuffer(np.uint32(i).tobytes(), dtype=np.float32)[0])

    c1 = trunc12(v)
    c2 = trunc12(v - c1)
    c3 = float(np.float32(v - c1 - c2))
    return c1, c2, c3


CW1, CW2, CW3 = _codywaite()
INV2PI = float(np.float32(1.0 / (2 * np.pi)))


class Prog:
    ENGINES = ("tensor", "scalar", "vector", "sync", "gpsimd")
    COMPUTE_LANES = ("mm", "act", "dve", "gpc")
    DMA_LANES = ("x0a", "x0b", "pea", "peb", "wv", "wq", "wk", "wo", "w1",
                 "w2", "tx0", "tx1", "tx2", "tx3", "ty0", "ty1", "ty2", "ty3",
                 "otb", "y1st", "y1ld", "out", "sta", "stb", "pest", "sc",
                 "cst", "bcv", "bcg", "bcbt", "bcb2", "dbg")

    def __init__(self):
        self.sched = {e: [] for e in self.ENGINES}
        self.cnt = {k: 0 for k in self.COMPUTE_LANES + self.DMA_LANES}
        self.marks = {}

    def at(self, tag):
        assert tag in self.marks, f"wait on undefined tag {tag}"
        return self.marks[tag]

    def has(self, tag):
        return tag in self.marks

    def emit(self, engine, lane, amt, fn, waits=(), tag=None):
        w = [self.at(t) if isinstance(t, str) else t for t in waits]
        byl = {}
        for (lk, lv) in w:
            byl[lk] = max(byl.get(lk, 0), lv)
        if tag is None and amt == 1:
            amt = 0
        self.sched[engine].append((fn, sorted(byl.items()), lane, amt))
        if amt:
            self.cnt[lane] += amt
            if tag:
                self.marks[tag] = (lane, self.cnt[lane])

    def mm(self, fn, waits=(), tag=None):
        self.emit("tensor", "mm", 1, fn, waits, tag)

    def act(self, fn, waits=(), tag=None):
        self.emit("scalar", "act", 1, fn, waits, tag)

    def dve(self, fn, waits=(), tag=None):
        self.emit("vector", "dve", 1, fn, waits, tag)

    def gp(self, fn, waits=(), tag=None):
        self.emit("gpsimd", "gpc", 1, fn, waits, tag)

    def dma(self, engine, lane, fn, waits=(), tag=None):
        self.emit(engine, lane, 16, fn, waits, tag)

    def barrier_tagged(self, prefix):
        for k, v in self.cnt.items():
            self.marks[f"{prefix}.{k}"] = (k, v)

    def gate(self, prefix):
        return [f"{prefix}.{k}" for k in self.cnt if self.cnt[k] > 0]

    def wait_only(self, engine, waits):
        w = [self.at(t) if isinstance(t, str) else t for t in waits]
        byl = {}
        for (lk, lv) in w:
            byl[lk] = max(byl.get(lk, 0), lv)
        self.sched[engine].append((None, sorted(byl.items()), None, 0))


def build_program(debug_outputs=False):
    nc = bass.Bass("TRN2", target_bir_lowering=False)

    x0_d = nc.dram_tensor("x0", [SPC * S, D], F32, kind="ExternalInput")
    rby_d = nc.dram_tensor("rby", [1, 3], F32, kind="ExternalInput")
    wqT_d = nc.dram_tensor("wqT", [D, D], F32, kind="ExternalInput")
    wkT_d = nc.dram_tensor("wkT", [D, D], F32, kind="ExternalInput")
    wvT_d = nc.dram_tensor("wvT", [D, D], F32, kind="ExternalInput")
    woT_d = nc.dram_tensor("woT", [D, D], F32, kind="ExternalInput")
    w1_d = nc.dram_tensor("w1", [D, DFF], F32, kind="ExternalInput")
    w2_d = nc.dram_tensor("w2", [DFF, D], F32, kind="ExternalInput")
    bv_d = nc.dram_tensor("bv", [1, D], F32, kind="ExternalInput")
    bo_d = nc.dram_tensor("bo", [1, D], F32, kind="ExternalInput")
    b2_d = nc.dram_tensor("b2", [1, D], F32, kind="ExternalInput")
    gamma_d = nc.dram_tensor("gamma", [1, D], F32, kind="ExternalInput")
    beta_d = nc.dram_tensor("beta", [1, D], F32, kind="ExternalInput")
    bqc_d = nc.dram_tensor("bqc", [128, 8], F32, kind="ExternalInput")
    bkc_d = nc.dram_tensor("bkc", [128, 8], F32, kind="ExternalInput")
    b1c_d = nc.dram_tensor("b1c", [128, 32], F32, kind="ExternalInput")
    pos_d = nc.dram_tensor("pos", [128, NT], F32, kind="ExternalInput")
    invd_d = nc.dram_tensor("invd", [1, D], F32, kind="ExternalInput")
    phase_d = nc.dram_tensor("phase", [1, D], F32, kind="ExternalInput")
    eye_d = nc.dram_tensor("eye", [128, 128], BF16, kind="ExternalInput")

    out_d = nc.dram_tensor("out", [SPC * S, D], F32, kind="ExternalOutput")

    pe_b = nc.dram_tensor("pe_b", [NT, 128, D], BF16, kind="Internal")
    peb_b = nc.dram_tensor("peb_b", [NT, 128, D], BF16, kind="Internal")
    y1_b = nc.dram_tensor("y1_b", [NT, 128, D], F32, kind="Internal")
    scl_hbm = nc.dram_tensor("scl_hbm", [1, 4], F32, kind="Internal")

    dbg = {}
    if debug_outputs:
        dbg["xpT"] = nc.dram_tensor("dbg_xpT", [8, 128, S], BF16, kind="ExternalOutput")
        dbg["v0"] = nc.dram_tensor("dbg_v0", [128, 1040], BF16, kind="ExternalOutput")
        dbg["ot"] = nc.dram_tensor("dbg_ot", [8, 128, S], BF16, kind="ExternalOutput")
        dbg["y1"] = nc.dram_tensor("dbg_y1", [NT, 128, D], F32, kind="ExternalOutput")
        dbg["pe"] = nc.dram_tensor("dbg_pe", [NT, 128, D], BF16, kind="ExternalOutput")
        dbg["z15"] = nc.dram_tensor("dbg_z15", [128, D], F32, kind="ExternalOutput")
        dbg["yl15"] = nc.dram_tensor("dbg_yl15", [128, D], F32, kind="ExternalOutput")
        dbg["h28"] = nc.dram_tensor("dbg_h28", [128, 512], BF16, kind="ExternalOutput")
        dbg["y1T0"] = nc.dram_tensor("dbg_y1T0", [128, S], BF16, kind="ExternalOutput")
        dbg["y1bf1"] = nc.dram_tensor("dbg_y1bf1", [128, D], BF16, kind="ExternalOutput")
        dbg["y1T0b"] = nc.dram_tensor("dbg_y1T0b", [128, 128], BF16, kind="ExternalOutput")

    def sb(name, shape, dtype, off):
        return nc.alloc_sbuf_tensor_at(name, shape, dtype, offset=off)

    BASE = 16512
    OFF_SMALL = BASE + 0
    OFF_H = BASE + 2048
    OFF_XPT = BASE + 15360
    OFF_W1R = BASE + 48128
    OFF_W2R = BASE + 113664
    OFF_HR = BASE + 179200

    ones64 = sb("ones64", [1, 64], BF16, OFF_SMALL + 0)
    expbias = sb("expbias", [128, 1], F32, OFF_SMALL + 128)
    zerocol = sb("zerocol", [128, 1], F32, OFF_SMALL + 160)
    eps5col = sb("eps5col", [128, 1], F32, OFF_SMALL + 192)
    bqcol = sb("bqcol", [128, 8], F32, OFF_SMALL + 224)
    bkcol = sb("bkcol", [128, 8], F32, OFF_SMALL + 256)
    b1col = sb("b1col", [128, 32], F32, OFF_SMALL + 288)
    poscol = sb("poscol", [128, NT], F32, OFF_SMALL + 416)
    sccol = sb("sccol", [128, 4], F32, OFF_SMALL + 480)
    rbysb = sb("rbysb", [1, 4], F32, OFF_SMALL + 512)
    scal = sb("scal", [1, 8], F32, OFF_SMALL + 544)
    statc = sb("statc", [128, 24], F32, OFF_SMALL + 576)
    eyesb = sb("eyesb", [128, 128], BF16, OFF_SMALL + 672)

    x0t = [sb(f"x0t{i}", [128, D], F32, OFF_H + i * 4096) for i in range(2)]
    xpbf = [sb(f"xpbf{i}", [128, D], BF16, OFF_H + 8192 + i * 2048) for i in range(2)]
    bv_bc = sb("bv_bc", [128, D], F32, OFF_H + 0)
    gamma_bc = sb("gamma_bc", [128, D], F32, OFF_H + 8192)
    out_st = [sb(f"out_st{i}", [128, D], F32, OFF_H + i * 4096) for i in range(2)]
    y1b2t = sb("y1b2t", [128, D], F32, OFF_H + 8192)
    lnrbuf = sb("lnrbuf", [1, 1024], F32, OFF_H + 4096)
    rinvbf = sb("rinvbf", [1, 1024], BF16, OFF_H + 11264)

    XPT = [sb(f"XPT{i}", [128, S], BF16, OFF_XPT + i * 4096) for i in range(8)]
    Y1T = [sb(f"Y1T{i}", [128, S], BF16, OFF_XPT + i * 4096) for i in range(8)]

    W1R = OFF_W1R
    Vt = [sb(f"Vt{i}", [128, 1040], BF16, W1R + i * 2080) for i in range(NT)]
    KTzp = [sb(f"KTzp{i}", [128, 2 * S], BF16, W1R + 33280 + i * 8192) for i in range(2)]
    QTp = [sb(f"QTp{i}", [128, S], BF16, W1R + 49664 + i * 4096) for i in range(2)]
    Ebuf = [sb(f"Ebuf{i}", [128, 1024], BF16, W1R + 57856 + i * 2048) for i in range(2)]
    Ebuf.append(sb("Ebuf2", [128, 1024], BF16, OFF_H + 0))
    OTBst = sb("OTBst", [64, 512], BF16, W1R + 61952)
    Rinv = sb("Rinv", [64, 512], F32, W1R + 62976)
    w1sb = [sb(f"w1sb{i}", [128, DFF], BF16, W1R + i * 8192) for i in range(8)]
    betab = sb("betab", [128, D], F32, W1R + 49152)
    b2b = sb("b2b", [128, D], F32, W1R + 53248)
    invd_bc = sb("invd_bc", [128, D], F32, W1R + 0)
    phase_bc = sb("phase_bc", [128, D], F32, W1R + 4096)
    ang = sb("ang", [128, D], F32, W1R + 8192)
    angk = sb("angk", [128, D], F32, W1R + 12288)
    angki = sb("angki", [128, D], I32, W1R + 16384)
    angr = sb("angr", [128, D], F32, W1R + 20480)
    pet = sb("pet", [128, D], BF16, W1R + 24576)
    pebt0 = sb("pebt0", [128, D], BF16, W1R + 26624)
    bo_bc = sb("bo_bc", [128, D], F32, W1R + 28672)

    W2R = OFF_W2R
    wvTsb = [sb(f"wvTsb{i}", [128, D], BF16, W2R + i * 2048) for i in range(8)]
    wqTsb = [sb(f"wqTsb{i}", [128, D], BF16, W2R + i * 2048) for i in range(8)]
    wkTsb = [sb(f"wkTsb{i}", [128, D], BF16, W2R + 16384 + i * 2048) for i in range(8)]
    xpe = [sb(f"xpe{i}", [128, D], F32, W2R + i * 4096) for i in range(2)]
    xpb = [sb(f"xpb{i}", [128, D], F32, W2R + 8192 + i * 4096) for i in range(2)]
    OT = [sb(f"OT{i}", [128, S], BF16, W2R + 32768 + i * 4096) for i in range(8)]
    w2sb = [sb(f"w2sb{i}", [128, D], BF16, W2R + i * 2048) for i in range(32)]

    HR = OFF_HR
    woTsb = [sb(f"woTsb{i}", [128, D], BF16, HR + i * 2048) for i in range(8)]
    pebt = [sb(f"pebt{i}", [128, D], BF16, HR + 16384 + i * 2048) for i in range(2)]
    ubuf = sb("ubuf", [128, D], F32, HR + 20480)
    ubuf2 = sb("ubuf2", [128, D], F32, W1R + 57856)
    vnbuf = sb("vnbuf", [128, D], F32, HR + 24576)
    y1bf = [sb(f"y1bf{i}", [128, D], BF16, HR + 28672 + i * 2048) for i in range(2)]
    hsb = [sb(f"hsb{i}", [128, 512], BF16, HR + i * 1024) for i in range(32)]

    PS = [nc.alloc_psum_tensor(f"PS{i}", [128, 1024], F32) for i in range(4)]

    P = Prog()
    sem = {}
    for name in P.COMPUTE_LANES + P.DMA_LANES:
        sem[name] = nc.semaphore(f"sem_{name}", None).__enter__()

    SY, SC, GP = "sync", "scalar", "gpsimd"

    def p0():
        P.dma(GP, "cst", lambda g: g.dma_start(rbysb[:, 0:3], rby_d.ap()))
        P.dma(GP, "cst", lambda g: g.dma_start(invd_bc[:, :], invd_d.ap().partition_broadcast(128)))
        P.dma(GP, "cst", lambda g: g.dma_start(phase_bc[:, :], phase_d.ap().partition_broadcast(128)))
        P.dma(GP, "cst", lambda g: g.dma_start(poscol[:, :], pos_d.ap()))
        P.dma(GP, "cst", lambda g: g.dma_start(bqcol[:, :], bqc_d.ap()))
        P.dma(GP, "cst", lambda g: g.dma_start(bkcol[:, :], bkc_d.ap()))
        P.dma(GP, "cst", lambda g: g.dma_start(b1col[:, :], b1c_d.ap()))
        P.dma(GP, "cst", lambda g: g.dma_start(bo_bc[:, :], bo_d.ap().partition_broadcast(128)))
        P.dma(GP, "cst", lambda g: g.dma_start(eyesb[:, :], eye_d.ap()), (), "csts")

        P.gp(lambda g: g.memset(ones64[:, :], 1.0), (), "ms.ones")
        P.gp(lambda g: g.memset(expbias[:, :], EXP_SHIFT), (), "ms.eb")
        P.gp(lambda g: g.memset(zerocol[:, :], 0.0), (), "ms.z")
        P.gp(lambda g: g.memset(eps5col[:, :], 1e-5), (), "ms.e5")

        r_, b_, y_ = rbysb[0:1, 0:1], rbysb[0:1, 1:2], rbysb[0:1, 2:3]

        def ts1(out, in_, s1, op0, s2=None, op1=None):
            if s2 is None:
                return lambda v: v.tensor_scalar(out, in_, s1, None, op0)
            return lambda v: v.tensor_scalar(out, in_, s1, s2, op0, op1)

        w = ["csts"]
        P.dve(ts1(scal[0:1, 3:4], r_, 0.0125, ALU.mult, 0.125, ALU.add), w, "sc.a")
        P.dve(ts1(scal[0:1, 4:5], b_, 0.00625, ALU.mult), w, "sc.b")
        P.dve(ts1(scal[0:1, 5:6], y_, 0.01875, ALU.mult), w, "sc.c")
        P.dve(lambda v: v.tensor_add(scal[0:1, 6:7], scal[0:1, 3:4], scal[0:1, 4:5]), ["sc.a", "sc.b"], "sc.d")
        P.dve(lambda v: v.tensor_add(scal[0:1, 0:1], scal[0:1, 6:7], scal[0:1, 5:6]), ["sc.c", "sc.d"], "sc.sp")
        P.dve(ts1(scal[0:1, 1:2], b_, 1e-6, ALU.mult, 1e-5, ALU.add), w, "sc.eps")
        P.dve(ts1(scal[0:1, 7:8], y_, 0.1, ALU.mult, 1.0, ALU.add), w, "sc.cy0")
        P.dve(lambda v: v.reciprocal(scal[0:1, 2:3], scal[0:1, 7:8]), ["sc.cy0"], "sc.cy")
        P.dve(ts1(scal[0:1, 4:5], r_, 0.1, ALU.mult, 1.0, ALU.add), ["sc.d"], "sc.g1")
        P.dve(ts1(scal[0:1, 5:6], y_, 0.05, ALU.mult), ["sc.sp"], "sc.g2")
        P.dve(lambda v: v.tensor_add(scal[0:1, 3:4], scal[0:1, 4:5], scal[0:1, 5:6]), ["sc.g1", "sc.g2"], "sc.gsc")
        P.dma(SY, "sc", lambda s: s.dma_start(scl_hbm.ap(), scal[0:1, 0:4]),
              ["sc.sp", "sc.eps", "sc.cy", "sc.gsc"], "sc.st")
        P.dma(SY, "sc", lambda s: s.dma_start(sccol[:, :], scl_hbm.ap().partition_broadcast(128)),
              ["sc.st"], "sc.bc")

        for t in range(NT):
            wd = ["csts", "sc.bc"]
            if t >= 1:
                wd.append(f"pe.st1.{t-1}")
            P.dve(lambda v, t=t: v.tensor_scalar(
                ang[:, :], invd_bc[:, :], sccol[:, 2:3], poscol[:, t:t + 1],
                ALU.mult, ALU.mult), wd, f"pe.ang.{t}")
            P.dve(lambda v: v.tensor_add(ang[:, :], ang[:, :], phase_bc[:, :]),
                  [f"pe.ang.{t}"], f"pe.ph.{t}")
            P.dve(lambda v: v.tensor_scalar(angk[:, :], ang[:, :], INV2PI, None,
                                            ALU.mult), [f"pe.ph.{t}"], f"pe.k0.{t}")
            P.dve(lambda v: v.tensor_copy(angki[:, :], angk[:, :]), [f"pe.k0.{t}"], f"pe.ki.{t}")
            P.dve(lambda v: v.tensor_copy(angk[:, :], angki[:, :]), [f"pe.ki.{t}"], f"pe.kf.{t}")
            P.dve(lambda v: v.scalar_tensor_tensor(
                angr[:, :], angk[:, :], -CW1, ang[:, :], ALU.mult, ALU.add),
                [f"pe.kf.{t}"], f"pe.cw1.{t}")
            P.dve(lambda v: v.scalar_tensor_tensor(
                ang[:, :], angk[:, :], -CW2, angr[:, :], ALU.mult, ALU.add),
                [f"pe.cw1.{t}"], f"pe.cw2.{t}")
            P.dve(lambda v: v.scalar_tensor_tensor(
                angr[:, :], angk[:, :], -CW3, ang[:, :], ALU.mult, ALU.add),
                [f"pe.cw2.{t}"], f"pe.cw.{t}")
            P.act(lambda sc_: sc_.activation(pet[:, :], angr[:, :], AF.Sin),
                  [f"pe.cw.{t}"], f"pe.sin.{t}")
            P.dve(lambda v: v.tensor_add(pebt0[:, :], pet[:, :], bo_bc[:, :]),
                  [f"pe.sin.{t}"], f"pe.peb.{t}")
            P.dma(SY, "pest", lambda s, t=t: s.dma_start(pe_b.ap()[t], pet[:, :]),
                  [f"pe.sin.{t}"], f"pe.st0.{t}")
            P.dma(SY, "pest", lambda s, t=t: s.dma_start(peb_b.ap()[t], pebt0[:, :]),
                  [f"pe.peb.{t}"], f"pe.st1.{t}")
            if debug_outputs:
                P.dma(SY, "dbg", lambda s, t=t: s.dma_start(dbg["pe"].ap()[t], pet[:, :]),
                      [f"pe.sin.{t}"], f"pe.dbg.{t}")
        P.marks["pes.done"] = ("pest", P.cnt["pest"])


    def p1(s):
        base = s * S
        if s > 0:
            for e in P.ENGINES:
                P.wait_only(e, P.gate(f"s{s-1}.end"))
        for t in range(NT):
            i = t % 2
            xl = "x0a" if i == 0 else "x0b"
            pl = "pea" if i == 0 else "peb"
            wts = []
            if t >= 2:
                wts.append(f"s{s}.p1.cast.{t-2}")
            P.dma(SY, xl, lambda sy, t=t, i=i: sy.dma_start(
                x0t[i][:, :], x0_d.ap()[base + t * 128:base + (t + 1) * 128, :]),
                wts, f"s{s}.p1.x0.{t}")
            wts = [f"s{s}.p1.x0.{t}", f"pe.st0.{t}"]
            if t >= 2:
                wts.append(f"s{s}.p1.tr.{t-2}.7")
            P.dma(GP, pl, lambda g, t=t, i=i: g.dma_start(xpbf[i][:, :], pe_b.ap()[t]),
                  wts, f"s{s}.p1.pe.{t}")
            P.dve(lambda v, i=i: v.tensor_tensor(xpbf[i][:, :], x0t[i][:, :],
                                                 xpbf[i][:, :], ALU.add),
                  [f"s{s}.p1.pe.{t}", f"s{s}.p1.x0.{t}"], f"s{s}.p1.cast.{t}")
            for ic in range(8):
                bank = PS[2][:, (ic % 2) * 512:(ic % 2) * 512 + 64].bitcast(BF16)
                wts = [f"s{s}.p1.cast.{t}"]
                if ic == 0 and t >= 1:
                    wts.append(f"s{s}.p1.ev.{t-1}.7")
                if ic >= 2:
                    wts.append(f"s{s}.p1.ev.{t}.{ic-2}")
                P.mm(lambda tns, i=i, ic=ic, bank=bank: tns.transpose(
                    bank, xpbf[i][:, ic * 128:(ic + 1) * 128], eyesb[:, :]),
                    wts, f"s{s}.p1.tr.{t}.{ic}")
                P.dve(lambda v, t=t, ic=ic, bank=bank: v.tensor_copy(
                    XPT[ic][:, t * 128:(t + 1) * 128], bank),
                    [f"s{s}.p1.tr.{t}.{ic}"], f"s{s}.p1.ev.{t}.{ic}")
        if debug_outputs and s == 0:
            for ic in range(8):
                P.dma(SY, "dbg", lambda sy, ic=ic: sy.dma_start(
                    dbg["xpT"].ap()[ic], XPT[ic][:, :]),
                    [f"s{s}.p1.ev.{NT-1}.{ic}"], f"dbg.xpT.{ic}")

    def proj_v(s):
        P.dma(GP, "bcv", lambda g: g.dma_start(bv_bc[:, :], bv_d.ap().partition_broadcast(128)),
              [f"s{s}.p1.cast.{NT-1}", f"s{s}.p1.cast.{NT-2}"], f"s{s}.bv")
        for i in range(8):
            P.dma(GP, "wv", lambda g, i=i: g.dma_start(
                wvTsb[i][:, :], wvT_d.ap()[i * 128:(i + 1) * 128, :]),
                (), f"s{s}.wvT.{i}")
        for t in range(NT):
            wts = [f"pe.peb.{NT-1}", f"pe.sin.{NT-1}", "pes.done"] if s == 0 else []
            P.gp(lambda g, t=t: g.memset(
                Vt[t][:, :].rearrange("p (h c) -> p h c", c=65)[:, :, 64:65], 1.0),
                wts, f"s{s}.vones.{t}")
        for t in range(NT):
            wts = [f"s{s}.p1.ev.{t}.{ic2}" for ic2 in range(8)]
            if t == 0:
                wts += [f"s{s}.wvT.7", f"s{s}.vones.{NT-1}"]
            if t >= 1:
                wts.append(f"s{s}.vev.{t-1}.1")
            for ic in range(8):
                for half in range(2):
                    P.mm(lambda tns, t=t, ic=ic, half=half: tns.matmul(
                        PS[3][:, half * 512:(half + 1) * 512],
                        XPT[ic][:, t * 128:(t + 1) * 128],
                        wvTsb[ic][:, half * 512:(half + 1) * 512],
                        start=(ic == 0), stop=(ic == 7)),
                        wts if (ic == 0 and half == 0) else (),
                        f"s{s}.vmm.{t}.{half}" if ic == 7 else None)
            for half in range(2):
                h0 = half * 8
                P.dve(lambda v, t=t, half=half, h0=h0: v.tensor_tensor(
                    Vt[t][:, :].rearrange("p (h c) -> p h c", c=65)[:, h0:h0 + 8, 0:64],
                    PS[3][:, half * 512:(half + 1) * 512].rearrange("p (h c) -> p h c", c=64),
                    bv_bc[:, half * 512:(half + 1) * 512].rearrange("p (h c) -> p h c", c=64),
                    ALU.add),
                    [f"s{s}.vmm.{t}.{half}", f"s{s}.bv"], f"s{s}.vev.{t}.{half}")
        if debug_outputs and s == 0:
            P.dma(SY, "dbg", lambda sy: sy.dma_start(dbg["v0"].ap(), Vt[0][:, :]),
                  [f"s{s}.vev.0.0", f"s{s}.vev.0.1"], "dbg.v0")

    def proj_qk_steps(s, c, last_proj):
        i = c % 2
        steps = []

        def zero_step():
            wz = []
            if c >= 2:
                wz.append(f"s{s}.attdone.{c-2}")
            P.dve(lambda v, i=i: v.memset(KTzp[i][64:128, 0:S], 0.0), wz, f"s{s}.kz0.{c}")
            P.dve(lambda v, i=i: v.memset(KTzp[i][0:64, S:2 * S], 0.0), wz, f"s{s}.kz1.{c}")

        def mk_mm(kind, tsl, ic):
            wsb = wqTsb if kind == "q" else wkTsb
            def em():
                wts = []
                if ic == 0:
                    wts.append(f"s{s}.wqT.7" if kind == "q" else f"s{s}.wkT.7")
                    if c >= 2:
                        wts.append(f"s{s}.attdone.{c-2}")
                    if last_proj[0] is not None:
                        wts.append(last_proj[0])
                    if last_proj[1] is not None:
                        wts.append(last_proj[1])
                for h2 in range(2):
                    P.mm(lambda tns, tsl=tsl, ic=ic, h2=h2, wsb=wsb: tns.matmul(
                        PS[3][:, h2 * 512:(h2 + 1) * 512],
                        wsb[ic][:, c * 128:(c + 1) * 128],
                        XPT[ic][:, (tsl + h2) * 512:(tsl + h2 + 1) * 512],
                        start=(ic == 0), stop=(ic == 7)),
                        wts if h2 == 0 else (),
                        f"s{s}.{kind}mm.{c}.{tsl + h2}" if ic == 7 else None)
                if ic == 7:
                    if kind == "q":
                        for h2 in range(2):
                            P.dve(lambda v, tsl=tsl, h2=h2: v.tensor_scalar(
                                QTp[i][:, (tsl + h2) * 512:(tsl + h2 + 1) * 512],
                                PS[3][:, h2 * 512:(h2 + 1) * 512],
                                bqcol[:, c:c + 1], sccol[:, 0:1], ALU.add, ALU.mult),
                                [f"s{s}.qmm.{c}.{tsl + h2}", "sc.bc"],
                                f"s{s}.qev.{c}.{tsl + h2}")
                        last_proj[0] = f"s{s}.qev.{c}.{tsl + 1}"
                    else:
                        for h2 in range(2):
                            P.dve(lambda v, tsl=tsl, h2=h2: v.tensor_scalar(
                                KTzp[i][0:64, (tsl + h2) * 512:(tsl + h2 + 1) * 512],
                                PS[3][0:64, h2 * 512:(h2 + 1) * 512],
                                bkcol[0:64, c:c + 1], None, ALU.add),
                                [f"s{s}.kmm.{c}.{tsl + h2}", f"s{s}.kz0.{c}",
                                 f"s{s}.kz1.{c}"], f"s{s}.kevA.{c}.{tsl + h2}")
                            P.dve(lambda v, tsl=tsl, h2=h2: v.tensor_scalar(
                                KTzp[i][64:128, S + (tsl + h2) * 512:S + (tsl + h2 + 1) * 512],
                                PS[3][64:128, h2 * 512:(h2 + 1) * 512],
                                bkcol[64:128, c:c + 1], None, ALU.add),
                                [f"s{s}.kevA.{c}.{tsl + h2}"], f"s{s}.kevB.{c}.{tsl + h2}")
                        last_proj[0] = f"s{s}.kevB.{c}.{tsl + 1}"
            return em

        first = True
        for kind, tsl in (("q", 0), ("q", 2), ("k", 0), ("k", 2)):
            for ic in range(8):
                em = mk_mm(kind, tsl, ic)
                if first:
                    def em0(em=em):
                        zero_step()
                        em()
                    steps.append(em0)
                    first = False
                else:
                    steps.append(em)
        return steps

    def attention_pair(s, c, interleave_steps, last_proj):
        i = c % 2

        def eb(g):
            return Ebuf[g % 3]

        def emit_sa_sb(q, kt):
            sp = kt % 2
            wts = []
            if kt == 0 and q == 0:
                wts += [f"s{s}.qev.{c}.{t}" for t in range(4)]
                wts += [f"s{s}.kevB.{c}.{t}" for t in range(4)]
            if kt >= 2:
                wts.append(f"s{s}.exp.{c}.{q}.{kt-2}")
            elif q > 0:
                wts.append(f"s{s}.exp.{c}.{q-1}.{NKT-2+kt}")
            elif c > 0:
                wts.append(f"s{s}.exp.{c-1}.{NQ-1}.{NKT-2+kt}")
            P.mm(lambda tns, i=i, q=q, kt=kt, sp=sp: tns.matmul(
                PS[sp][:, 0:512],
                KTzp[i][:, kt * 128:(kt + 1) * 128],
                QTp[i][:, q * 512:(q + 1) * 512],
                start=True, stop=True), wts, f"s{s}.sa.{c}.{q}.{kt}")
            P.mm(lambda tns, i=i, q=q, kt=kt, sp=sp: tns.matmul(
                PS[sp][:, 512:1024],
                KTzp[i][:, S + kt * 128:S + (kt + 1) * 128],
                QTp[i][:, q * 512:(q + 1) * 512],
                start=True, stop=True), (), f"s{s}.sb.{c}.{q}.{kt}")

        def emit_exp(q, kt):
            sp = kt % 2
            g = q * NKT + kt
            wts = [f"s{s}.sb.{c}.{q}.{kt}", "ms.eb"]
            gp = g - 3
            if gp >= 0:
                wts.append(f"s{s}.pvB.{c}.{gp // NKT}.{gp % NKT}")
            elif c > 0:
                gq = NPAIR_G + gp
                wts.append(f"s{s}.pvB.{c-1}.{gq // NKT}.{gq % NKT}")
            if c == 0 and g == 2:
                wts += [f"s{s}.vev.{NT-1}.1"]
            P.act(lambda sc_, sp=sp, g=g: sc_.activation(
                eb(g)[:, :], PS[sp][:, :], AF.Exp, bias=expbias[:, :]),
                wts, f"s{s}.exp.{c}.{q}.{kt}")

        def emit_pv(q, kt, extra=()):
            g = q * NKT + kt
            wts = [f"s{s}.exp.{c}.{q}.{kt}"] + list(extra)
            if c == 0 and q == 0:
                wts += [f"s{s}.vev.{kt}.0", f"s{s}.vev.{kt}.1"]
            P.mm(lambda tns, c=c, kt=kt, g=g: tns.matmul(
                PS[2][0:65, 0:512], Vt[kt][:, 130 * c:130 * c + 65],
                eb(g)[:, 0:512],
                start=(kt == 0), stop=(kt == NKT - 1)), wts,
                f"s{s}.pvA.{c}.{q}.{kt}")
            P.mm(lambda tns, c=c, kt=kt, g=g: tns.matmul(
                PS[2][0:65, 512:1024], Vt[kt][:, 130 * c + 65:130 * c + 130],
                eb(g)[:, 512:1024],
                start=(kt == 0), stop=(kt == NKT - 1)), (),
                f"s{s}.pvB.{c}.{q}.{kt}")

        def tail_act(q):
            wts = [f"s{s}.pvA.{c}.{q}.{NKT-1}", f"s{s}.pvB.{c}.{q}.{NKT-1}"]
            P.dve(lambda v: v.reciprocal(lnrbuf[:, :], PS[2][64:65, 0:1024]),
                  wts, f"s{s}.lnr.{c}.{q}")
            wts = [f"s{s}.lnr.{c}.{q}"]
            if P.has(f"s{s}.Rmm.{c}.{q-1}.B"):
                wts.append(f"s{s}.Rmm.{c}.{q-1}.B")
            if c > 0 and q == 0:
                wts.append(f"s{s}.Rmm.{c-1}.{NQ-1}.B")
            P.dve(lambda v: v.tensor_copy(rinvbf[:, :], lnrbuf[:, :]),
                  wts, f"s{s}.rinv.{c}.{q}")

        def tail_pe(q):
            wts = [f"s{s}.rinv.{c}.{q}", "ms.ones"]
            if P.has(f"s{s}.rcpB.{c}.{q-1}"):
                wts.append(f"s{s}.rcpB.{c}.{q-1}")
            if c > 0 and q == 0:
                wts.append(f"s{s}.rcpB.{c-1}.{NQ-1}")
            if last_proj[0] is not None:
                wts.append(last_proj[0])
            P.mm(lambda tns: tns.matmul(PS[3][0:64, 512:1024], ones64[:, :],
                                        rinvbf[0:1, 0:512], start=True, stop=True),
                 wts, f"s{s}.Rmm.{c}.{q}.A")
            P.dve(lambda v: v.tensor_copy(Rinv[:, :], PS[3][0:64, 512:1024]),
                  [f"s{s}.Rmm.{c}.{q}.A"], f"s{s}.rcpA.{c}.{q}")
            P.dve(lambda v, c=c, q=q: v.tensor_tensor(
                OT[c][0:64, q * 512:(q + 1) * 512], PS[2][0:64, 0:512],
                Rinv[:, :], ALU.mult), [f"s{s}.rcpA.{c}.{q}"], f"s{s}.otA.{c}.{q}")
            P.mm(lambda tns: tns.matmul(PS[3][0:64, 512:1024], ones64[:, :],
                                        rinvbf[0:1, 512:1024], start=True, stop=True),
                 [f"s{s}.rcpA.{c}.{q}"], f"s{s}.Rmm.{c}.{q}.B")
            P.dve(lambda v: v.tensor_copy(Rinv[:, :], PS[3][0:64, 512:1024]),
                  [f"s{s}.Rmm.{c}.{q}.B", f"s{s}.otA.{c}.{q}"], f"s{s}.rcpB.{c}.{q}")
            wts = [f"s{s}.rcpB.{c}.{q}"]
            if P.has(f"s{s}.otbdma.{c}.{q-1}"):
                wts.append(f"s{s}.otbdma.{c}.{q-1}")
            elif c > 0:
                wts.append(f"s{s}.otbdma.{c-1}.{NQ-1}")
            P.dve(lambda v: v.tensor_tensor(OTBst[:, :], PS[2][0:64, 512:1024],
                                            Rinv[:, :], ALU.mult), wts, f"s{s}.otB.{c}.{q}")
            P.dma(SY, "otb", lambda sy, c=c, q=q: sy.dma_start(
                OT[c][64:128, q * 512:(q + 1) * 512], OTBst[:, :]),
                [f"s{s}.otB.{c}.{q}"], f"s{s}.otbdma.{c}.{q}")
            last_proj[1] = f"s{s}.rcpB.{c}.{q}"

        def pop():
            if interleave_steps:
                interleave_steps.pop(0)()

        for q in range(NQ):
            for kt in range(NKT):
                if kt < 3:
                    emit_sa_sb(q, kt)
                    emit_exp(q, kt)
                    if kt == 1 and q > 0:
                        tail_act(q - 1)
                    if kt == 2 and q > 0:
                        tail_pe(q - 1)
                elif kt == 3:
                    emit_sa_sb(q, kt)
                    extra = []
                    if q > 0:
                        extra = [f"s{s}.otA.{c}.{q-1}", f"s{s}.otB.{c}.{q-1}"]
                    elif c > 0:
                        extra = [f"s{s}.otA.{c-1}.{NQ-1}", f"s{s}.otB.{c-1}.{NQ-1}"]
                    emit_pv(q, 0, extra=extra)
                    emit_pv(q, 1)
                    emit_pv(q, 2)
                    emit_exp(q, kt)
                else:
                    emit_sa_sb(q, kt)
                    emit_pv(q, kt - 1)
                    emit_exp(q, kt)
                    if 4 <= kt <= 11:
                        pop()
            emit_pv(q, NKT - 1)
        tail_act(NQ - 1)
        tail_pe(NQ - 1)
        while interleave_steps:
            interleave_steps.pop(0)()
        P.marks[f"s{s}.attdone.{c}"] = ("dve", P.cnt["dve"])

    def p23(s):
        proj_v(s)
        for ic in range(8):
            P.dma(GP, "wq", lambda g, ic=ic: g.dma_start(
                wqTsb[ic][:, :], wqT_d.ap()[ic * 128:(ic + 1) * 128, :]),
                [f"s{s}.vmm.{NT-1}.1"] if ic == 0 else (), f"s{s}.wqT.{ic}")
        for ic in range(8):
            P.dma(GP, "wk", lambda g, ic=ic: g.dma_start(
                wkTsb[ic][:, :], wkT_d.ap()[ic * 128:(ic + 1) * 128, :]),
                (), f"s{s}.wkT.{ic}")
        last_proj = [None, None]
        for st in proj_qk_steps(s, 0, last_proj):
            st()
        for c in range(NPAIR):
            steps = proj_qk_steps(s, c + 1, last_proj) if c + 1 < NPAIR else []
            attention_pair(s, c, steps, last_proj)
        P.barrier_tagged(f"s{s}.attend")
        if debug_outputs and s == 0:
            for ic in range(8):
                P.dma(SY, "dbg", lambda sy, ic=ic: sy.dma_start(
                    dbg["ot"].ap()[ic], OT[ic][:, :]),
                    [f"s{s}.otA.{ic}.{NQ-1}", f"s{s}.otbdma.{ic}.{NQ-1}"],
                    f"dbg.ot.{ic}")

    def p4(s):
        base = s * S
        gate = [f"s{s}.attend.mm", f"s{s}.attend.dve", f"s{s}.attend.act",
                f"s{s}.attend.otb"]
        for e in P.ENGINES:
            P.wait_only(e, gate)
        for ic in range(8):
            P.dma(GP, "wo", lambda g, ic=ic: g.dma_start(
                woTsb[ic][:, :], woT_d.ap()[ic * 128:(ic + 1) * 128, :]),
                (), f"s{s}.woT.{ic}")
        P.dma(GP, "bcg", lambda g: g.dma_start(gamma_bc[:, :], gamma_d.ap().partition_broadcast(128)),
              (), f"s{s}.gbc0")
        P.dve(lambda v: v.tensor_scalar(gamma_bc[:, :], gamma_bc[:, :], sccol[:, 3:4],
                                        None, ALU.mult), [f"s{s}.gbc0", "sc.bc"], f"s{s}.gbc")
        P.dma(GP, "bcbt", lambda g: g.dma_start(betab[:, :], beta_d.ap().partition_broadcast(128)),
              (), f"s{s}.bbc")
        P.dma(GP, "bcb2", lambda g: g.dma_start(b2b[:, :], b2_d.ap().partition_broadcast(128)),
              (), f"s{s}.b2bc")
        for ic in (0, 1, 2, 3, 4, 5):
            P.dma(GP, "w1", lambda g, ic=ic: g.dma_start(
                w1sb[ic][:, :], w1_d.ap()[ic * 128:(ic + 1) * 128, :]),
                (), f"s{s}.w1.{ic}")

        def ln_stats(pref, srcb, wtag, so):
            P.act(lambda sc_, srcb=srcb, so=so: sc_.activation(
                PS[1][:, :], srcb[:, :], AF.Copy, accum_out=statc[:, so:so + 1]),
                [wtag], f"{pref}.s1")
            P.act(lambda sc_, srcb=srcb, so=so: sc_.activation(
                PS[1][:, :], srcb[:, :], AF.Square, accum_out=statc[:, so + 2:so + 3]),
                [wtag], f"{pref}.s2")
            P.dve(lambda v, so=so: v.tensor_scalar(
                statc[:, so + 5:so + 6], statc[:, so:so + 1], 1.0 / D, None, ALU.mult),
                [f"{pref}.s1"], f"{pref}.m")
            P.dve(lambda v, so=so: v.tensor_tensor(
                statc[:, so + 1:so + 2], statc[:, so:so + 1], statc[:, so:so + 1],
                ALU.mult), [f"{pref}.s1"], f"{pref}.t1")
            P.dve(lambda v, so=so: v.scalar_tensor_tensor(
                statc[:, so + 7:so + 8], statc[:, so + 1:so + 2], -1.0 / D,
                statc[:, so + 2:so + 3], ALU.mult, ALU.add),
                [f"{pref}.t1", f"{pref}.s2"], f"{pref}.var")

        UB = [ubuf, ubuf2]

        def stage_a(t):
            i = t % 2
            so = 12 * i
            xl = "x0a" if i == 0 else "x0b"
            pl = "pea" if i == 0 else "peb"
            ps = PS[3] if t % 2 == 0 else PS[0]
            ub = UB[i]
            wts = [] if t < 2 else [f"s{s}.p4.xpe.{t-2}", f"s{s}.p4.xpb.{t-2}"]
            P.dma(SY, xl, lambda sy, t=t, i=i: sy.dma_start(
                x0t[i][:, :], x0_d.ap()[base + t * 128:base + (t + 1) * 128, :]),
                wts, f"s{s}.p4.x0.{t}")
            wts = [f"pe.st1.{t}"] if t < 2 else [f"s{s}.p4.xpe.{t-2}", f"pe.st1.{t}"]
            P.dma(GP, pl, lambda g, t=t, i=i: g.dma_start(pebt[i][:, :], peb_b.ap()[t]),
                  wts, f"s{s}.p4.pe.{t}")
            wts = [f"s{s}.p4.x0.{t}", f"s{s}.p4.pe.{t}"]
            if t >= 2:
                wts.append(f"s{s}.p4.u.{t-2}")
            P.gp(lambda g, i=i: g.tensor_tensor(xpe[i][:, :], x0t[i][:, :],
                                                pebt[i][:, :], ALU.add),
                 wts, f"s{s}.p4.xpe.{t}")
            wts = [f"s{s}.p4.x0.{t}", f"s{s}.bbc"]
            if t >= 2:
                wts.append(f"s{s}.p4.w.{t-2}")
            P.gp(lambda g, i=i: g.tensor_tensor(xpb[i][:, :], x0t[i][:, :],
                                                betab[:, :], ALU.add),
                 wts, f"s{s}.p4.xpb.{t}")
            wts = [f"s{s}.woT.7"]
            if t >= 2:
                wts.append(f"s{s}.p4.u.{t-2}")
            for ic in range(8):
                for half in range(2):
                    P.mm(lambda tns, t=t, ic=ic, half=half, ps=ps: tns.matmul(
                        ps[:, half * 512:(half + 1) * 512],
                        OT[ic][:, t * 128:(t + 1) * 128],
                        woTsb[ic][:, half * 512:(half + 1) * 512],
                        start=(ic == 0), stop=(ic == 7)),
                        wts if (ic == 0 and half == 0) else (),
                        f"s{s}.p4.mm.{t}" if (ic == 7 and half == 1) else None)
            P.dve(lambda v, ps=ps, i=i, ub=ub: v.tensor_tensor(
                ub[:, :], ps[:, :], xpe[i][:, :], ALU.add),
                  [f"s{s}.p4.mm.{t}", f"s{s}.p4.xpe.{t}"], f"s{s}.p4.u.{t}")
            ln_stats(f"s{s}.p4.a{t}", ub, f"s{s}.p4.u.{t}", so)
            P.act(lambda sc_, so=so: sc_.activation(statc[:, so + 9:so + 10],
                                                    statc[:, so + 7:so + 8], AF.Ln,
                                                    bias=sccol[:, 1:2], scale=1.0 / D),
                  [f"s{s}.p4.a{t}.var", "sc.bc"], f"s{s}.p4.ln.{t}")
            P.act(lambda sc_, so=so: sc_.activation(statc[:, so + 8:so + 9],
                                                    statc[:, so + 9:so + 10], AF.Exp,
                                                    scale=-0.5), [f"s{s}.p4.ln.{t}"],
                  f"s{s}.p4.rstd.{t}")

        def stage_b(t):
            i = t % 2
            so = 12 * i
            ub = UB[i]
            wts = [f"s{s}.p4.rstd.{t}"]
            if t >= 1:
                wts.append(f"s{s}.p4.y1dma.{t-1}")
            P.dve(lambda v, ub=ub, so=so: v.tensor_scalar(
                vnbuf[:, :], ub[:, :], statc[:, so + 5:so + 6],
                statc[:, so + 8:so + 9], ALU.subtract, ALU.mult),
                wts, f"s{s}.p4.n1.{t}")
            P.dve(lambda v: v.tensor_tensor(vnbuf[:, :], vnbuf[:, :], gamma_bc[:, :],
                                            ALU.mult),
                  [f"s{s}.p4.n1.{t}", f"s{s}.gbc"], f"s{s}.p4.n2.{t}")
            P.dve(lambda v, i=i, ub=ub: v.tensor_tensor(
                ub[:, :], vnbuf[:, :], xpb[i][:, :], ALU.add),
                  [f"s{s}.p4.n2.{t}", f"s{s}.p4.xpb.{t}"], f"s{s}.p4.w.{t}")
            ln_stats(f"s{s}.p4.b{t}", ub, f"s{s}.p4.w.{t}", so)
            P.act(lambda sc_, so=so: sc_.activation(statc[:, so + 9:so + 10],
                                                    statc[:, so + 7:so + 8], AF.Ln,
                                                    bias=eps5col[:, :], scale=1.0 / D),
                  [f"s{s}.p4.b{t}.var", "ms.e5"], f"s{s}.p4.lnb.{t}")
            P.act(lambda sc_, so=so: sc_.activation(statc[:, so + 8:so + 9],
                                                    statc[:, so + 9:so + 10], AF.Exp,
                                                    scale=-0.5), [f"s{s}.p4.lnb.{t}"],
                  f"s{s}.p4.rstdb.{t}")
            P.dve(lambda v, ub=ub, so=so: v.tensor_scalar(
                vnbuf[:, :], ub[:, :], statc[:, so + 5:so + 6],
                statc[:, so + 8:so + 9], ALU.subtract, ALU.mult),
                [f"s{s}.p4.rstdb.{t}"], f"s{s}.p4.y1.{t}")
            wts = [f"s{s}.p4.y1.{t}"]
            if t >= 2:
                wts.append(f"s{s}.p4.tr.{t-2}.7")
            P.dve(lambda v, i=i: v.tensor_copy(y1bf[i][:, :], vnbuf[:, :]), wts,
                  f"s{s}.p4.y1bf.{t}")
            P.gp(lambda g: g.tensor_tensor(vnbuf[:, :], vnbuf[:, :], b2b[:, :], ALU.add),
                 [f"s{s}.p4.y1bf.{t}", f"s{s}.b2bc"], f"s{s}.p4.y1b2.{t}")
            P.dma(SY, "y1st", lambda sy, t=t: sy.dma_start(y1_b.ap()[t], vnbuf[:, :]),
                  [f"s{s}.p4.y1b2.{t}"], f"s{s}.p4.y1dma.{t}")
            if debug_outputs and s == 0:
                P.dma(SY, "dbg", lambda sy, t=t: sy.dma_start(dbg["y1"].ap()[t], vnbuf[:, :]),
                      [f"s{s}.p4.y1b2.{t}"], f"dbg.y1.{t}")
            for ic in range(8):
                bank = PS[2][:, (ic % 2) * 512:(ic % 2) * 512 + 64].bitcast(BF16)
                wts = [f"s{s}.p4.y1bf.{t}"]
                if ic == 0 and t >= 1:
                    wts.append(f"s{s}.p4.ev.{t-1}.7")
                if ic >= 2:
                    wts.append(f"s{s}.p4.ev.{t}.{ic-2}")
                P.mm(lambda tns, i=i, ic=ic, bank=bank: tns.transpose(
                    bank, y1bf[i][:, ic * 128:(ic + 1) * 128], eyesb[:, :]),
                    wts, f"s{s}.p4.tr.{t}.{ic}")
                P.dve(lambda v, t=t, ic=ic, bank=bank: v.tensor_copy(
                    Y1T[ic][:, t * 128:(t + 1) * 128], bank),
                    [f"s{s}.p4.tr.{t}.{ic}"], f"s{s}.p4.ev.{t}.{ic}")

        stage_a(0)
        for t in range(1, NT):
            stage_a(t)
            stage_b(t - 1)
        stage_b(NT - 1)
        P.marks[f"s{s}.y1all"] = ("y1st", P.cnt["y1st"])
        if debug_outputs and s == 0:
            P.dma(SY, "dbg", lambda sy: sy.dma_start(dbg["y1T0"].ap(), Y1T[0][:, :]),
                  [f"s{s}.p4.tr.{NT-1}.6", f"s{s}.p4.tr.{NT-1}.7"], "dbg.y1T0")
            P.dma(SY, "dbg", lambda sy: sy.dma_start(dbg["y1bf1"].ap(), y1bf[1][:, :]),
                  [f"s{s}.p4.tr.{NT-1}.6", f"s{s}.p4.tr.{NT-1}.7"], "dbg.y1bf1")

    def p5(s):
        base = s * S
        P.dma(GP, "w1", lambda g: g.dma_start(
            w1sb[6][:, :], w1_d.ap()[6 * 128:7 * 128, :]),
            [f"s{s}.p4.xpb.{NT-1}", f"s{s}.p4.y1b2.{NT-1}"], f"s{s}.w1.6")
        P.dma(GP, "w1", lambda g: g.dma_start(
            w1sb[7][:, :], w1_d.ap()[7 * 128:8 * 128, :]),
            [f"s{s}.p4.y1.{NT-1}"], f"s{s}.w1.7")
        w2gate = {0: [f"s{s}.p4.u.{NT-2}"], 2: [f"s{s}.p4.u.{NT-1}"],
                  4: [f"s{s}.p4.w.{NT-2}"], 6: [f"s{s}.p4.w.{NT-1}"],
                  16: [f"s{s}.p4.mm.{NT-1}"]}
        for i in range(32):
            P.dma(GP, "w2", lambda g, i=i: g.dma_start(
                w2sb[i][:, :], w2_d.ap()[i * 128:(i + 1) * 128, :]),
                w2gate.get(i, ()), f"s{s}.w2.{i}")
        for tq in range(NQ):
            for oc in range(32):
                pbank = oc % 2
                wts = [f"s{s}.w1.7", f"s{s}.w1.6"]
                if tq > 0 or oc >= 2:
                    pt, po = (tq, oc - 2) if oc >= 2 else (tq - 1, 30 + oc)
                    wts.append(f"s{s}.f1ev.{pt}.{po}")
                if tq == 0 and oc == 0:
                    wts += [f"s{s}.p4.ev.{NT-1}.{ic2}" for ic2 in range(8)]
                    wts += [f"s{s}.p4.ev.{NT-2}.{ic2}" for ic2 in range(8)]
                for ic in range(8):
                    P.mm(lambda tns, tq=tq, oc=oc, ic=ic, pbank=pbank: tns.matmul(
                        PS[0][:, pbank * 512:(pbank + 1) * 512],
                        w1sb[ic][:, oc * 128:(oc + 1) * 128],
                        Y1T[ic][:, tq * 512:(tq + 1) * 512],
                        start=(ic == 0), stop=(ic == 7)),
                        wts if ic == 0 else (),
                        f"s{s}.f1.{tq}.{oc}" if ic == 7 else None)
                if debug_outputs and s == 0 and tq == 0 and oc == 31:
                    P.dma(SY, "dbg", lambda sy: sy.dma_start(
                        dbg["y1T0b"].ap(), Y1T[0][:, 1920:2048]),
                        [f"s{s}.f1.{tq}.{oc}"], "dbg.y1T0b")
                wts = [f"s{s}.f1.{tq}.{oc}", "csts", "ms.z"]
                if tq == 0:
                    wts.append(f"s{s}.p4.y1dma.{NT-1}")
                if tq > 0:
                    wts.append(f"s{s}.f2.{tq * 4 - 1}")
                P.dve(lambda v, oc=oc, pbank=pbank: v.tensor_scalar(
                    hsb[oc][:, :], PS[0][:, pbank * 512:(pbank + 1) * 512],
                    b1col[:, oc:oc + 1], zerocol[:, 0:1], ALU.add, ALU.max),
                    wts, f"s{s}.f1ev.{tq}.{oc}")
            for tt in range(4):
                t = tq * 4 + tt
                ps2 = PS[3] if t % 2 == 0 else PS[2]
                wts = [f"s{s}.y1all"]
                if P.has(f"s{s}.p5.z.{t-1}"):
                    wts.append(f"s{s}.p5.z.{t-1}")
                if t == 0:
                    wts.append(f"s{s}.p4.n2.{NT-1}")
                P.dma(GP, "y1ld", lambda g, t=t: g.dma_start(y1b2t[:, :], y1_b.ap()[t]),
                      wts, f"s{s}.p5.y1.{t}")
                wts = [f"s{s}.f1ev.{tq}.31", f"s{s}.w2.31"]
                if P.has(f"s{s}.p5.z.{t-2}"):
                    wts.append(f"s{s}.p5.z.{t-2}")
                for wc in range(32):
                    for half in range(2):
                        P.mm(lambda tns, tt=tt, wc=wc, half=half, ps2=ps2: tns.matmul(
                            ps2[:, half * 512:(half + 1) * 512],
                            hsb[wc][:, tt * 128:(tt + 1) * 128],
                            w2sb[wc][:, half * 512:(half + 1) * 512],
                            start=(wc == 0), stop=(wc == 31)),
                            wts if (wc == 0 and half == 0) else (),
                            f"s{s}.f2.{t}" if (wc == 31 and half == 1) else None)
                P.dve(lambda v, ps2=ps2: v.tensor_tensor(out_st[1][:, :], ps2[:, :],
                                                         y1b2t[:, :], ALU.add),
                      [f"s{s}.f2.{t}", f"s{s}.p5.y1.{t}"], f"s{s}.p5.z.{t}")
                pref = f"s{s}.p5.c{t}"
                P.dve(lambda v: v.reduce_sum(statc[:, 0:1], out_st[1][:, :],
                                             mybir.AxisListType.X),
                      [f"s{s}.p5.z.{t}"], f"{pref}.s1")
                P.act(lambda sc_: sc_.activation(PS[1][:, :], out_st[1][:, :],
                                                 AF.Square, accum_out=statc[:, 2:3]),
                      [f"s{s}.p5.z.{t}"], f"{pref}.s2")
                P.dve(lambda v: v.tensor_scalar(statc[:, 5:6], statc[:, 0:1],
                                                1.0 / D, None, ALU.mult),
                      [f"{pref}.s1"], f"{pref}.m")
                P.dve(lambda v: v.tensor_tensor(statc[:, 1:2], statc[:, 0:1],
                                                statc[:, 0:1], ALU.mult),
                      [f"{pref}.s1"], f"{pref}.t1")
                P.dve(lambda v: v.scalar_tensor_tensor(
                    statc[:, 7:8], statc[:, 1:2], -1.0 / D, statc[:, 2:3],
                    ALU.mult, ALU.add), [f"{pref}.t1", f"{pref}.s2"], f"{pref}.var")
                P.act(lambda sc_: sc_.activation(statc[:, 9:10], statc[:, 7:8], AF.Ln,
                                                 bias=eps5col[:, :], scale=1.0 / D),
                      [f"{pref}.var"], f"{pref}.ln")
                P.act(lambda sc_: sc_.activation(statc[:, 8:9], statc[:, 9:10], AF.Exp,
                                                 scale=-0.5), [f"{pref}.ln"], f"{pref}.rstd")
                if debug_outputs and s == 0 and t == 15:
                    P.dma(SY, "dbg", lambda sy: sy.dma_start(dbg["z15"].ap(), out_st[1][:, :]),
                          [f"s{s}.p5.z.{t}"], "dbg.z15")
                    P.dma(SY, "dbg", lambda sy: sy.dma_start(dbg["yl15"].ap(), y1b2t[:, :]),
                          [f"s{s}.p5.z.{t}"], "dbg.yl15")
                    P.dma(SY, "dbg", lambda sy: sy.dma_start(dbg["h28"].ap(), hsb[28][:, :]),
                          [f"s{s}.p5.z.{t}"], "dbg.h28")
                wts = [f"{pref}.rstd"]
                if P.has(f"s{s}.out.{t-1}"):
                    wts.append(f"s{s}.out.{t-1}")
                P.dve(lambda v: v.tensor_scalar(out_st[0][:, :], out_st[1][:, :],
                                                statc[:, 5:6], statc[:, 8:9],
                                                ALU.subtract, ALU.mult),
                      wts, f"s{s}.p5.y2.{t}")
                P.dma(SY, "out", lambda sy, t=t: sy.dma_start(
                    out_d.ap()[base + t * 128:base + (t + 1) * 128, :], out_st[0][:, :]),
                    [f"s{s}.p5.y2.{t}"], f"s{s}.out.{t}")
        P.barrier_tagged(f"s{s}.end")

    p0()
    for s in range(SPC):
        p1(s)
        p23(s)
        p4(s)
        p5(s)

    with nc.Block() as block:
        def run(eng, lst, final_waits=()):
            for fn, waits, lane, amt in lst:
                for (wk, wv) in waits:
                    eng.wait_ge(sem[wk], wv)
                if fn is None:
                    continue
                instr = fn(eng)
                if amt:
                    instr.then_inc(sem[lane], amt)
            for (wk, wv) in final_waits:
                eng.wait_ge(sem[wk], wv)

        final = [(k, v) for k, v in P.cnt.items() if v > 0]

        @block.tensor
        def _(t):
            run(t, P.sched["tensor"])

        @block.scalar
        def _(s_):
            run(s_, P.sched["scalar"])

        @block.vector
        def _(v):
            run(v, P.sched["vector"])

        @block.sync
        def _(sy):
            run(sy, P.sched["sync"], final)

        @block.gpsimd
        def _(g):
            run(g, P.sched["gpsimd"])

    return nc


_cached = {}


def kernel(L0_attn_x, L0_attn_rby, L0_attn_Wq, L0_attn_bq, L0_attn_Wk,
           L0_attn_bk, L0_attn_Wv, L0_attn_bv, L0_attn_Wo, L0_attn_bo,
           L0_attn_ln_gamma, L0_attn_ln_beta, L0_ffn_w1, L0_ffn_b1,
           L0_ffn_w2, L0_ffn_b2):
    x = np.ascontiguousarray(np.asarray(L0_attn_x, dtype=np.float32))

    j = np.arange(512, dtype=np.float64)
    invd = np.repeat(np.power(10000.0, -j / 256.0), 2).astype(np.float32)[None, :]
    phase = np.zeros((1, D), np.float32)
    phase[0, 1::2] = np.float32(np.pi / 2)
    pos = (np.arange(128, dtype=np.float32)[:, None]
           + 128.0 * np.arange(NT, dtype=np.float32)[None, :])

    def col128(v, ncols):
        return np.ascontiguousarray(np.asarray(v, np.float32).reshape(ncols, 128).T)

    common = dict(
        rby=np.asarray(L0_attn_rby, np.float32).reshape(1, 3),
        wqT=np.ascontiguousarray(np.asarray(L0_attn_Wq, np.float32).T),
        wkT=np.ascontiguousarray(np.asarray(L0_attn_Wk, np.float32).T),
        wvT=np.ascontiguousarray(np.asarray(L0_attn_Wv, np.float32).T),
        woT=np.ascontiguousarray(np.asarray(L0_attn_Wo, np.float32).T),
        w1=np.ascontiguousarray(np.asarray(L0_ffn_w1, np.float32)),
        w2=np.ascontiguousarray(np.asarray(L0_ffn_w2, np.float32)),
        bv=np.asarray(L0_attn_bv, np.float32).reshape(1, D),
        bo=np.asarray(L0_attn_bo, np.float32).reshape(1, D),
        b2=np.asarray(L0_ffn_b2, np.float32).reshape(1, D),
        gamma=np.asarray(L0_attn_ln_gamma, np.float32).reshape(1, D),
        beta=np.asarray(L0_attn_ln_beta, np.float32).reshape(1, D),
        bqc=col128(L0_attn_bq, 8),
        bkc=col128(L0_attn_bk, 8),
        b1c=col128(L0_ffn_b1, 32),
        pos=np.ascontiguousarray(pos), invd=invd, phase=phase,
        eye=np.eye(128, dtype=np.float32).astype(ml_dtypes.bfloat16),
    )
    in_maps = []
    for c in range(NCORES):
        m = dict(common)
        m["x0"] = np.ascontiguousarray(x[c * SPC:(c + 1) * SPC].reshape(SPC * S, D))
        in_maps.append(m)

    if "nc" not in _cached:
        _cached["nc"] = build_program()
    nc = _cached["nc"]
    res = run_bass_kernel_spmd(nc, in_maps, core_ids=list(range(NCORES)))
    out = np.concatenate(
        [np.asarray(r["out"]).reshape(SPC, S, D) for r in res.results], axis=0)
    return out.astype(np.float32)


# version 15
# speedup vs baseline: 1.2666x; 1.2666x over previous
import numpy as np
import ml_dtypes
import concourse.bass as bass
import concourse.mybir as mybir
from concourse.bass_utils import run_bass_kernel_spmd

F32 = mybir.dt.float32
BF16 = mybir.dt.bfloat16
I32 = mybir.dt.int32
AF = mybir.ActivationFunctionType
ALU = mybir.AluOpType

B, S, D = 16, 2048, 1024
NCORES = 8
SPC = B // NCORES
NH, DK, NPAIR = 16, 64, 8
NT = S // 128
NQ = S // 512
NKT = S // 128
NPAIR_G = (S // 512) * (S // 128)
DFF = 4096
EXP_SHIFT = -14.0


def _codywaite():
    v = np.float64(2 * np.pi)

    def trunc12(x):
        i = np.frombuffer(np.float32(x).tobytes(), dtype=np.uint32)[0]
        i &= np.uint32(0xFFFFF000)
        return float(np.frombuffer(np.uint32(i).tobytes(), dtype=np.float32)[0])

    c1 = trunc12(v)
    c2 = trunc12(v - c1)
    c3 = float(np.float32(v - c1 - c2))
    return c1, c2, c3


CW1, CW2, CW3 = _codywaite()
INV2PI = float(np.float32(1.0 / (2 * np.pi)))


class Prog:
    ENGINES = ("tensor", "scalar", "vector", "sync", "gpsimd")
    COMPUTE_LANES = ("mm", "act", "dve", "gpc")
    DMA_LANES = ("x0a", "x0b", "pea", "peb", "wv", "wq", "wk", "wo", "w1",
                 "w2", "tx0", "tx1", "tx2", "tx3", "ty0", "ty1", "ty2", "ty3",
                 "otb", "y1st", "y1ld", "out", "sta", "stb", "pest", "sc",
                 "cst", "bcv", "bcg", "bcbt", "bcb2", "dbg")

    def __init__(self):
        self.sched = {e: [] for e in self.ENGINES}
        self.cnt = {k: 0 for k in self.COMPUTE_LANES + self.DMA_LANES}
        self.marks = {}

    def at(self, tag):
        assert tag in self.marks, f"wait on undefined tag {tag}"
        return self.marks[tag]

    def has(self, tag):
        return tag in self.marks

    def emit(self, engine, lane, amt, fn, waits=(), tag=None):
        w = [self.at(t) if isinstance(t, str) else t for t in waits]
        byl = {}
        for (lk, lv) in w:
            byl[lk] = max(byl.get(lk, 0), lv)
        if tag is None and amt == 1:
            amt = 0
        self.sched[engine].append((fn, sorted(byl.items()), lane, amt))
        if amt:
            self.cnt[lane] += amt
            if tag:
                self.marks[tag] = (lane, self.cnt[lane])

    def mm(self, fn, waits=(), tag=None):
        self.emit("tensor", "mm", 1, fn, waits, tag)

    def act(self, fn, waits=(), tag=None):
        self.emit("scalar", "act", 1, fn, waits, tag)

    def dve(self, fn, waits=(), tag=None):
        self.emit("vector", "dve", 1, fn, waits, tag)

    def gp(self, fn, waits=(), tag=None):
        self.emit("gpsimd", "gpc", 1, fn, waits, tag)

    def dma(self, engine, lane, fn, waits=(), tag=None):
        self.emit(engine, lane, 16, fn, waits, tag)

    def barrier_tagged(self, prefix):
        for k, v in self.cnt.items():
            self.marks[f"{prefix}.{k}"] = (k, v)

    def gate(self, prefix):
        return [f"{prefix}.{k}" for k in self.cnt if self.cnt[k] > 0]

    def wait_only(self, engine, waits):
        w = [self.at(t) if isinstance(t, str) else t for t in waits]
        byl = {}
        for (lk, lv) in w:
            byl[lk] = max(byl.get(lk, 0), lv)
        self.sched[engine].append((None, sorted(byl.items()), None, 0))


def build_program(debug_outputs=False):
    nc = bass.Bass("TRN2", target_bir_lowering=False)

    x0_d = nc.dram_tensor("x0", [SPC * S, D], F32, kind="ExternalInput")
    rby_d = nc.dram_tensor("rby", [1, 3], F32, kind="ExternalInput")
    wqT_d = nc.dram_tensor("wqT", [D, D], F32, kind="ExternalInput")
    wkT_d = nc.dram_tensor("wkT", [D, D], F32, kind="ExternalInput")
    wvT_d = nc.dram_tensor("wvT", [D, D], F32, kind="ExternalInput")
    woT_d = nc.dram_tensor("woT", [D, D], F32, kind="ExternalInput")
    w1_d = nc.dram_tensor("w1", [D, DFF], F32, kind="ExternalInput")
    w2_d = nc.dram_tensor("w2", [DFF, D], F32, kind="ExternalInput")
    bv_d = nc.dram_tensor("bv", [1, D], F32, kind="ExternalInput")
    bo_d = nc.dram_tensor("bo", [1, D], F32, kind="ExternalInput")
    b2_d = nc.dram_tensor("b2", [1, D], F32, kind="ExternalInput")
    gamma_d = nc.dram_tensor("gamma", [1, D], F32, kind="ExternalInput")
    beta_d = nc.dram_tensor("beta", [1, D], F32, kind="ExternalInput")
    bqc_d = nc.dram_tensor("bqc", [128, 8], F32, kind="ExternalInput")
    bkc_d = nc.dram_tensor("bkc", [128, 8], F32, kind="ExternalInput")
    b1c_d = nc.dram_tensor("b1c", [128, 32], F32, kind="ExternalInput")
    pos_d = nc.dram_tensor("pos", [128, NT], F32, kind="ExternalInput")
    invd_d = nc.dram_tensor("invd", [1, D], F32, kind="ExternalInput")
    phase_d = nc.dram_tensor("phase", [1, D], F32, kind="ExternalInput")
    eye_d = nc.dram_tensor("eye", [128, 128], BF16, kind="ExternalInput")

    out_d = nc.dram_tensor("out", [SPC * S, D], F32, kind="ExternalOutput")

    pe_b = nc.dram_tensor("pe_b", [NT, 128, D], BF16, kind="Internal")
    peb_b = nc.dram_tensor("peb_b", [NT, 128, D], BF16, kind="Internal")
    y1_b = nc.dram_tensor("y1_b", [NT, 128, D], F32, kind="Internal")
    scl_hbm = nc.dram_tensor("scl_hbm", [1, 4], F32, kind="Internal")

    dbg = {}
    if debug_outputs:
        dbg["xpT"] = nc.dram_tensor("dbg_xpT", [8, 128, S], BF16, kind="ExternalOutput")
        dbg["v0"] = nc.dram_tensor("dbg_v0", [128, 1040], BF16, kind="ExternalOutput")
        dbg["ot"] = nc.dram_tensor("dbg_ot", [8, 128, S], BF16, kind="ExternalOutput")
        dbg["y1"] = nc.dram_tensor("dbg_y1", [NT, 128, D], F32, kind="ExternalOutput")
        dbg["pe"] = nc.dram_tensor("dbg_pe", [NT, 128, D], BF16, kind="ExternalOutput")
        dbg["z15"] = nc.dram_tensor("dbg_z15", [128, D], F32, kind="ExternalOutput")
        dbg["yl15"] = nc.dram_tensor("dbg_yl15", [128, D], F32, kind="ExternalOutput")
        dbg["h28"] = nc.dram_tensor("dbg_h28", [128, 512], BF16, kind="ExternalOutput")
        dbg["y1T0"] = nc.dram_tensor("dbg_y1T0", [128, S], BF16, kind="ExternalOutput")
        dbg["y1bf1"] = nc.dram_tensor("dbg_y1bf1", [128, D], BF16, kind="ExternalOutput")
        dbg["y1T0b"] = nc.dram_tensor("dbg_y1T0b", [128, 128], BF16, kind="ExternalOutput")

    def sb(name, shape, dtype, off):
        return nc.alloc_sbuf_tensor_at(name, shape, dtype, offset=off)

    BASE = 16512
    OFF_SMALL = BASE + 0
    OFF_H = BASE + 2048
    OFF_XPT = BASE + 15360
    OFF_W1R = BASE + 48128
    OFF_W2R = BASE + 113664
    OFF_HR = BASE + 179200

    ones64 = sb("ones64", [1, 64], BF16, OFF_SMALL + 0)
    expbias = sb("expbias", [128, 1], F32, OFF_SMALL + 128)
    zerocol = sb("zerocol", [128, 1], F32, OFF_SMALL + 160)
    eps5col = sb("eps5col", [128, 1], F32, OFF_SMALL + 192)
    bqcol = sb("bqcol", [128, 8], F32, OFF_SMALL + 224)
    bkcol = sb("bkcol", [128, 8], F32, OFF_SMALL + 256)
    b1col = sb("b1col", [128, 32], F32, OFF_SMALL + 288)
    poscol = sb("poscol", [128, NT], F32, OFF_SMALL + 416)
    sccol = sb("sccol", [128, 4], F32, OFF_SMALL + 480)
    rbysb = sb("rbysb", [1, 4], F32, OFF_SMALL + 512)
    scal = sb("scal", [1, 8], F32, OFF_SMALL + 544)
    statc = sb("statc", [128, 24], F32, OFF_SMALL + 576)
    eyesb = sb("eyesb", [128, 128], BF16, OFF_SMALL + 672)

    x0t = [sb(f"x0t{i}", [128, D], F32, OFF_H + i * 4096) for i in range(2)]
    xpbf = [sb(f"xpbf{i}", [128, D], BF16, OFF_H + 8192 + i * 2048) for i in range(2)]
    bv_bc = sb("bv_bc", [128, D], F32, OFF_H + 0)
    gamma_bc = sb("gamma_bc", [128, D], F32, OFF_H + 8192)
    out_st = [sb(f"out_st{i}", [128, D], F32, OFF_H + i * 4096) for i in range(2)]
    y1b2t = sb("y1b2t", [128, D], F32, OFF_H + 8192)
    lnrbuf = sb("lnrbuf", [1, 1024], F32, OFF_H + 4096)
    rinvbf = sb("rinvbf", [1, 1024], BF16, OFF_H + 11264)

    XPT = [sb(f"XPT{i}", [128, S], BF16, OFF_XPT + i * 4096) for i in range(8)]
    Y1T = [sb(f"Y1T{i}", [128, S], BF16, OFF_XPT + i * 4096) for i in range(8)]

    W1R = OFF_W1R
    Vt = [sb(f"Vt{i}", [128, 1040], BF16, W1R + i * 2080) for i in range(NT)]
    KTzp = [sb(f"KTzp{i}", [128, 2 * S], BF16, W1R + 33280 + i * 8192) for i in range(2)]
    QTp = [sb(f"QTp{i}", [128, S], BF16, W1R + 49664 + i * 4096) for i in range(2)]
    Ebuf = [sb(f"Ebuf{i}", [128, 1024], BF16, W1R + 57856 + i * 2048) for i in range(2)]
    Ebuf.append(sb("Ebuf2", [128, 1024], BF16, OFF_H + 0))
    OTBst = sb("OTBst", [64, 512], BF16, W1R + 61952)
    Rinv = sb("Rinv", [64, 512], F32, W1R + 62976)
    w1sb = [sb(f"w1sb{i}", [128, DFF], BF16, W1R + i * 8192) for i in range(8)]
    betab = sb("betab", [128, D], F32, W1R + 49152)
    b2b = sb("b2b", [128, D], F32, W1R + 53248)
    invd_bc = sb("invd_bc", [128, D], F32, W1R + 0)
    phase_bc = sb("phase_bc", [128, D], F32, W1R + 4096)
    ang = sb("ang", [128, D], F32, W1R + 8192)
    angk = sb("angk", [128, D], F32, W1R + 12288)
    angki = sb("angki", [128, D], I32, W1R + 16384)
    angr = sb("angr", [128, D], F32, W1R + 20480)
    pet = sb("pet", [128, D], BF16, W1R + 24576)
    pebt0 = sb("pebt0", [128, D], BF16, W1R + 26624)
    bo_bc = sb("bo_bc", [128, D], F32, W1R + 28672)

    W2R = OFF_W2R
    wvTsb = [sb(f"wvTsb{i}", [128, D], BF16, W2R + i * 2048) for i in range(8)]
    wqTsb = [sb(f"wqTsb{i}", [128, D], BF16, W2R + i * 2048) for i in range(8)]
    wkTsb = [sb(f"wkTsb{i}", [128, D], BF16, W2R + 16384 + i * 2048) for i in range(8)]
    xpe = [sb(f"xpe{i}", [128, D], F32, W2R + i * 4096) for i in range(2)]
    xpb = [sb(f"xpb{i}", [128, D], F32, W2R + 8192 + i * 4096) for i in range(2)]
    vnbuf2 = sb("vnbuf2", [128, D], F32, W2R + 16384)
    OT = [sb(f"OT{i}", [128, S], BF16, W2R + 32768 + i * 4096) for i in range(8)]
    w2sb = [sb(f"w2sb{i}", [128, D], BF16, W2R + i * 2048) for i in range(32)]

    HR = OFF_HR
    woTsb = [sb(f"woTsb{i}", [128, D], BF16, HR + i * 2048) for i in range(8)]
    pebt = [sb(f"pebt{i}", [128, D], BF16, HR + 16384 + i * 2048) for i in range(2)]
    ubuf = sb("ubuf", [128, D], F32, HR + 20480)
    ubuf2 = sb("ubuf2", [128, D], F32, W1R + 57856)
    vnbuf = sb("vnbuf", [128, D], F32, HR + 24576)
    y1bf = [sb(f"y1bf{i}", [128, D], BF16, HR + 28672 + i * 2048) for i in range(2)]
    hsb = [sb(f"hsb{i}", [128, 512], BF16, HR + i * 1024) for i in range(32)]

    PS = [nc.alloc_psum_tensor(f"PS{i}", [128, 1024], F32) for i in range(4)]

    P = Prog()
    sem = {}
    for name in P.COMPUTE_LANES + P.DMA_LANES:
        sem[name] = nc.semaphore(f"sem_{name}", None).__enter__()

    SY, SC, GP = "sync", "scalar", "gpsimd"

    def p0():
        P.dma(GP, "cst", lambda g: g.dma_start(rbysb[:, 0:3], rby_d.ap()))
        P.dma(GP, "cst", lambda g: g.dma_start(invd_bc[:, :], invd_d.ap().partition_broadcast(128)))
        P.dma(GP, "cst", lambda g: g.dma_start(phase_bc[:, :], phase_d.ap().partition_broadcast(128)))
        P.dma(GP, "cst", lambda g: g.dma_start(poscol[:, :], pos_d.ap()))
        P.dma(GP, "cst", lambda g: g.dma_start(bqcol[:, :], bqc_d.ap()))
        P.dma(GP, "cst", lambda g: g.dma_start(bkcol[:, :], bkc_d.ap()))
        P.dma(GP, "cst", lambda g: g.dma_start(b1col[:, :], b1c_d.ap()))
        P.dma(GP, "cst", lambda g: g.dma_start(bo_bc[:, :], bo_d.ap().partition_broadcast(128)))
        P.dma(GP, "cst", lambda g: g.dma_start(eyesb[:, :], eye_d.ap()), (), "csts")

        P.gp(lambda g: g.memset(ones64[:, :], 1.0), (), "ms.ones")
        P.gp(lambda g: g.memset(expbias[:, :], EXP_SHIFT), (), "ms.eb")
        P.gp(lambda g: g.memset(zerocol[:, :], 0.0), (), "ms.z")
        P.gp(lambda g: g.memset(eps5col[:, :], 1e-5), (), "ms.e5")

        r_, b_, y_ = rbysb[0:1, 0:1], rbysb[0:1, 1:2], rbysb[0:1, 2:3]

        def ts1(out, in_, s1, op0, s2=None, op1=None):
            if s2 is None:
                return lambda v: v.tensor_scalar(out, in_, s1, None, op0)
            return lambda v: v.tensor_scalar(out, in_, s1, s2, op0, op1)

        w = ["csts"]
        P.dve(ts1(scal[0:1, 3:4], r_, 0.0125, ALU.mult, 0.125, ALU.add), w, "sc.a")
        P.dve(ts1(scal[0:1, 4:5], b_, 0.00625, ALU.mult), w, "sc.b")
        P.dve(ts1(scal[0:1, 5:6], y_, 0.01875, ALU.mult), w, "sc.c")
        P.dve(lambda v: v.tensor_add(scal[0:1, 6:7], scal[0:1, 3:4], scal[0:1, 4:5]), ["sc.a", "sc.b"], "sc.d")
        P.dve(lambda v: v.tensor_add(scal[0:1, 0:1], scal[0:1, 6:7], scal[0:1, 5:6]), ["sc.c", "sc.d"], "sc.sp")
        P.dve(ts1(scal[0:1, 1:2], b_, 1e-6, ALU.mult, 1e-5, ALU.add), w, "sc.eps")
        P.dve(ts1(scal[0:1, 7:8], y_, 0.1, ALU.mult, 1.0, ALU.add), w, "sc.cy0")
        P.dve(lambda v: v.reciprocal(scal[0:1, 2:3], scal[0:1, 7:8]), ["sc.cy0"], "sc.cy")
        P.dve(ts1(scal[0:1, 4:5], r_, 0.1, ALU.mult, 1.0, ALU.add), ["sc.d"], "sc.g1")
        P.dve(ts1(scal[0:1, 5:6], y_, 0.05, ALU.mult), ["sc.sp"], "sc.g2")
        P.dve(lambda v: v.tensor_add(scal[0:1, 3:4], scal[0:1, 4:5], scal[0:1, 5:6]), ["sc.g1", "sc.g2"], "sc.gsc")
        P.dma(SY, "sc", lambda s: s.dma_start(scl_hbm.ap(), scal[0:1, 0:4]),
              ["sc.sp", "sc.eps", "sc.cy", "sc.gsc"], "sc.st")
        P.dma(SY, "sc", lambda s: s.dma_start(sccol[:, :], scl_hbm.ap().partition_broadcast(128)),
              ["sc.st"], "sc.bc")

        for t in range(NT):
            wd = ["csts", "sc.bc"]
            if t >= 1:
                wd.append(f"pe.st1.{t-1}")
            P.dve(lambda v, t=t: v.tensor_scalar(
                ang[:, :], invd_bc[:, :], sccol[:, 2:3], poscol[:, t:t + 1],
                ALU.mult, ALU.mult), wd, f"pe.ang.{t}")
            P.dve(lambda v: v.tensor_add(ang[:, :], ang[:, :], phase_bc[:, :]),
                  [f"pe.ang.{t}"], f"pe.ph.{t}")
            P.dve(lambda v: v.tensor_scalar(angk[:, :], ang[:, :], INV2PI, None,
                                            ALU.mult), [f"pe.ph.{t}"], f"pe.k0.{t}")
            P.dve(lambda v: v.tensor_copy(angki[:, :], angk[:, :]), [f"pe.k0.{t}"], f"pe.ki.{t}")
            P.dve(lambda v: v.tensor_copy(angk[:, :], angki[:, :]), [f"pe.ki.{t}"], f"pe.kf.{t}")
            P.dve(lambda v: v.scalar_tensor_tensor(
                angr[:, :], angk[:, :], -CW1, ang[:, :], ALU.mult, ALU.add),
                [f"pe.kf.{t}"], f"pe.cw1.{t}")
            P.dve(lambda v: v.scalar_tensor_tensor(
                ang[:, :], angk[:, :], -CW2, angr[:, :], ALU.mult, ALU.add),
                [f"pe.cw1.{t}"], f"pe.cw2.{t}")
            P.dve(lambda v: v.scalar_tensor_tensor(
                angr[:, :], angk[:, :], -CW3, ang[:, :], ALU.mult, ALU.add),
                [f"pe.cw2.{t}"], f"pe.cw.{t}")
            P.act(lambda sc_: sc_.activation(pet[:, :], angr[:, :], AF.Sin),
                  [f"pe.cw.{t}"], f"pe.sin.{t}")
            P.dve(lambda v: v.tensor_add(pebt0[:, :], pet[:, :], bo_bc[:, :]),
                  [f"pe.sin.{t}"], f"pe.peb.{t}")
            P.dma(SY, "pest", lambda s, t=t: s.dma_start(pe_b.ap()[t], pet[:, :]),
                  [f"pe.sin.{t}"], f"pe.st0.{t}")
            P.dma(SY, "pest", lambda s, t=t: s.dma_start(peb_b.ap()[t], pebt0[:, :]),
                  [f"pe.peb.{t}"], f"pe.st1.{t}")
            if debug_outputs:
                P.dma(SY, "dbg", lambda s, t=t: s.dma_start(dbg["pe"].ap()[t], pet[:, :]),
                      [f"pe.sin.{t}"], f"pe.dbg.{t}")
        P.marks["pes.done"] = ("pest", P.cnt["pest"])


    def p1(s):
        base = s * S
        if s > 0:
            for e in P.ENGINES:
                P.wait_only(e, P.gate(f"s{s-1}.end"))
        for t in range(NT):
            i = t % 2
            xl = "x0a" if i == 0 else "x0b"
            pl = "pea" if i == 0 else "peb"
            wts = []
            if t >= 2:
                wts.append(f"s{s}.p1.cast.{t-2}")
            P.dma(SY, xl, lambda sy, t=t, i=i: sy.dma_start(
                x0t[i][:, :], x0_d.ap()[base + t * 128:base + (t + 1) * 128, :]),
                wts, f"s{s}.p1.x0.{t}")
            wts = [f"s{s}.p1.x0.{t}", f"pe.st0.{t}"]
            if t >= 2:
                wts.append(f"s{s}.p1.tr.{t-2}.7")
            P.dma(GP, pl, lambda g, t=t, i=i: g.dma_start(xpbf[i][:, :], pe_b.ap()[t]),
                  wts, f"s{s}.p1.pe.{t}")
            P.dve(lambda v, i=i: v.tensor_tensor(xpbf[i][:, :], x0t[i][:, :],
                                                 xpbf[i][:, :], ALU.add),
                  [f"s{s}.p1.pe.{t}", f"s{s}.p1.x0.{t}"], f"s{s}.p1.cast.{t}")
            for ic in range(8):
                bank = PS[2][:, (ic % 2) * 512:(ic % 2) * 512 + 64].bitcast(BF16)
                wts = [f"s{s}.p1.cast.{t}"]
                if ic == 0 and t >= 1:
                    wts.append(f"s{s}.p1.ev.{t-1}.7")
                if ic >= 2:
                    wts.append(f"s{s}.p1.ev.{t}.{ic-2}")
                P.mm(lambda tns, i=i, ic=ic, bank=bank: tns.transpose(
                    bank, xpbf[i][:, ic * 128:(ic + 1) * 128], eyesb[:, :]),
                    wts, f"s{s}.p1.tr.{t}.{ic}")
                P.dve(lambda v, t=t, ic=ic, bank=bank: v.tensor_copy(
                    XPT[ic][:, t * 128:(t + 1) * 128], bank),
                    [f"s{s}.p1.tr.{t}.{ic}"], f"s{s}.p1.ev.{t}.{ic}")
        if debug_outputs and s == 0:
            for ic in range(8):
                P.dma(SY, "dbg", lambda sy, ic=ic: sy.dma_start(
                    dbg["xpT"].ap()[ic], XPT[ic][:, :]),
                    [f"s{s}.p1.ev.{NT-1}.{ic}"], f"dbg.xpT.{ic}")

    def proj_v(s):
        P.dma(GP, "bcv", lambda g: g.dma_start(bv_bc[:, :], bv_d.ap().partition_broadcast(128)),
              [f"s{s}.p1.cast.{NT-1}", f"s{s}.p1.cast.{NT-2}"], f"s{s}.bv")
        for i in range(8):
            P.dma(GP, "wv", lambda g, i=i: g.dma_start(
                wvTsb[i][:, :], wvT_d.ap()[i * 128:(i + 1) * 128, :]),
                (), f"s{s}.wvT.{i}")
        for t in range(NT):
            wts = [f"pe.peb.{NT-1}", f"pe.sin.{NT-1}", "pes.done"] if s == 0 else []
            P.gp(lambda g, t=t: g.memset(
                Vt[t][:, :].rearrange("p (h c) -> p h c", c=65)[:, :, 64:65], 1.0),
                wts, f"s{s}.vones.{t}")
        for t in range(NT):
            wts = [f"s{s}.p1.ev.{t}.{ic2}" for ic2 in range(8)]
            if t == 0:
                wts += [f"s{s}.wvT.7", f"s{s}.vones.{NT-1}"]
            if t >= 1:
                wts.append(f"s{s}.vev.{t-1}.1")
            for ic in range(8):
                for half in range(2):
                    P.mm(lambda tns, t=t, ic=ic, half=half: tns.matmul(
                        PS[3][:, half * 512:(half + 1) * 512],
                        XPT[ic][:, t * 128:(t + 1) * 128],
                        wvTsb[ic][:, half * 512:(half + 1) * 512],
                        start=(ic == 0), stop=(ic == 7)),
                        wts if (ic == 0 and half == 0) else (),
                        f"s{s}.vmm.{t}.{half}" if ic == 7 else None)
            for half in range(2):
                h0 = half * 8
                P.dve(lambda v, t=t, half=half, h0=h0: v.tensor_tensor(
                    Vt[t][:, :].rearrange("p (h c) -> p h c", c=65)[:, h0:h0 + 8, 0:64],
                    PS[3][:, half * 512:(half + 1) * 512].rearrange("p (h c) -> p h c", c=64),
                    bv_bc[:, half * 512:(half + 1) * 512].rearrange("p (h c) -> p h c", c=64),
                    ALU.add),
                    [f"s{s}.vmm.{t}.{half}", f"s{s}.bv"], f"s{s}.vev.{t}.{half}")
        if debug_outputs and s == 0:
            P.dma(SY, "dbg", lambda sy: sy.dma_start(dbg["v0"].ap(), Vt[0][:, :]),
                  [f"s{s}.vev.0.0", f"s{s}.vev.0.1"], "dbg.v0")

    def proj_qk_steps(s, c, last_proj):
        i = c % 2
        steps = []

        def zero_step():
            wz = []
            if c >= 2:
                wz.append(f"s{s}.attdone.{c-2}")
            P.dve(lambda v, i=i: v.memset(KTzp[i][64:128, 0:S], 0.0), wz, f"s{s}.kz0.{c}")
            P.dve(lambda v, i=i: v.memset(KTzp[i][0:64, S:2 * S], 0.0), wz, f"s{s}.kz1.{c}")

        def mk_mm(kind, tsl, ic):
            wsb = wqTsb if kind == "q" else wkTsb
            def em():
                wts = []
                if ic == 0:
                    wts.append(f"s{s}.wqT.7" if kind == "q" else f"s{s}.wkT.7")
                    if c >= 2:
                        wts.append(f"s{s}.attdone.{c-2}")
                    if last_proj[0] is not None:
                        wts.append(last_proj[0])
                    if last_proj[1] is not None:
                        wts.append(last_proj[1])
                for h2 in range(2):
                    P.mm(lambda tns, tsl=tsl, ic=ic, h2=h2, wsb=wsb: tns.matmul(
                        PS[3][:, h2 * 512:(h2 + 1) * 512],
                        wsb[ic][:, c * 128:(c + 1) * 128],
                        XPT[ic][:, (tsl + h2) * 512:(tsl + h2 + 1) * 512],
                        start=(ic == 0), stop=(ic == 7)),
                        wts if h2 == 0 else (),
                        f"s{s}.{kind}mm.{c}.{tsl + h2}" if ic == 7 else None)
                if ic == 7:
                    if kind == "q":
                        for h2 in range(2):
                            P.dve(lambda v, tsl=tsl, h2=h2: v.tensor_scalar(
                                QTp[i][:, (tsl + h2) * 512:(tsl + h2 + 1) * 512],
                                PS[3][:, h2 * 512:(h2 + 1) * 512],
                                bqcol[:, c:c + 1], sccol[:, 0:1], ALU.add, ALU.mult),
                                [f"s{s}.qmm.{c}.{tsl + h2}", "sc.bc"],
                                f"s{s}.qev.{c}.{tsl + h2}")
                        last_proj[0] = f"s{s}.qev.{c}.{tsl + 1}"
                    else:
                        for h2 in range(2):
                            P.dve(lambda v, tsl=tsl, h2=h2: v.tensor_scalar(
                                KTzp[i][0:64, (tsl + h2) * 512:(tsl + h2 + 1) * 512],
                                PS[3][0:64, h2 * 512:(h2 + 1) * 512],
                                bkcol[0:64, c:c + 1], None, ALU.add),
                                [f"s{s}.kmm.{c}.{tsl + h2}", f"s{s}.kz0.{c}",
                                 f"s{s}.kz1.{c}"], f"s{s}.kevA.{c}.{tsl + h2}")
                            P.dve(lambda v, tsl=tsl, h2=h2: v.tensor_scalar(
                                KTzp[i][64:128, S + (tsl + h2) * 512:S + (tsl + h2 + 1) * 512],
                                PS[3][64:128, h2 * 512:(h2 + 1) * 512],
                                bkcol[64:128, c:c + 1], None, ALU.add),
                                [f"s{s}.kevA.{c}.{tsl + h2}"], f"s{s}.kevB.{c}.{tsl + h2}")
                        last_proj[0] = f"s{s}.kevB.{c}.{tsl + 1}"
            return em

        first = True
        for kind, tsl in (("q", 0), ("q", 2), ("k", 0), ("k", 2)):
            for ic in range(8):
                em = mk_mm(kind, tsl, ic)
                if first:
                    def em0(em=em):
                        zero_step()
                        em()
                    steps.append(em0)
                    first = False
                else:
                    steps.append(em)
        return steps

    def attention_pair(s, c, interleave_steps, last_proj):
        i = c % 2

        def eb(g):
            return Ebuf[g % 3]

        def emit_sa_sb(q, kt):
            sp = kt % 2
            wts = []
            if kt == 0 and q == 0:
                wts += [f"s{s}.qev.{c}.{t}" for t in range(4)]
                wts += [f"s{s}.kevB.{c}.{t}" for t in range(4)]
            if kt >= 2:
                wts.append(f"s{s}.exp.{c}.{q}.{kt-2}")
            elif q > 0:
                wts.append(f"s{s}.exp.{c}.{q-1}.{NKT-2+kt}")
            elif c > 0:
                wts.append(f"s{s}.exp.{c-1}.{NQ-1}.{NKT-2+kt}")
            P.mm(lambda tns, i=i, q=q, kt=kt, sp=sp: tns.matmul(
                PS[sp][:, 0:512],
                KTzp[i][:, kt * 128:(kt + 1) * 128],
                QTp[i][:, q * 512:(q + 1) * 512],
                start=True, stop=True), wts, f"s{s}.sa.{c}.{q}.{kt}")
            P.mm(lambda tns, i=i, q=q, kt=kt, sp=sp: tns.matmul(
                PS[sp][:, 512:1024],
                KTzp[i][:, S + kt * 128:S + (kt + 1) * 128],
                QTp[i][:, q * 512:(q + 1) * 512],
                start=True, stop=True), (), f"s{s}.sb.{c}.{q}.{kt}")

        def emit_exp(q, kt):
            sp = kt % 2
            g = q * NKT + kt
            wts = [f"s{s}.sb.{c}.{q}.{kt}", "ms.eb"]
            gp = g - 3
            if gp >= 0:
                wts.append(f"s{s}.pvB.{c}.{gp // NKT}.{gp % NKT}")
            elif c > 0:
                gq = NPAIR_G + gp
                wts.append(f"s{s}.pvB.{c-1}.{gq // NKT}.{gq % NKT}")
            if c == 0 and g == 2:
                wts += [f"s{s}.vev.{NT-1}.1"]
            P.act(lambda sc_, sp=sp, g=g: sc_.activation(
                eb(g)[:, :], PS[sp][:, :], AF.Exp, bias=expbias[:, :]),
                wts, f"s{s}.exp.{c}.{q}.{kt}")

        def emit_pv(q, kt, extra=()):
            g = q * NKT + kt
            wts = [f"s{s}.exp.{c}.{q}.{kt}"] + list(extra)
            if c == 0 and q == 0:
                wts += [f"s{s}.vev.{kt}.0", f"s{s}.vev.{kt}.1"]
            P.mm(lambda tns, c=c, kt=kt, g=g: tns.matmul(
                PS[2][0:65, 0:512], Vt[kt][:, 130 * c:130 * c + 65],
                eb(g)[:, 0:512],
                start=(kt == 0), stop=(kt == NKT - 1)), wts,
                f"s{s}.pvA.{c}.{q}.{kt}")
            P.mm(lambda tns, c=c, kt=kt, g=g: tns.matmul(
                PS[2][0:65, 512:1024], Vt[kt][:, 130 * c + 65:130 * c + 130],
                eb(g)[:, 512:1024],
                start=(kt == 0), stop=(kt == NKT - 1)), (),
                f"s{s}.pvB.{c}.{q}.{kt}")

        def tail_act(q):
            wts = [f"s{s}.pvA.{c}.{q}.{NKT-1}", f"s{s}.pvB.{c}.{q}.{NKT-1}"]
            P.act(lambda sc_: sc_.activation(lnrbuf[:, :], PS[2][64:65, 0:1024],
                                             AF.Ln), wts, f"s{s}.lnr.{c}.{q}")
            wts = [f"s{s}.lnr.{c}.{q}"]
            if P.has(f"s{s}.Rmm.{c}.{q-1}.B"):
                wts.append(f"s{s}.Rmm.{c}.{q-1}.B")
            if c > 0 and q == 0:
                wts.append(f"s{s}.Rmm.{c-1}.{NQ-1}.B")
            P.act(lambda sc_: sc_.activation(rinvbf[:, :], lnrbuf[:, :], AF.Exp,
                                             scale=-1.0), wts, f"s{s}.rinv.{c}.{q}")

        def tail_pe(q):
            wts = [f"s{s}.rinv.{c}.{q}", "ms.ones"]
            if P.has(f"s{s}.rcpB.{c}.{q-1}"):
                wts.append(f"s{s}.rcpB.{c}.{q-1}")
            if c > 0 and q == 0:
                wts.append(f"s{s}.rcpB.{c-1}.{NQ-1}")
            if last_proj[0] is not None:
                wts.append(last_proj[0])
            P.mm(lambda tns: tns.matmul(PS[3][0:64, 512:1024], ones64[:, :],
                                        rinvbf[0:1, 0:512], start=True, stop=True),
                 wts, f"s{s}.Rmm.{c}.{q}.A")
            P.dve(lambda v: v.tensor_copy(Rinv[:, :], PS[3][0:64, 512:1024]),
                  [f"s{s}.Rmm.{c}.{q}.A"], f"s{s}.rcpA.{c}.{q}")
            P.dve(lambda v, c=c, q=q: v.tensor_tensor(
                OT[c][0:64, q * 512:(q + 1) * 512], PS[2][0:64, 0:512],
                Rinv[:, :], ALU.mult), [f"s{s}.rcpA.{c}.{q}"], f"s{s}.otA.{c}.{q}")
            P.mm(lambda tns: tns.matmul(PS[3][0:64, 512:1024], ones64[:, :],
                                        rinvbf[0:1, 512:1024], start=True, stop=True),
                 [f"s{s}.rcpA.{c}.{q}"], f"s{s}.Rmm.{c}.{q}.B")
            P.dve(lambda v: v.tensor_copy(Rinv[:, :], PS[3][0:64, 512:1024]),
                  [f"s{s}.Rmm.{c}.{q}.B", f"s{s}.otA.{c}.{q}"], f"s{s}.rcpB.{c}.{q}")
            wts = [f"s{s}.rcpB.{c}.{q}"]
            if P.has(f"s{s}.otbdma.{c}.{q-1}"):
                wts.append(f"s{s}.otbdma.{c}.{q-1}")
            elif c > 0:
                wts.append(f"s{s}.otbdma.{c-1}.{NQ-1}")
            P.dve(lambda v: v.tensor_tensor(OTBst[:, :], PS[2][0:64, 512:1024],
                                            Rinv[:, :], ALU.mult), wts, f"s{s}.otB.{c}.{q}")
            P.dma(SY, "otb", lambda sy, c=c, q=q: sy.dma_start(
                OT[c][64:128, q * 512:(q + 1) * 512], OTBst[:, :]),
                [f"s{s}.otB.{c}.{q}"], f"s{s}.otbdma.{c}.{q}")
            last_proj[1] = f"s{s}.rcpB.{c}.{q}"

        def pop():
            if interleave_steps:
                interleave_steps.pop(0)()

        for q in range(NQ):
            for kt in range(NKT):
                if kt < 3:
                    emit_sa_sb(q, kt)
                    emit_exp(q, kt)
                    if kt == 1 and q > 0:
                        tail_pe(q - 1)
                elif kt == 3:
                    emit_sa_sb(q, kt)
                    extra = []
                    if q > 0:
                        extra = [f"s{s}.otA.{c}.{q-1}", f"s{s}.otB.{c}.{q-1}"]
                    elif c > 0:
                        extra = [f"s{s}.otA.{c-1}.{NQ-1}", f"s{s}.otB.{c-1}.{NQ-1}"]
                    emit_pv(q, 0, extra=extra)
                    emit_pv(q, 1)
                    emit_pv(q, 2)
                    emit_exp(q, kt)
                else:
                    emit_sa_sb(q, kt)
                    emit_pv(q, kt - 1)
                    emit_exp(q, kt)
                    if 4 <= kt <= 11:
                        pop()
            emit_pv(q, NKT - 1)
            tail_act(q)
        tail_pe(NQ - 1)
        while interleave_steps:
            interleave_steps.pop(0)()
        P.marks[f"s{s}.attdone.{c}"] = ("dve", P.cnt["dve"])

    def p23(s):
        proj_v(s)
        for ic in range(8):
            P.dma(GP, "wq", lambda g, ic=ic: g.dma_start(
                wqTsb[ic][:, :], wqT_d.ap()[ic * 128:(ic + 1) * 128, :]),
                [f"s{s}.vmm.{NT-1}.1"] if ic == 0 else (), f"s{s}.wqT.{ic}")
        for ic in range(8):
            P.dma(GP, "wk", lambda g, ic=ic: g.dma_start(
                wkTsb[ic][:, :], wkT_d.ap()[ic * 128:(ic + 1) * 128, :]),
                (), f"s{s}.wkT.{ic}")
        last_proj = [None, None]
        for st in proj_qk_steps(s, 0, last_proj):
            st()
        for c in range(NPAIR):
            steps = proj_qk_steps(s, c + 1, last_proj) if c + 1 < NPAIR else []
            attention_pair(s, c, steps, last_proj)
        P.barrier_tagged(f"s{s}.attend")
        if debug_outputs and s == 0:
            for ic in range(8):
                P.dma(SY, "dbg", lambda sy, ic=ic: sy.dma_start(
                    dbg["ot"].ap()[ic], OT[ic][:, :]),
                    [f"s{s}.otA.{ic}.{NQ-1}", f"s{s}.otbdma.{ic}.{NQ-1}"],
                    f"dbg.ot.{ic}")

    def p4(s):
        base = s * S
        gate = [f"s{s}.attend.mm", f"s{s}.attend.dve", f"s{s}.attend.act",
                f"s{s}.attend.otb"]
        for e in P.ENGINES:
            P.wait_only(e, gate)
        for ic in range(8):
            P.dma(GP, "wo", lambda g, ic=ic: g.dma_start(
                woTsb[ic][:, :], woT_d.ap()[ic * 128:(ic + 1) * 128, :]),
                (), f"s{s}.woT.{ic}")
        P.dma(GP, "bcg", lambda g: g.dma_start(gamma_bc[:, :], gamma_d.ap().partition_broadcast(128)),
              (), f"s{s}.gbc0")
        P.dve(lambda v: v.tensor_scalar(gamma_bc[:, :], gamma_bc[:, :], sccol[:, 3:4],
                                        None, ALU.mult), [f"s{s}.gbc0", "sc.bc"], f"s{s}.gbc")
        P.dma(GP, "bcbt", lambda g: g.dma_start(betab[:, :], beta_d.ap().partition_broadcast(128)),
              (), f"s{s}.bbc")
        P.dma(GP, "bcb2", lambda g: g.dma_start(b2b[:, :], b2_d.ap().partition_broadcast(128)),
              (), f"s{s}.b2bc")
        for ic in (0, 1, 2, 3, 4, 5):
            P.dma(GP, "w1", lambda g, ic=ic: g.dma_start(
                w1sb[ic][:, :], w1_d.ap()[ic * 128:(ic + 1) * 128, :]),
                (), f"s{s}.w1.{ic}")

        def ln_stats(pref, srcb, wtag, so):
            P.act(lambda sc_, srcb=srcb, so=so: sc_.activation(
                PS[1][:, :], srcb[:, :], AF.Copy, accum_out=statc[:, so:so + 1]),
                [wtag], f"{pref}.s1")
            P.act(lambda sc_, srcb=srcb, so=so: sc_.activation(
                PS[1][:, :], srcb[:, :], AF.Square, accum_out=statc[:, so + 2:so + 3]),
                [wtag], f"{pref}.s2")
            P.dve(lambda v, so=so: v.tensor_scalar(
                statc[:, so + 5:so + 6], statc[:, so:so + 1], 1.0 / D, None, ALU.mult),
                [f"{pref}.s1"], f"{pref}.m")
            P.dve(lambda v, so=so: v.tensor_tensor(
                statc[:, so + 1:so + 2], statc[:, so:so + 1], statc[:, so:so + 1],
                ALU.mult), [f"{pref}.s1"], f"{pref}.t1")
            P.dve(lambda v, so=so: v.scalar_tensor_tensor(
                statc[:, so + 7:so + 8], statc[:, so + 1:so + 2], -1.0 / D,
                statc[:, so + 2:so + 3], ALU.mult, ALU.add),
                [f"{pref}.t1", f"{pref}.s2"], f"{pref}.var")

        UB = [ubuf, ubuf2]

        def stage_a(t):
            i = t % 2
            so = 12 * i
            xl = "x0a" if i == 0 else "x0b"
            pl = "pea" if i == 0 else "peb"
            ps = PS[3] if t % 2 == 0 else PS[0]
            ub = UB[i]
            wts = [] if t < 2 else [f"s{s}.p4.xpe.{t-2}", f"s{s}.p4.xpb.{t-2}"]
            P.dma(SY, xl, lambda sy, t=t, i=i: sy.dma_start(
                x0t[i][:, :], x0_d.ap()[base + t * 128:base + (t + 1) * 128, :]),
                wts, f"s{s}.p4.x0.{t}")
            wts = [f"pe.st1.{t}"] if t < 2 else [f"s{s}.p4.xpe.{t-2}", f"pe.st1.{t}"]
            P.dma(GP, pl, lambda g, t=t, i=i: g.dma_start(pebt[i][:, :], peb_b.ap()[t]),
                  wts, f"s{s}.p4.pe.{t}")
            wts = [f"s{s}.p4.x0.{t}", f"s{s}.p4.pe.{t}"]
            if t >= 2:
                wts.append(f"s{s}.p4.u.{t-2}")
            P.gp(lambda g, i=i: g.tensor_tensor(xpe[i][:, :], x0t[i][:, :],
                                                pebt[i][:, :], ALU.add),
                 wts, f"s{s}.p4.xpe.{t}")
            wts = [f"s{s}.p4.x0.{t}", f"s{s}.bbc"]
            if t >= 2:
                wts.append(f"s{s}.p4.w.{t-2}")
            P.gp(lambda g, i=i: g.tensor_tensor(xpb[i][:, :], x0t[i][:, :],
                                                betab[:, :], ALU.add),
                 wts, f"s{s}.p4.xpb.{t}")
            wts = [f"s{s}.woT.7"]
            if t >= 2:
                wts.append(f"s{s}.p4.u.{t-2}")
            for ic in range(8):
                for half in range(2):
                    P.mm(lambda tns, t=t, ic=ic, half=half, ps=ps: tns.matmul(
                        ps[:, half * 512:(half + 1) * 512],
                        OT[ic][:, t * 128:(t + 1) * 128],
                        woTsb[ic][:, half * 512:(half + 1) * 512],
                        start=(ic == 0), stop=(ic == 7)),
                        wts if (ic == 0 and half == 0) else (),
                        f"s{s}.p4.mm.{t}" if (ic == 7 and half == 1) else None)
            P.dve(lambda v, ps=ps, i=i, ub=ub: v.tensor_tensor(
                ub[:, :], ps[:, :], xpe[i][:, :], ALU.add),
                  [f"s{s}.p4.mm.{t}", f"s{s}.p4.xpe.{t}"], f"s{s}.p4.u.{t}")
            ln_stats(f"s{s}.p4.a{t}", ub, f"s{s}.p4.u.{t}", so)
            P.act(lambda sc_, so=so: sc_.activation(statc[:, so + 9:so + 10],
                                                    statc[:, so + 7:so + 8], AF.Ln,
                                                    bias=sccol[:, 1:2], scale=1.0 / D),
                  [f"s{s}.p4.a{t}.var", "sc.bc"], f"s{s}.p4.ln.{t}")
            P.act(lambda sc_, so=so: sc_.activation(statc[:, so + 8:so + 9],
                                                    statc[:, so + 9:so + 10], AF.Exp,
                                                    scale=-0.5), [f"s{s}.p4.ln.{t}"],
                  f"s{s}.p4.rstd.{t}")

        VN = [vnbuf, vnbuf2]

        def stage_b(t):
            i = t % 2
            so = 12 * i
            ub = UB[i]
            vn = VN[i]
            wts = [f"s{s}.p4.rstd.{t}"]
            if t >= 2:
                wts.append(f"s{s}.p4.y1dma.{t-2}")
            P.dve(lambda v, ub=ub, so=so, vn=vn: v.tensor_scalar(
                vn[:, :], ub[:, :], statc[:, so + 5:so + 6],
                statc[:, so + 8:so + 9], ALU.subtract, ALU.mult),
                wts, f"s{s}.p4.n1.{t}")
            P.dve(lambda v, vn=vn: v.tensor_tensor(vn[:, :], vn[:, :], gamma_bc[:, :],
                                                   ALU.mult),
                  [f"s{s}.p4.n1.{t}", f"s{s}.gbc"], f"s{s}.p4.n2.{t}")
            P.dve(lambda v, i=i, ub=ub, vn=vn: v.tensor_tensor(
                ub[:, :], vn[:, :], xpb[i][:, :], ALU.add),
                  [f"s{s}.p4.n2.{t}", f"s{s}.p4.xpb.{t}"], f"s{s}.p4.w.{t}")
            ln_stats(f"s{s}.p4.b{t}", ub, f"s{s}.p4.w.{t}", so)
            P.act(lambda sc_, so=so: sc_.activation(statc[:, so + 9:so + 10],
                                                    statc[:, so + 7:so + 8], AF.Ln,
                                                    bias=eps5col[:, :], scale=1.0 / D),
                  [f"s{s}.p4.b{t}.var", "ms.e5"], f"s{s}.p4.lnb.{t}")
            P.act(lambda sc_, so=so: sc_.activation(statc[:, so + 8:so + 9],
                                                    statc[:, so + 9:so + 10], AF.Exp,
                                                    scale=-0.5), [f"s{s}.p4.lnb.{t}"],
                  f"s{s}.p4.rstdb.{t}")
            P.dve(lambda v, ub=ub, so=so, vn=vn: v.tensor_scalar(
                vn[:, :], ub[:, :], statc[:, so + 5:so + 6],
                statc[:, so + 8:so + 9], ALU.subtract, ALU.mult),
                [f"s{s}.p4.rstdb.{t}"], f"s{s}.p4.y1.{t}")
            wts = [f"s{s}.p4.y1.{t}"]
            if t >= 2:
                wts.append(f"s{s}.p4.tr.{t-2}.7")
            P.dve(lambda v, i=i, vn=vn: v.tensor_copy(y1bf[i][:, :], vn[:, :]), wts,
                  f"s{s}.p4.y1bf.{t}")
            P.gp(lambda g, vn=vn: g.tensor_tensor(vn[:, :], vn[:, :], b2b[:, :], ALU.add),
                 [f"s{s}.p4.y1bf.{t}", f"s{s}.b2bc"], f"s{s}.p4.y1b2.{t}")
            P.dma(SY, "y1st", lambda sy, t=t, vn=vn: sy.dma_start(y1_b.ap()[t], vn[:, :]),
                  [f"s{s}.p4.y1b2.{t}"], f"s{s}.p4.y1dma.{t}")
            if debug_outputs and s == 0:
                P.dma(SY, "dbg", lambda sy, t=t, vn=vn: sy.dma_start(dbg["y1"].ap()[t], vn[:, :]),
                      [f"s{s}.p4.y1b2.{t}"], f"dbg.y1.{t}")
            for ic in range(8):
                bank = PS[2][:, (ic % 2) * 512:(ic % 2) * 512 + 64].bitcast(BF16)
                wts = [f"s{s}.p4.y1bf.{t}"]
                if ic == 0 and t >= 1:
                    wts.append(f"s{s}.p4.ev.{t-1}.7")
                if ic >= 2:
                    wts.append(f"s{s}.p4.ev.{t}.{ic-2}")
                P.mm(lambda tns, i=i, ic=ic, bank=bank: tns.transpose(
                    bank, y1bf[i][:, ic * 128:(ic + 1) * 128], eyesb[:, :]),
                    wts, f"s{s}.p4.tr.{t}.{ic}")
                P.dve(lambda v, t=t, ic=ic, bank=bank: v.tensor_copy(
                    Y1T[ic][:, t * 128:(t + 1) * 128], bank),
                    [f"s{s}.p4.tr.{t}.{ic}"], f"s{s}.p4.ev.{t}.{ic}")

        stage_a(0)
        for t in range(1, NT):
            stage_a(t)
            stage_b(t - 1)
        stage_b(NT - 1)
        P.marks[f"s{s}.y1all"] = ("y1st", P.cnt["y1st"])
        if debug_outputs and s == 0:
            P.dma(SY, "dbg", lambda sy: sy.dma_start(dbg["y1T0"].ap(), Y1T[0][:, :]),
                  [f"s{s}.p4.tr.{NT-1}.6", f"s{s}.p4.tr.{NT-1}.7"], "dbg.y1T0")
            P.dma(SY, "dbg", lambda sy: sy.dma_start(dbg["y1bf1"].ap(), y1bf[1][:, :]),
                  [f"s{s}.p4.tr.{NT-1}.6", f"s{s}.p4.tr.{NT-1}.7"], "dbg.y1bf1")

    def p5(s):
        base = s * S
        P.dma(GP, "w1", lambda g: g.dma_start(
            w1sb[6][:, :], w1_d.ap()[6 * 128:7 * 128, :]),
            [f"s{s}.p4.xpb.{NT-1}", f"s{s}.p4.y1b2.{NT-1}"], f"s{s}.w1.6")
        P.dma(GP, "w1", lambda g: g.dma_start(
            w1sb[7][:, :], w1_d.ap()[7 * 128:8 * 128, :]),
            [f"s{s}.p4.y1.{NT-1}"], f"s{s}.w1.7")
        w2gate = {0: [f"s{s}.p4.u.{NT-2}"], 2: [f"s{s}.p4.u.{NT-1}"],
                  4: [f"s{s}.p4.w.{NT-2}"], 6: [f"s{s}.p4.w.{NT-1}"],
                  8: [f"s{s}.p4.y1dma.{NT-1}"], 9: [f"s{s}.p4.y1dma.{NT-1}"],
                  16: [f"s{s}.p4.mm.{NT-1}"]}
        for i in range(32):
            P.dma(GP, "w2", lambda g, i=i: g.dma_start(
                w2sb[i][:, :], w2_d.ap()[i * 128:(i + 1) * 128, :]),
                w2gate.get(i, ()), f"s{s}.w2.{i}")
        for tq in range(NQ):
            for oc in range(32):
                pbank = oc % 2
                wts = [f"s{s}.w1.7", f"s{s}.w1.6"]
                if tq > 0 or oc >= 2:
                    pt, po = (tq, oc - 2) if oc >= 2 else (tq - 1, 30 + oc)
                    wts.append(f"s{s}.f1ev.{pt}.{po}")
                if tq == 0 and oc == 0:
                    wts += [f"s{s}.p4.ev.{NT-1}.{ic2}" for ic2 in range(8)]
                    wts += [f"s{s}.p4.ev.{NT-2}.{ic2}" for ic2 in range(8)]
                for ic in range(8):
                    P.mm(lambda tns, tq=tq, oc=oc, ic=ic, pbank=pbank: tns.matmul(
                        PS[0][:, pbank * 512:(pbank + 1) * 512],
                        w1sb[ic][:, oc * 128:(oc + 1) * 128],
                        Y1T[ic][:, tq * 512:(tq + 1) * 512],
                        start=(ic == 0), stop=(ic == 7)),
                        wts if ic == 0 else (),
                        f"s{s}.f1.{tq}.{oc}" if ic == 7 else None)
                if debug_outputs and s == 0 and tq == 0 and oc == 31:
                    P.dma(SY, "dbg", lambda sy: sy.dma_start(
                        dbg["y1T0b"].ap(), Y1T[0][:, 1920:2048]),
                        [f"s{s}.f1.{tq}.{oc}"], "dbg.y1T0b")
                wts = [f"s{s}.f1.{tq}.{oc}", "csts", "ms.z"]
                if tq == 0:
                    wts.append(f"s{s}.p4.y1dma.{NT-1}")
                if tq > 0:
                    wts.append(f"s{s}.f2.{tq * 4 - 1}")
                P.dve(lambda v, oc=oc, pbank=pbank: v.tensor_scalar(
                    hsb[oc][:, :], PS[0][:, pbank * 512:(pbank + 1) * 512],
                    b1col[:, oc:oc + 1], zerocol[:, 0:1], ALU.add, ALU.max),
                    wts, f"s{s}.f1ev.{tq}.{oc}")
            for tt in range(4):
                t = tq * 4 + tt
                ps2 = PS[3] if t % 2 == 0 else PS[2]
                wts = [f"s{s}.y1all"]
                if P.has(f"s{s}.p5.z.{t-1}"):
                    wts.append(f"s{s}.p5.z.{t-1}")
                if t == 0:
                    wts.append(f"s{s}.p4.n2.{NT-1}")
                P.dma(GP, "y1ld", lambda g, t=t: g.dma_start(y1b2t[:, :], y1_b.ap()[t]),
                      wts, f"s{s}.p5.y1.{t}")
                wts = [f"s{s}.f1ev.{tq}.31", f"s{s}.w2.31"]
                if P.has(f"s{s}.p5.z.{t-2}"):
                    wts.append(f"s{s}.p5.z.{t-2}")
                for wc in range(32):
                    for half in range(2):
                        P.mm(lambda tns, tt=tt, wc=wc, half=half, ps2=ps2: tns.matmul(
                            ps2[:, half * 512:(half + 1) * 512],
                            hsb[wc][:, tt * 128:(tt + 1) * 128],
                            w2sb[wc][:, half * 512:(half + 1) * 512],
                            start=(wc == 0), stop=(wc == 31)),
                            wts if (wc == 0 and half == 0) else (),
                            f"s{s}.f2.{t}" if (wc == 31 and half == 1) else None)
                P.dve(lambda v, ps2=ps2: v.tensor_tensor(out_st[1][:, :], ps2[:, :],
                                                         y1b2t[:, :], ALU.add),
                      [f"s{s}.f2.{t}", f"s{s}.p5.y1.{t}"], f"s{s}.p5.z.{t}")
                pref = f"s{s}.p5.c{t}"
                P.dve(lambda v: v.reduce_sum(statc[:, 0:1], out_st[1][:, :],
                                             mybir.AxisListType.X),
                      [f"s{s}.p5.z.{t}"], f"{pref}.s1")
                P.act(lambda sc_: sc_.activation(PS[1][:, :], out_st[1][:, :],
                                                 AF.Square, accum_out=statc[:, 2:3]),
                      [f"s{s}.p5.z.{t}"], f"{pref}.s2")
                P.dve(lambda v: v.tensor_scalar(statc[:, 5:6], statc[:, 0:1],
                                                1.0 / D, None, ALU.mult),
                      [f"{pref}.s1"], f"{pref}.m")
                P.dve(lambda v: v.tensor_tensor(statc[:, 1:2], statc[:, 0:1],
                                                statc[:, 0:1], ALU.mult),
                      [f"{pref}.s1"], f"{pref}.t1")
                P.dve(lambda v: v.scalar_tensor_tensor(
                    statc[:, 7:8], statc[:, 1:2], -1.0 / D, statc[:, 2:3],
                    ALU.mult, ALU.add), [f"{pref}.t1", f"{pref}.s2"], f"{pref}.var")
                P.act(lambda sc_: sc_.activation(statc[:, 9:10], statc[:, 7:8], AF.Ln,
                                                 bias=eps5col[:, :], scale=1.0 / D),
                      [f"{pref}.var"], f"{pref}.ln")
                P.act(lambda sc_: sc_.activation(statc[:, 8:9], statc[:, 9:10], AF.Exp,
                                                 scale=-0.5), [f"{pref}.ln"], f"{pref}.rstd")
                if debug_outputs and s == 0 and t == 15:
                    P.dma(SY, "dbg", lambda sy: sy.dma_start(dbg["z15"].ap(), out_st[1][:, :]),
                          [f"s{s}.p5.z.{t}"], "dbg.z15")
                    P.dma(SY, "dbg", lambda sy: sy.dma_start(dbg["yl15"].ap(), y1b2t[:, :]),
                          [f"s{s}.p5.z.{t}"], "dbg.yl15")
                    P.dma(SY, "dbg", lambda sy: sy.dma_start(dbg["h28"].ap(), hsb[28][:, :]),
                          [f"s{s}.p5.z.{t}"], "dbg.h28")
                wts = [f"{pref}.rstd"]
                if P.has(f"s{s}.out.{t-1}"):
                    wts.append(f"s{s}.out.{t-1}")
                P.dve(lambda v: v.tensor_scalar(out_st[0][:, :], out_st[1][:, :],
                                                statc[:, 5:6], statc[:, 8:9],
                                                ALU.subtract, ALU.mult),
                      wts, f"s{s}.p5.y2.{t}")
                P.dma(SY, "out", lambda sy, t=t: sy.dma_start(
                    out_d.ap()[base + t * 128:base + (t + 1) * 128, :], out_st[0][:, :]),
                    [f"s{s}.p5.y2.{t}"], f"s{s}.out.{t}")
        P.barrier_tagged(f"s{s}.end")

    p0()
    for s in range(SPC):
        p1(s)
        p23(s)
        p4(s)
        p5(s)

    with nc.Block() as block:
        def run(eng, lst, final_waits=()):
            for fn, waits, lane, amt in lst:
                for (wk, wv) in waits:
                    eng.wait_ge(sem[wk], wv)
                if fn is None:
                    continue
                instr = fn(eng)
                if amt:
                    instr.then_inc(sem[lane], amt)
            for (wk, wv) in final_waits:
                eng.wait_ge(sem[wk], wv)

        final = [(k, v) for k, v in P.cnt.items() if v > 0]

        @block.tensor
        def _(t):
            run(t, P.sched["tensor"])

        @block.scalar
        def _(s_):
            run(s_, P.sched["scalar"])

        @block.vector
        def _(v):
            run(v, P.sched["vector"])

        @block.sync
        def _(sy):
            run(sy, P.sched["sync"], final)

        @block.gpsimd
        def _(g):
            run(g, P.sched["gpsimd"])

    return nc


_cached = {}


def kernel(L0_attn_x, L0_attn_rby, L0_attn_Wq, L0_attn_bq, L0_attn_Wk,
           L0_attn_bk, L0_attn_Wv, L0_attn_bv, L0_attn_Wo, L0_attn_bo,
           L0_attn_ln_gamma, L0_attn_ln_beta, L0_ffn_w1, L0_ffn_b1,
           L0_ffn_w2, L0_ffn_b2):
    x = np.ascontiguousarray(np.asarray(L0_attn_x, dtype=np.float32))

    j = np.arange(512, dtype=np.float64)
    invd = np.repeat(np.power(10000.0, -j / 256.0), 2).astype(np.float32)[None, :]
    phase = np.zeros((1, D), np.float32)
    phase[0, 1::2] = np.float32(np.pi / 2)
    pos = (np.arange(128, dtype=np.float32)[:, None]
           + 128.0 * np.arange(NT, dtype=np.float32)[None, :])

    def col128(v, ncols):
        return np.ascontiguousarray(np.asarray(v, np.float32).reshape(ncols, 128).T)

    common = dict(
        rby=np.asarray(L0_attn_rby, np.float32).reshape(1, 3),
        wqT=np.ascontiguousarray(np.asarray(L0_attn_Wq, np.float32).T),
        wkT=np.ascontiguousarray(np.asarray(L0_attn_Wk, np.float32).T),
        wvT=np.ascontiguousarray(np.asarray(L0_attn_Wv, np.float32).T),
        woT=np.ascontiguousarray(np.asarray(L0_attn_Wo, np.float32).T),
        w1=np.ascontiguousarray(np.asarray(L0_ffn_w1, np.float32)),
        w2=np.ascontiguousarray(np.asarray(L0_ffn_w2, np.float32)),
        bv=np.asarray(L0_attn_bv, np.float32).reshape(1, D),
        bo=np.asarray(L0_attn_bo, np.float32).reshape(1, D),
        b2=np.asarray(L0_ffn_b2, np.float32).reshape(1, D),
        gamma=np.asarray(L0_attn_ln_gamma, np.float32).reshape(1, D),
        beta=np.asarray(L0_attn_ln_beta, np.float32).reshape(1, D),
        bqc=col128(L0_attn_bq, 8),
        bkc=col128(L0_attn_bk, 8),
        b1c=col128(L0_ffn_b1, 32),
        pos=np.ascontiguousarray(pos), invd=invd, phase=phase,
        eye=np.eye(128, dtype=np.float32).astype(ml_dtypes.bfloat16),
    )
    in_maps = []
    for c in range(NCORES):
        m = dict(common)
        m["x0"] = np.ascontiguousarray(x[c * SPC:(c + 1) * SPC].reshape(SPC * S, D))
        in_maps.append(m)

    if "nc" not in _cached:
        _cached["nc"] = build_program()
    nc = _cached["nc"]
    res = run_bass_kernel_spmd(nc, in_maps, core_ids=list(range(NCORES)))
    out = np.concatenate(
        [np.asarray(r["out"]).reshape(SPC, S, D) for r in res.results], axis=0)
    return out.astype(np.float32)


# version 17 (final)
# speedup vs baseline: 1.3508x; 1.0665x over previous
import numpy as np
import ml_dtypes
import concourse.bass as bass
import concourse.mybir as mybir
from concourse.bass_utils import run_bass_kernel_spmd

F32 = mybir.dt.float32
BF16 = mybir.dt.bfloat16
I32 = mybir.dt.int32
AF = mybir.ActivationFunctionType
ALU = mybir.AluOpType

B, S, D = 16, 2048, 1024
NCORES = 8
SPC = B // NCORES
NH, DK, NPAIR = 16, 64, 8
NT = S // 128
NQ = S // 512
NKT = S // 128
NPAIR_G = (S // 512) * (S // 128)
DFF = 4096
EXP_SHIFT = -14.0


def _codywaite():
    v = np.float64(2 * np.pi)

    def trunc12(x):
        i = np.frombuffer(np.float32(x).tobytes(), dtype=np.uint32)[0]
        i &= np.uint32(0xFFFFF000)
        return float(np.frombuffer(np.uint32(i).tobytes(), dtype=np.float32)[0])

    c1 = trunc12(v)
    c2 = trunc12(v - c1)
    c3 = float(np.float32(v - c1 - c2))
    return c1, c2, c3


CW1, CW2, CW3 = _codywaite()
INV2PI = float(np.float32(1.0 / (2 * np.pi)))


class Prog:
    ENGINES = ("tensor", "scalar", "vector", "sync", "gpsimd")
    COMPUTE_LANES = ("mm", "act", "dve", "gpc")
    DMA_LANES = ("x0a", "x0b", "x0c", "x0d", "pea", "peb", "pec", "ped",
                 "wv", "wq", "wk", "wo", "w1",
                 "w2", "tx0", "tx1", "tx2", "tx3", "ty0", "ty1", "ty2", "ty3",
                 "otb", "y1st", "y1s2", "y1ld", "out", "sta", "stb", "pest", "sc",
                 "cst", "bcv", "bcg", "bcbt", "bcb2", "dbg")

    def __init__(self):
        self.sched = {e: [] for e in self.ENGINES}
        self.cnt = {k: 0 for k in self.COMPUTE_LANES + self.DMA_LANES}
        self.marks = {}

    def at(self, tag):
        assert tag in self.marks, f"wait on undefined tag {tag}"
        return self.marks[tag]

    def has(self, tag):
        return tag in self.marks

    def emit(self, engine, lane, amt, fn, waits=(), tag=None):
        w = [self.at(t) if isinstance(t, str) else t for t in waits]
        byl = {}
        for (lk, lv) in w:
            byl[lk] = max(byl.get(lk, 0), lv)
        if tag is None and amt == 1:
            amt = 0
        self.sched[engine].append((fn, sorted(byl.items()), lane, amt))
        if amt:
            self.cnt[lane] += amt
            if tag:
                self.marks[tag] = (lane, self.cnt[lane])

    def mm(self, fn, waits=(), tag=None):
        self.emit("tensor", "mm", 1, fn, waits, tag)

    def act(self, fn, waits=(), tag=None):
        self.emit("scalar", "act", 1, fn, waits, tag)

    def dve(self, fn, waits=(), tag=None):
        self.emit("vector", "dve", 1, fn, waits, tag)

    def gp(self, fn, waits=(), tag=None):
        self.emit("gpsimd", "gpc", 1, fn, waits, tag)

    def dma(self, engine, lane, fn, waits=(), tag=None):
        self.emit(engine, lane, 16, fn, waits, tag)

    def barrier_tagged(self, prefix):
        for k, v in self.cnt.items():
            self.marks[f"{prefix}.{k}"] = (k, v)

    def gate(self, prefix):
        return [f"{prefix}.{k}" for k in self.cnt if self.cnt[k] > 0]

    def wait_only(self, engine, waits):
        w = [self.at(t) if isinstance(t, str) else t for t in waits]
        byl = {}
        for (lk, lv) in w:
            byl[lk] = max(byl.get(lk, 0), lv)
        self.sched[engine].append((None, sorted(byl.items()), None, 0))


def build_program(debug_outputs=False):
    nc = bass.Bass("TRN2", target_bir_lowering=False)

    x0_d = nc.dram_tensor("x0", [SPC * S, D], F32, kind="ExternalInput")
    rby_d = nc.dram_tensor("rby", [1, 3], F32, kind="ExternalInput")
    wqT_d = nc.dram_tensor("wqT", [D, D], F32, kind="ExternalInput")
    wkT_d = nc.dram_tensor("wkT", [D, D], F32, kind="ExternalInput")
    wvT_d = nc.dram_tensor("wvT", [D, D], F32, kind="ExternalInput")
    woT_d = nc.dram_tensor("woT", [D, D], F32, kind="ExternalInput")
    w1_d = nc.dram_tensor("w1", [D, DFF], F32, kind="ExternalInput")
    w2_d = nc.dram_tensor("w2", [DFF, D], F32, kind="ExternalInput")
    bv_d = nc.dram_tensor("bv", [1, D], F32, kind="ExternalInput")
    bo_d = nc.dram_tensor("bo", [1, D], F32, kind="ExternalInput")
    b2_d = nc.dram_tensor("b2", [1, D], F32, kind="ExternalInput")
    gamma_d = nc.dram_tensor("gamma", [1, D], F32, kind="ExternalInput")
    beta_d = nc.dram_tensor("beta", [1, D], F32, kind="ExternalInput")
    bqc_d = nc.dram_tensor("bqc", [128, 8], F32, kind="ExternalInput")
    bkc_d = nc.dram_tensor("bkc", [128, 8], F32, kind="ExternalInput")
    b1c_d = nc.dram_tensor("b1c", [128, 32], F32, kind="ExternalInput")
    pos_d = nc.dram_tensor("pos", [128, NT], F32, kind="ExternalInput")
    invd_d = nc.dram_tensor("invd", [1, D], F32, kind="ExternalInput")
    phase_d = nc.dram_tensor("phase", [1, D], F32, kind="ExternalInput")
    eye_d = nc.dram_tensor("eye", [128, 128], BF16, kind="ExternalInput")

    out_d = nc.dram_tensor("out", [SPC * S, D], F32, kind="ExternalOutput")

    pe_b = nc.dram_tensor("pe_b", [NT, 128, D], BF16, kind="Internal")
    peb_b = nc.dram_tensor("peb_b", [NT, 128, D], BF16, kind="Internal")
    y1_b = nc.dram_tensor("y1_b", [NT, 128, D], F32, kind="Internal")
    scl_hbm = nc.dram_tensor("scl_hbm", [1, 4], F32, kind="Internal")

    dbg = {}
    if debug_outputs:
        dbg["xpT"] = nc.dram_tensor("dbg_xpT", [8, 128, S], BF16, kind="ExternalOutput")
        dbg["v0"] = nc.dram_tensor("dbg_v0", [128, 1040], BF16, kind="ExternalOutput")
        dbg["ot"] = nc.dram_tensor("dbg_ot", [8, 128, S], BF16, kind="ExternalOutput")
        dbg["y1"] = nc.dram_tensor("dbg_y1", [NT, 128, D], F32, kind="ExternalOutput")
        dbg["pe"] = nc.dram_tensor("dbg_pe", [NT, 128, D], BF16, kind="ExternalOutput")
        dbg["z15"] = nc.dram_tensor("dbg_z15", [128, D], F32, kind="ExternalOutput")
        dbg["yl15"] = nc.dram_tensor("dbg_yl15", [128, D], F32, kind="ExternalOutput")
        dbg["h28"] = nc.dram_tensor("dbg_h28", [128, 512], BF16, kind="ExternalOutput")
        dbg["y1T0"] = nc.dram_tensor("dbg_y1T0", [128, S], BF16, kind="ExternalOutput")
        dbg["y1bf1"] = nc.dram_tensor("dbg_y1bf1", [128, D], BF16, kind="ExternalOutput")
        dbg["y1T0b"] = nc.dram_tensor("dbg_y1T0b", [128, 128], BF16, kind="ExternalOutput")

    def sb(name, shape, dtype, off):
        return nc.alloc_sbuf_tensor_at(name, shape, dtype, offset=off)

    BASE = 16512
    OFF_SMALL = BASE + 0
    OFF_H = BASE + 2048
    OFF_XPT = BASE + 15360
    OFF_W1R = BASE + 48128
    OFF_W2R = BASE + 113664
    OFF_HR = BASE + 179200

    ones64 = sb("ones64", [1, 64], BF16, OFF_SMALL + 0)
    expbias = sb("expbias", [128, 1], F32, OFF_SMALL + 128)
    zerocol = sb("zerocol", [128, 1], F32, OFF_SMALL + 160)
    eps5col = sb("eps5col", [128, 1], F32, OFF_SMALL + 192)
    bqcol = sb("bqcol", [128, 8], F32, OFF_SMALL + 224)
    bkcol = sb("bkcol", [128, 8], F32, OFF_SMALL + 256)
    b1col = sb("b1col", [128, 32], F32, OFF_SMALL + 288)
    poscol = sb("poscol", [128, NT], F32, OFF_SMALL + 416)
    sccol = sb("sccol", [128, 4], F32, OFF_SMALL + 480)
    rbysb = sb("rbysb", [1, 4], F32, OFF_SMALL + 512)
    scal = sb("scal", [1, 8], F32, OFF_SMALL + 544)
    statc = sb("statc", [128, 24], F32, OFF_SMALL + 576)
    eyesb = sb("eyesb", [128, 128], BF16, OFF_SMALL + 672)

    x0t = [sb(f"x0t{i}", [128, D], F32, OFF_H + i * 4096) for i in range(2)]
    xpbf = [sb(f"xpbf{i}", [128, D], BF16, OFF_H + 8192 + i * 2048) for i in range(2)]
    bv_bc = sb("bv_bc", [128, D], F32, OFF_H + 0)
    gamma_bc = sb("gamma_bc", [128, D], F32, OFF_H + 8192)
    out_st = [sb(f"out_st{i}", [128, D], F32, OFF_H + i * 4096) for i in range(2)]
    y1b2t = sb("y1b2t", [128, D], F32, OFF_H + 8192)
    lnrbuf = sb("lnrbuf", [1, 1024], F32, OFF_H + 4096)
    rinvbf = sb("rinvbf", [1, 1024], BF16, OFF_H + 11264)

    XPT = [sb(f"XPT{i}", [128, S], BF16, OFF_XPT + i * 4096) for i in range(8)]
    Y1T = [sb(f"Y1T{i}", [128, S], BF16, OFF_XPT + i * 4096) for i in range(8)]

    W1R = OFF_W1R
    Vt = [sb(f"Vt{i}", [128, 1040], BF16, W1R + i * 2080) for i in range(NT)]
    KTzp = [sb(f"KTzp{i}", [128, 2 * S], BF16, W1R + 33280 + i * 8192) for i in range(2)]
    QTp = [sb(f"QTp{i}", [128, S], BF16, W1R + 49664 + i * 4096) for i in range(2)]
    Ebuf = [sb(f"Ebuf{i}", [128, 1024], BF16, W1R + 57856 + i * 2048) for i in range(2)]
    Ebuf.append(sb("Ebuf2", [128, 1024], BF16, OFF_H + 0))
    OTBst = sb("OTBst", [64, 512], BF16, W1R + 61952)
    Rinv = sb("Rinv", [64, 512], F32, W1R + 62976)
    w1sb = [sb(f"w1sb{i}", [128, DFF], BF16, W1R + i * 8192) for i in range(8)]
    betab = sb("betab", [128, D], F32, W1R + 49152)
    b2b = sb("b2b", [128, D], F32, W1R + 53248)
    invd_bc = sb("invd_bc", [128, D], F32, W1R + 0)
    phase_bc = sb("phase_bc", [128, D], F32, W1R + 4096)
    ang = sb("ang", [128, D], F32, W1R + 8192)
    angk = sb("angk", [128, D], F32, W1R + 12288)
    angki = sb("angki", [128, D], I32, W1R + 16384)
    angr = sb("angr", [128, D], F32, W1R + 20480)
    pet = sb("pet", [128, D], BF16, W1R + 24576)
    pebt0 = sb("pebt0", [128, D], BF16, W1R + 26624)
    bo_bc = sb("bo_bc", [128, D], F32, W1R + 28672)

    W2R = OFF_W2R
    wvTsb = [sb(f"wvTsb{i}", [128, D], BF16, W2R + i * 2048) for i in range(8)]
    wqTsb = [sb(f"wqTsb{i}", [128, D], BF16, W2R + i * 2048) for i in range(8)]
    wkTsb = [sb(f"wkTsb{i}", [128, D], BF16, W2R + 16384 + i * 2048) for i in range(8)]
    xpe = [sb(f"xpe{i}", [128, D], F32, W2R + i * 4096) for i in range(2)]
    xpb = [sb(f"xpb{i}", [128, D], F32, W2R + 8192 + i * 4096) for i in range(2)]
    vnbuf2 = sb("vnbuf2", [128, D], F32, W2R + 16384)
    x0t2 = sb("x0t2", [128, D], F32, W2R + 20480)
    x0t3 = sb("x0t3", [128, D], F32, W2R + 24576)
    pebt2 = sb("pebt2", [128, D], BF16, W2R + 28672)
    pebt3 = sb("pebt3", [128, D], BF16, W2R + 30720)
    OT = [sb(f"OT{i}", [128, S], BF16, W2R + 32768 + i * 4096) for i in range(8)]
    w2sb = [sb(f"w2sb{i}", [128, D], BF16, W2R + i * 2048) for i in range(32)]

    HR = OFF_HR
    woTsb = [sb(f"woTsb{i}", [128, D], BF16, HR + i * 2048) for i in range(8)]
    pebt = [sb(f"pebt{i}", [128, D], BF16, HR + 16384 + i * 2048) for i in range(2)]
    ubuf = sb("ubuf", [128, D], F32, HR + 20480)
    ubuf2 = sb("ubuf2", [128, D], F32, W1R + 57856)
    vnbuf = sb("vnbuf", [128, D], F32, HR + 24576)
    y1bf = [sb(f"y1bf{i}", [128, D], BF16, HR + 28672 + i * 2048) for i in range(2)]
    hsb = [sb(f"hsb{i}", [128, 512], BF16, HR + i * 1024) for i in range(32)]

    PS = [nc.alloc_psum_tensor(f"PS{i}", [128, 1024], F32) for i in range(4)]

    P = Prog()
    sem = {}
    for name in P.COMPUTE_LANES + P.DMA_LANES:
        sem[name] = nc.semaphore(f"sem_{name}", None).__enter__()

    SY, SC, GP = "sync", "scalar", "gpsimd"

    def p0():
        P.dma(GP, "cst", lambda g: g.dma_start(rbysb[:, 0:3], rby_d.ap()))
        P.dma(GP, "cst", lambda g: g.dma_start(invd_bc[:, :], invd_d.ap().partition_broadcast(128)))
        P.dma(GP, "cst", lambda g: g.dma_start(phase_bc[:, :], phase_d.ap().partition_broadcast(128)))
        P.dma(GP, "cst", lambda g: g.dma_start(poscol[:, :], pos_d.ap()))
        P.dma(GP, "cst", lambda g: g.dma_start(bqcol[:, :], bqc_d.ap()))
        P.dma(GP, "cst", lambda g: g.dma_start(bkcol[:, :], bkc_d.ap()))
        P.dma(GP, "cst", lambda g: g.dma_start(b1col[:, :], b1c_d.ap()))
        P.dma(GP, "cst", lambda g: g.dma_start(bo_bc[:, :], bo_d.ap().partition_broadcast(128)))
        P.dma(GP, "cst", lambda g: g.dma_start(eyesb[:, :], eye_d.ap()), (), "csts")

        P.gp(lambda g: g.memset(ones64[:, :], 1.0), (), "ms.ones")
        P.gp(lambda g: g.memset(expbias[:, :], EXP_SHIFT), (), "ms.eb")
        P.gp(lambda g: g.memset(zerocol[:, :], 0.0), (), "ms.z")
        P.gp(lambda g: g.memset(eps5col[:, :], 1e-5), (), "ms.e5")

        r_, b_, y_ = rbysb[0:1, 0:1], rbysb[0:1, 1:2], rbysb[0:1, 2:3]

        def ts1(out, in_, s1, op0, s2=None, op1=None):
            if s2 is None:
                return lambda v: v.tensor_scalar(out, in_, s1, None, op0)
            return lambda v: v.tensor_scalar(out, in_, s1, s2, op0, op1)

        w = ["csts"]
        P.dve(ts1(scal[0:1, 3:4], r_, 0.0125, ALU.mult, 0.125, ALU.add), w, "sc.a")
        P.dve(ts1(scal[0:1, 4:5], b_, 0.00625, ALU.mult), w, "sc.b")
        P.dve(ts1(scal[0:1, 5:6], y_, 0.01875, ALU.mult), w, "sc.c")
        P.dve(lambda v: v.tensor_add(scal[0:1, 6:7], scal[0:1, 3:4], scal[0:1, 4:5]), ["sc.a", "sc.b"], "sc.d")
        P.dve(lambda v: v.tensor_add(scal[0:1, 0:1], scal[0:1, 6:7], scal[0:1, 5:6]), ["sc.c", "sc.d"], "sc.sp")
        P.dve(ts1(scal[0:1, 1:2], b_, 1e-6, ALU.mult, 1e-5, ALU.add), w, "sc.eps")
        P.dve(ts1(scal[0:1, 7:8], y_, 0.1, ALU.mult, 1.0, ALU.add), w, "sc.cy0")
        P.dve(lambda v: v.reciprocal(scal[0:1, 2:3], scal[0:1, 7:8]), ["sc.cy0"], "sc.cy")
        P.dve(ts1(scal[0:1, 4:5], r_, 0.1, ALU.mult, 1.0, ALU.add), ["sc.d"], "sc.g1")
        P.dve(ts1(scal[0:1, 5:6], y_, 0.05, ALU.mult), ["sc.sp"], "sc.g2")
        P.dve(lambda v: v.tensor_add(scal[0:1, 3:4], scal[0:1, 4:5], scal[0:1, 5:6]), ["sc.g1", "sc.g2"], "sc.gsc")
        P.dma(SY, "sc", lambda s: s.dma_start(scl_hbm.ap(), scal[0:1, 0:4]),
              ["sc.sp", "sc.eps", "sc.cy", "sc.gsc"], "sc.st")
        P.dma(SY, "sc", lambda s: s.dma_start(sccol[:, :], scl_hbm.ap().partition_broadcast(128)),
              ["sc.st"], "sc.bc")

        for t in range(NT):
            wd = ["csts", "sc.bc"]
            if t >= 1:
                wd.append(f"pe.st1.{t-1}")
            P.dve(lambda v, t=t: v.tensor_scalar(
                ang[:, :], invd_bc[:, :], sccol[:, 2:3], poscol[:, t:t + 1],
                ALU.mult, ALU.mult), wd, f"pe.ang.{t}")
            P.dve(lambda v: v.tensor_add(ang[:, :], ang[:, :], phase_bc[:, :]),
                  [f"pe.ang.{t}"], f"pe.ph.{t}")
            P.dve(lambda v: v.tensor_scalar(angk[:, :], ang[:, :], INV2PI, None,
                                            ALU.mult), [f"pe.ph.{t}"], f"pe.k0.{t}")
            P.dve(lambda v: v.tensor_copy(angki[:, :], angk[:, :]), [f"pe.k0.{t}"], f"pe.ki.{t}")
            P.dve(lambda v: v.tensor_copy(angk[:, :], angki[:, :]), [f"pe.ki.{t}"], f"pe.kf.{t}")
            P.dve(lambda v: v.scalar_tensor_tensor(
                angr[:, :], angk[:, :], -CW1, ang[:, :], ALU.mult, ALU.add),
                [f"pe.kf.{t}"], f"pe.cw1.{t}")
            P.dve(lambda v: v.scalar_tensor_tensor(
                ang[:, :], angk[:, :], -CW2, angr[:, :], ALU.mult, ALU.add),
                [f"pe.cw1.{t}"], f"pe.cw2.{t}")
            P.dve(lambda v: v.scalar_tensor_tensor(
                angr[:, :], angk[:, :], -CW3, ang[:, :], ALU.mult, ALU.add),
                [f"pe.cw2.{t}"], f"pe.cw.{t}")
            P.act(lambda sc_: sc_.activation(pet[:, :], angr[:, :], AF.Sin),
                  [f"pe.cw.{t}"], f"pe.sin.{t}")
            P.dve(lambda v: v.tensor_add(pebt0[:, :], pet[:, :], bo_bc[:, :]),
                  [f"pe.sin.{t}"], f"pe.peb.{t}")
            P.dma(SY, "pest", lambda s, t=t: s.dma_start(pe_b.ap()[t], pet[:, :]),
                  [f"pe.sin.{t}"], f"pe.st0.{t}")
            P.dma(SY, "pest", lambda s, t=t: s.dma_start(peb_b.ap()[t], pebt0[:, :]),
                  [f"pe.peb.{t}"], f"pe.st1.{t}")
            if debug_outputs:
                P.dma(SY, "dbg", lambda s, t=t: s.dma_start(dbg["pe"].ap()[t], pet[:, :]),
                      [f"pe.sin.{t}"], f"pe.dbg.{t}")
        P.marks["pes.done"] = ("pest", P.cnt["pest"])


    def p1(s):
        base = s * S
        if s > 0:
            for e in P.ENGINES:
                P.wait_only(e, P.gate(f"s{s-1}.end"))
        for t in range(NT):
            i = t % 2
            xl = "x0a" if i == 0 else "x0b"
            pl = "pea" if i == 0 else "peb"
            wts = []
            if t >= 2:
                wts.append(f"s{s}.p1.cast.{t-2}")
            P.dma(SY, xl, lambda sy, t=t, i=i: sy.dma_start(
                x0t[i][:, :], x0_d.ap()[base + t * 128:base + (t + 1) * 128, :]),
                wts, f"s{s}.p1.x0.{t}")
            wts = [f"s{s}.p1.x0.{t}", f"pe.st0.{t}"]
            if t >= 2:
                wts.append(f"s{s}.p1.tr.{t-2}.7")
            P.dma(GP, pl, lambda g, t=t, i=i: g.dma_start(xpbf[i][:, :], pe_b.ap()[t]),
                  wts, f"s{s}.p1.pe.{t}")
            P.dve(lambda v, i=i: v.tensor_tensor(xpbf[i][:, :], x0t[i][:, :],
                                                 xpbf[i][:, :], ALU.add),
                  [f"s{s}.p1.pe.{t}", f"s{s}.p1.x0.{t}"], f"s{s}.p1.cast.{t}")
            for ic in range(8):
                bank = PS[2][:, (ic % 2) * 512:(ic % 2) * 512 + 64].bitcast(BF16)
                wts = [f"s{s}.p1.cast.{t}"]
                if ic == 0 and t >= 1:
                    wts.append(f"s{s}.p1.ev.{t-1}.7")
                if ic >= 2:
                    wts.append(f"s{s}.p1.ev.{t}.{ic-2}")
                P.mm(lambda tns, i=i, ic=ic, bank=bank: tns.transpose(
                    bank, xpbf[i][:, ic * 128:(ic + 1) * 128], eyesb[:, :]),
                    wts, f"s{s}.p1.tr.{t}.{ic}")
                P.dve(lambda v, t=t, ic=ic, bank=bank: v.tensor_copy(
                    XPT[ic][:, t * 128:(t + 1) * 128], bank),
                    [f"s{s}.p1.tr.{t}.{ic}"], f"s{s}.p1.ev.{t}.{ic}")
        if debug_outputs and s == 0:
            for ic in range(8):
                P.dma(SY, "dbg", lambda sy, ic=ic: sy.dma_start(
                    dbg["xpT"].ap()[ic], XPT[ic][:, :]),
                    [f"s{s}.p1.ev.{NT-1}.{ic}"], f"dbg.xpT.{ic}")

    def proj_v(s):
        P.dma(GP, "bcv", lambda g: g.dma_start(bv_bc[:, :], bv_d.ap().partition_broadcast(128)),
              [f"s{s}.p1.cast.{NT-1}", f"s{s}.p1.cast.{NT-2}"], f"s{s}.bv")
        for i in range(8):
            P.dma(GP, "wv", lambda g, i=i: g.dma_start(
                wvTsb[i][:, :], wvT_d.ap()[i * 128:(i + 1) * 128, :]),
                (), f"s{s}.wvT.{i}")
        for t in range(NT):
            wts = [f"pe.peb.{NT-1}", f"pe.sin.{NT-1}", "pes.done"] if s == 0 else []
            P.gp(lambda g, t=t: g.memset(
                Vt[t][:, :].rearrange("p (h c) -> p h c", c=65)[:, :, 64:65], 1.0),
                wts, f"s{s}.vones.{t}")
        for t in range(NT):
            wts = [f"s{s}.p1.ev.{t}.{ic2}" for ic2 in range(8)]
            if t == 0:
                wts += [f"s{s}.wvT.7", f"s{s}.vones.{NT-1}"]
            if t >= 1:
                wts.append(f"s{s}.vev.{t-1}.1")
            for ic in range(8):
                for half in range(2):
                    P.mm(lambda tns, t=t, ic=ic, half=half: tns.matmul(
                        PS[3][:, half * 512:(half + 1) * 512],
                        XPT[ic][:, t * 128:(t + 1) * 128],
                        wvTsb[ic][:, half * 512:(half + 1) * 512],
                        start=(ic == 0), stop=(ic == 7)),
                        wts if (ic == 0 and half == 0) else (),
                        f"s{s}.vmm.{t}.{half}" if ic == 7 else None)
            for half in range(2):
                h0 = half * 8
                P.dve(lambda v, t=t, half=half, h0=h0: v.tensor_tensor(
                    Vt[t][:, :].rearrange("p (h c) -> p h c", c=65)[:, h0:h0 + 8, 0:64],
                    PS[3][:, half * 512:(half + 1) * 512].rearrange("p (h c) -> p h c", c=64),
                    bv_bc[:, half * 512:(half + 1) * 512].rearrange("p (h c) -> p h c", c=64),
                    ALU.add),
                    [f"s{s}.vmm.{t}.{half}", f"s{s}.bv"], f"s{s}.vev.{t}.{half}")
        if debug_outputs and s == 0:
            P.dma(SY, "dbg", lambda sy: sy.dma_start(dbg["v0"].ap(), Vt[0][:, :]),
                  [f"s{s}.vev.0.0", f"s{s}.vev.0.1"], "dbg.v0")

    def proj_qk_steps(s, c, last_proj):
        i = c % 2
        steps = []

        def zero_step():
            wz = []
            if c >= 2:
                wz.append(f"s{s}.attdone.{c-2}")
            P.dve(lambda v, i=i: v.memset(KTzp[i][64:128, 0:S], 0.0), wz, f"s{s}.kz0.{c}")
            P.dve(lambda v, i=i: v.memset(KTzp[i][0:64, S:2 * S], 0.0), wz, f"s{s}.kz1.{c}")

        def mk_mm(kind, tsl, ic):
            wsb = wqTsb if kind == "q" else wkTsb
            def em():
                wts = []
                if ic == 0:
                    wts.append(f"s{s}.wqT.7" if kind == "q" else f"s{s}.wkT.7")
                    if c >= 2:
                        wts.append(f"s{s}.attdone.{c-2}")
                    if last_proj[0] is not None:
                        wts.append(last_proj[0])
                    if last_proj[1] is not None:
                        wts.append(last_proj[1])
                for h2 in range(2):
                    P.mm(lambda tns, tsl=tsl, ic=ic, h2=h2, wsb=wsb: tns.matmul(
                        PS[3][:, h2 * 512:(h2 + 1) * 512],
                        wsb[ic][:, c * 128:(c + 1) * 128],
                        XPT[ic][:, (tsl + h2) * 512:(tsl + h2 + 1) * 512],
                        start=(ic == 0), stop=(ic == 7)),
                        wts if h2 == 0 else (),
                        f"s{s}.{kind}mm.{c}.{tsl + h2}" if ic == 7 else None)
                if ic == 7:
                    if kind == "q":
                        for h2 in range(2):
                            P.dve(lambda v, tsl=tsl, h2=h2: v.tensor_scalar(
                                QTp[i][:, (tsl + h2) * 512:(tsl + h2 + 1) * 512],
                                PS[3][:, h2 * 512:(h2 + 1) * 512],
                                bqcol[:, c:c + 1], sccol[:, 0:1], ALU.add, ALU.mult),
                                [f"s{s}.qmm.{c}.{tsl + h2}", "sc.bc"],
                                f"s{s}.qev.{c}.{tsl + h2}")
                        last_proj[0] = f"s{s}.qev.{c}.{tsl + 1}"
                    else:
                        for h2 in range(2):
                            P.dve(lambda v, tsl=tsl, h2=h2: v.tensor_scalar(
                                KTzp[i][0:64, (tsl + h2) * 512:(tsl + h2 + 1) * 512],
                                PS[3][0:64, h2 * 512:(h2 + 1) * 512],
                                bkcol[0:64, c:c + 1], None, ALU.add),
                                [f"s{s}.kmm.{c}.{tsl + h2}", f"s{s}.kz0.{c}",
                                 f"s{s}.kz1.{c}"], f"s{s}.kevA.{c}.{tsl + h2}")
                            P.dve(lambda v, tsl=tsl, h2=h2: v.tensor_scalar(
                                KTzp[i][64:128, S + (tsl + h2) * 512:S + (tsl + h2 + 1) * 512],
                                PS[3][64:128, h2 * 512:(h2 + 1) * 512],
                                bkcol[64:128, c:c + 1], None, ALU.add),
                                [f"s{s}.kevA.{c}.{tsl + h2}"], f"s{s}.kevB.{c}.{tsl + h2}")
                        last_proj[0] = f"s{s}.kevB.{c}.{tsl + 1}"
            return em

        first = True
        for kind, tsl in (("q", 0), ("q", 2), ("k", 0), ("k", 2)):
            for ic in range(8):
                em = mk_mm(kind, tsl, ic)
                if first:
                    def em0(em=em):
                        zero_step()
                        em()
                    steps.append(em0)
                    first = False
                else:
                    steps.append(em)
        return steps

    def attention_pair(s, c, interleave_steps, last_proj):
        i = c % 2

        def eb(g):
            return Ebuf[g % 3]

        def emit_sa_sb(q, kt):
            sp = kt % 2
            wts = []
            if kt == 0 and q == 0:
                wts += [f"s{s}.qev.{c}.{t}" for t in range(4)]
                wts += [f"s{s}.kevB.{c}.{t}" for t in range(4)]
            if kt >= 2:
                wts.append(f"s{s}.exp.{c}.{q}.{kt-2}")
            elif q > 0:
                wts.append(f"s{s}.exp.{c}.{q-1}.{NKT-2+kt}")
            elif c > 0:
                wts.append(f"s{s}.exp.{c-1}.{NQ-1}.{NKT-2+kt}")
            P.mm(lambda tns, i=i, q=q, kt=kt, sp=sp: tns.matmul(
                PS[sp][:, 0:512],
                KTzp[i][:, kt * 128:(kt + 1) * 128],
                QTp[i][:, q * 512:(q + 1) * 512],
                start=True, stop=True), wts, f"s{s}.sa.{c}.{q}.{kt}")
            P.mm(lambda tns, i=i, q=q, kt=kt, sp=sp: tns.matmul(
                PS[sp][:, 512:1024],
                KTzp[i][:, S + kt * 128:S + (kt + 1) * 128],
                QTp[i][:, q * 512:(q + 1) * 512],
                start=True, stop=True), (), f"s{s}.sb.{c}.{q}.{kt}")

        def emit_exp(q, kt):
            sp = kt % 2
            g = q * NKT + kt
            wts = [f"s{s}.sb.{c}.{q}.{kt}", "ms.eb"]
            gp = g - 3
            if gp >= 0:
                wts.append(f"s{s}.pvB.{c}.{gp // NKT}.{gp % NKT}")
            elif c > 0:
                gq = NPAIR_G + gp
                wts.append(f"s{s}.pvB.{c-1}.{gq // NKT}.{gq % NKT}")
            if c == 0 and g == 2:
                wts += [f"s{s}.vev.{NT-1}.1"]
            P.act(lambda sc_, sp=sp, g=g: sc_.activation(
                eb(g)[:, :], PS[sp][:, :], AF.Exp, bias=expbias[:, :]),
                wts, f"s{s}.exp.{c}.{q}.{kt}")

        def emit_pv(q, kt, extra=()):
            g = q * NKT + kt
            wts = [f"s{s}.exp.{c}.{q}.{kt}"] + list(extra)
            if c == 0 and q == 0:
                wts += [f"s{s}.vev.{kt}.0", f"s{s}.vev.{kt}.1"]
            P.mm(lambda tns, c=c, kt=kt, g=g: tns.matmul(
                PS[2][0:65, 0:512], Vt[kt][:, 130 * c:130 * c + 65],
                eb(g)[:, 0:512],
                start=(kt == 0), stop=(kt == NKT - 1)), wts,
                f"s{s}.pvA.{c}.{q}.{kt}")
            P.mm(lambda tns, c=c, kt=kt, g=g: tns.matmul(
                PS[2][0:65, 512:1024], Vt[kt][:, 130 * c + 65:130 * c + 130],
                eb(g)[:, 512:1024],
                start=(kt == 0), stop=(kt == NKT - 1)), (),
                f"s{s}.pvB.{c}.{q}.{kt}")

        def tail_act(q):
            wts = [f"s{s}.pvA.{c}.{q}.{NKT-1}", f"s{s}.pvB.{c}.{q}.{NKT-1}"]
            P.act(lambda sc_: sc_.activation(lnrbuf[:, :], PS[2][64:65, 0:1024],
                                             AF.Ln), wts, f"s{s}.lnr.{c}.{q}")
            wts = [f"s{s}.lnr.{c}.{q}"]
            if P.has(f"s{s}.Rmm.{c}.{q-1}.B"):
                wts.append(f"s{s}.Rmm.{c}.{q-1}.B")
            if c > 0 and q == 0:
                wts.append(f"s{s}.Rmm.{c-1}.{NQ-1}.B")
            P.act(lambda sc_: sc_.activation(rinvbf[:, :], lnrbuf[:, :], AF.Exp,
                                             scale=-1.0), wts, f"s{s}.rinv.{c}.{q}")

        def tail_pe(q):
            wts = [f"s{s}.rinv.{c}.{q}", "ms.ones"]
            if P.has(f"s{s}.rcpB.{c}.{q-1}"):
                wts.append(f"s{s}.rcpB.{c}.{q-1}")
            if c > 0 and q == 0:
                wts.append(f"s{s}.rcpB.{c-1}.{NQ-1}")
            if last_proj[0] is not None:
                wts.append(last_proj[0])
            P.mm(lambda tns: tns.matmul(PS[3][0:64, 512:1024], ones64[:, :],
                                        rinvbf[0:1, 0:512], start=True, stop=True),
                 wts, f"s{s}.Rmm.{c}.{q}.A")
            P.dve(lambda v: v.tensor_copy(Rinv[:, :], PS[3][0:64, 512:1024]),
                  [f"s{s}.Rmm.{c}.{q}.A"], f"s{s}.rcpA.{c}.{q}")
            P.dve(lambda v, c=c, q=q: v.tensor_tensor(
                OT[c][0:64, q * 512:(q + 1) * 512], PS[2][0:64, 0:512],
                Rinv[:, :], ALU.mult), [f"s{s}.rcpA.{c}.{q}"], f"s{s}.otA.{c}.{q}")
            P.mm(lambda tns: tns.matmul(PS[3][0:64, 512:1024], ones64[:, :],
                                        rinvbf[0:1, 512:1024], start=True, stop=True),
                 [f"s{s}.rcpA.{c}.{q}"], f"s{s}.Rmm.{c}.{q}.B")
            P.dve(lambda v: v.tensor_copy(Rinv[:, :], PS[3][0:64, 512:1024]),
                  [f"s{s}.Rmm.{c}.{q}.B", f"s{s}.otA.{c}.{q}"], f"s{s}.rcpB.{c}.{q}")
            wts = [f"s{s}.rcpB.{c}.{q}"]
            if P.has(f"s{s}.otbdma.{c}.{q-1}"):
                wts.append(f"s{s}.otbdma.{c}.{q-1}")
            elif c > 0:
                wts.append(f"s{s}.otbdma.{c-1}.{NQ-1}")
            P.dve(lambda v: v.tensor_tensor(OTBst[:, :], PS[2][0:64, 512:1024],
                                            Rinv[:, :], ALU.mult), wts, f"s{s}.otB.{c}.{q}")
            P.dma(SY, "otb", lambda sy, c=c, q=q: sy.dma_start(
                OT[c][64:128, q * 512:(q + 1) * 512], OTBst[:, :]),
                [f"s{s}.otB.{c}.{q}"], f"s{s}.otbdma.{c}.{q}")
            last_proj[1] = f"s{s}.rcpB.{c}.{q}"

        def pop():
            if interleave_steps:
                interleave_steps.pop(0)()

        for q in range(NQ):
            for kt in range(NKT):
                if kt < 3:
                    emit_sa_sb(q, kt)
                    emit_exp(q, kt)
                    if kt == 1 and q > 0:
                        tail_pe(q - 1)
                elif kt == 3:
                    emit_sa_sb(q, kt)
                    extra = []
                    if q > 0:
                        extra = [f"s{s}.otA.{c}.{q-1}", f"s{s}.otB.{c}.{q-1}"]
                    elif c > 0:
                        extra = [f"s{s}.otA.{c-1}.{NQ-1}", f"s{s}.otB.{c-1}.{NQ-1}"]
                    emit_pv(q, 0, extra=extra)
                    emit_pv(q, 1)
                    emit_pv(q, 2)
                    emit_exp(q, kt)
                else:
                    emit_sa_sb(q, kt)
                    emit_pv(q, kt - 1)
                    emit_exp(q, kt)
                    if 4 <= kt <= 11:
                        pop()
            emit_pv(q, NKT - 1)
            tail_act(q)
        tail_pe(NQ - 1)
        while interleave_steps:
            interleave_steps.pop(0)()
        P.marks[f"s{s}.attdone.{c}"] = ("dve", P.cnt["dve"])

    def p23(s):
        proj_v(s)
        for ic in range(8):
            P.dma(GP, "wq", lambda g, ic=ic: g.dma_start(
                wqTsb[ic][:, :], wqT_d.ap()[ic * 128:(ic + 1) * 128, :]),
                [f"s{s}.vmm.{NT-1}.1"] if ic == 0 else (), f"s{s}.wqT.{ic}")
        for ic in range(8):
            P.dma(GP, "wk", lambda g, ic=ic: g.dma_start(
                wkTsb[ic][:, :], wkT_d.ap()[ic * 128:(ic + 1) * 128, :]),
                (), f"s{s}.wkT.{ic}")
        for ic in range(8):
            P.dma(GP, "wo", lambda g, ic=ic: g.dma_start(
                woTsb[ic][:, :], woT_d.ap()[ic * 128:(ic + 1) * 128, :]),
                (), f"s{s}.woT.{ic}")
        last_proj = [None, None]
        for st in proj_qk_steps(s, 0, last_proj):
            st()
        for c in range(NPAIR):
            steps = proj_qk_steps(s, c + 1, last_proj) if c + 1 < NPAIR else []
            attention_pair(s, c, steps, last_proj)
        P.barrier_tagged(f"s{s}.attend")
        if debug_outputs and s == 0:
            for ic in range(8):
                P.dma(SY, "dbg", lambda sy, ic=ic: sy.dma_start(
                    dbg["ot"].ap()[ic], OT[ic][:, :]),
                    [f"s{s}.otA.{ic}.{NQ-1}", f"s{s}.otbdma.{ic}.{NQ-1}"],
                    f"dbg.ot.{ic}")

    def p4(s):
        base = s * S
        gate = [f"s{s}.attend.mm", f"s{s}.attend.dve", f"s{s}.attend.act",
                f"s{s}.attend.otb"]
        for e in P.ENGINES:
            P.wait_only(e, gate)
        P.dma(GP, "bcg", lambda g: g.dma_start(gamma_bc[:, :], gamma_d.ap().partition_broadcast(128)),
              (), f"s{s}.gbc0")
        P.dve(lambda v: v.tensor_scalar(gamma_bc[:, :], gamma_bc[:, :], sccol[:, 3:4],
                                        None, ALU.mult), [f"s{s}.gbc0", "sc.bc"], f"s{s}.gbc")
        P.dma(GP, "bcbt", lambda g: g.dma_start(betab[:, :], beta_d.ap().partition_broadcast(128)),
              (), f"s{s}.bbc")
        P.dma(GP, "bcb2", lambda g: g.dma_start(b2b[:, :], b2_d.ap().partition_broadcast(128)),
              (), f"s{s}.b2bc")

        def ln_stats(pref, srcb, wtag, so):
            P.act(lambda sc_, srcb=srcb, so=so: sc_.activation(
                PS[1][:, :], srcb[:, :], AF.Copy, accum_out=statc[:, so:so + 1]),
                [wtag], f"{pref}.s1")
            P.act(lambda sc_, srcb=srcb, so=so: sc_.activation(
                PS[1][:, :], srcb[:, :], AF.Square, accum_out=statc[:, so + 2:so + 3]),
                [wtag], f"{pref}.s2")
            P.dve(lambda v, so=so: v.tensor_scalar(
                statc[:, so + 5:so + 6], statc[:, so:so + 1], 1.0 / D, None, ALU.mult),
                [f"{pref}.s1"], f"{pref}.m")
            P.dve(lambda v, so=so: v.tensor_tensor(
                statc[:, so + 1:so + 2], statc[:, so:so + 1], statc[:, so:so + 1],
                ALU.mult), [f"{pref}.s1"], f"{pref}.t1")
            P.dve(lambda v, so=so: v.scalar_tensor_tensor(
                statc[:, so + 7:so + 8], statc[:, so + 1:so + 2], -1.0 / D,
                statc[:, so + 2:so + 3], ALU.mult, ALU.add),
                [f"{pref}.t1", f"{pref}.s2"], f"{pref}.var")

        UB = [ubuf, ubuf2]
        X04 = [x0t[0], x0t[1], x0t2, x0t3]
        PB4 = [pebt[0], pebt[1], pebt2, pebt3]
        XL4 = ["x0a", "x0b", "x0c", "x0d"]
        PL4 = ["pea", "peb", "pec", "ped"]

        def load_x0pe(t, xw, pw):
            j = t % 4
            P.dma(SY, XL4[j], lambda sy, t=t, j=j: sy.dma_start(
                X04[j][:, :], x0_d.ap()[base + t * 128:base + (t + 1) * 128, :]),
                xw, f"s{s}.p4.x0.{t}")
            P.dma(SC, PL4[j], lambda sc_, t=t, j=j: sc_.dma_start(
                PB4[j][:, :], peb_b.ap()[t]),
                pw + [f"pe.st1.{t}"], f"s{s}.p4.pe.{t}")

        for k in range(4):
            load_x0pe(k, [], [])
        for ic in (0, 1, 2, 3, 4, 5):
            P.dma(GP, "w1", lambda g, ic=ic: g.dma_start(
                w1sb[ic][:, :], w1_d.ap()[ic * 128:(ic + 1) * 128, :]),
                (), f"s{s}.w1.{ic}")

        def stage_a_mm(t):
            ps = PS[3] if t % 2 == 0 else PS[0]
            wts = [f"s{s}.woT.7"]
            if t >= 2:
                wts.append(f"s{s}.p4.u.{t-2}")
            for ic in range(8):
                for half in range(2):
                    P.mm(lambda tns, t=t, ic=ic, half=half, ps=ps: tns.matmul(
                        ps[:, half * 512:(half + 1) * 512],
                        OT[ic][:, t * 128:(t + 1) * 128],
                        woTsb[ic][:, half * 512:(half + 1) * 512],
                        start=(ic == 0), stop=(ic == 7)),
                        wts if (ic == 0 and half == 0) else (),
                        f"s{s}.p4.mm.{t}" if (ic == 7 and half == 1) else None)

        def stage_a(t):
            i = t % 2
            j = t % 4
            so = 12 * i
            ps = PS[3] if t % 2 == 0 else PS[0]
            ub = UB[i]
            wts = [f"s{s}.p4.x0.{t}", f"s{s}.p4.pe.{t}"]
            if t >= 2:
                wts.append(f"s{s}.p4.u.{t-2}")
            P.gp(lambda g, i=i, j=j: g.tensor_tensor(xpe[i][:, :], X04[j][:, :],
                                                     PB4[j][:, :], ALU.add),
                 wts, f"s{s}.p4.xpe.{t}")
            wts = [f"s{s}.p4.x0.{t}", f"s{s}.bbc"]
            if t >= 2:
                wts.append(f"s{s}.p4.w.{t-2}")
            P.gp(lambda g, i=i, j=j: g.tensor_tensor(xpb[i][:, :], X04[j][:, :],
                                                     betab[:, :], ALU.add),
                 wts, f"s{s}.p4.xpb.{t}")
            if t + 4 < NT:
                load_x0pe(t + 4, [f"s{s}.p4.xpe.{t}", f"s{s}.p4.xpb.{t}"],
                          [f"s{s}.p4.xpe.{t}"])
            P.dve(lambda v, ps=ps, i=i, ub=ub: v.tensor_tensor(
                ub[:, :], ps[:, :], xpe[i][:, :], ALU.add),
                  [f"s{s}.p4.mm.{t}", f"s{s}.p4.xpe.{t}"], f"s{s}.p4.u.{t}")
            ln_stats(f"s{s}.p4.a{t}", ub, f"s{s}.p4.u.{t}", so)
            P.act(lambda sc_, so=so: sc_.activation(statc[:, so + 9:so + 10],
                                                    statc[:, so + 7:so + 8], AF.Ln,
                                                    bias=sccol[:, 1:2], scale=1.0 / D),
                  [f"s{s}.p4.a{t}.var", "sc.bc"], f"s{s}.p4.ln.{t}")
            P.act(lambda sc_, so=so: sc_.activation(statc[:, so + 8:so + 9],
                                                    statc[:, so + 9:so + 10], AF.Exp,
                                                    scale=-0.5), [f"s{s}.p4.ln.{t}"],
                  f"s{s}.p4.rstd.{t}")

        VN = [vnbuf, vnbuf2]

        def stage_b(t):
            i = t % 2
            so = 12 * i
            ub = UB[i]
            vn = VN[i]
            wts = [f"s{s}.p4.rstd.{t}"]
            if t >= 2:
                wts.append(f"s{s}.p4.y1dma.{t-2}")
            P.dve(lambda v, ub=ub, so=so, vn=vn: v.tensor_scalar(
                vn[:, :], ub[:, :], statc[:, so + 5:so + 6],
                statc[:, so + 8:so + 9], ALU.subtract, ALU.mult),
                wts, f"s{s}.p4.n1.{t}")
            P.dve(lambda v, vn=vn: v.tensor_tensor(vn[:, :], vn[:, :], gamma_bc[:, :],
                                                   ALU.mult),
                  [f"s{s}.p4.n1.{t}", f"s{s}.gbc"], f"s{s}.p4.n2.{t}")
            P.dve(lambda v, i=i, ub=ub, vn=vn: v.tensor_tensor(
                ub[:, :], vn[:, :], xpb[i][:, :], ALU.add),
                  [f"s{s}.p4.n2.{t}", f"s{s}.p4.xpb.{t}"], f"s{s}.p4.w.{t}")
            ln_stats(f"s{s}.p4.b{t}", ub, f"s{s}.p4.w.{t}", so)
            P.act(lambda sc_, so=so: sc_.activation(statc[:, so + 9:so + 10],
                                                    statc[:, so + 7:so + 8], AF.Ln,
                                                    bias=eps5col[:, :], scale=1.0 / D),
                  [f"s{s}.p4.b{t}.var", "ms.e5"], f"s{s}.p4.lnb.{t}")
            P.act(lambda sc_, so=so: sc_.activation(statc[:, so + 8:so + 9],
                                                    statc[:, so + 9:so + 10], AF.Exp,
                                                    scale=-0.5), [f"s{s}.p4.lnb.{t}"],
                  f"s{s}.p4.rstdb.{t}")
            P.dve(lambda v, ub=ub, so=so, vn=vn: v.tensor_scalar(
                vn[:, :], ub[:, :], statc[:, so + 5:so + 6],
                statc[:, so + 8:so + 9], ALU.subtract, ALU.mult),
                [f"s{s}.p4.rstdb.{t}"], f"s{s}.p4.y1.{t}")
            wts = [f"s{s}.p4.y1.{t}"]
            if t >= 2:
                wts.append(f"s{s}.p4.tr.{t-2}.7")
            P.dve(lambda v, i=i, vn=vn: v.tensor_copy(y1bf[i][:, :], vn[:, :]), wts,
                  f"s{s}.p4.y1bf.{t}")
            P.gp(lambda g, vn=vn: g.tensor_tensor(vn[:, :], vn[:, :], b2b[:, :], ALU.add),
                 [f"s{s}.p4.y1bf.{t}", f"s{s}.b2bc"], f"s{s}.p4.y1b2.{t}")
            P.dma(SY, "y1st" if i == 0 else "y1s2",
                  lambda sy, t=t, vn=vn: sy.dma_start(y1_b.ap()[t], vn[:, :]),
                  [f"s{s}.p4.y1b2.{t}"], f"s{s}.p4.y1dma.{t}")
            if debug_outputs and s == 0:
                P.dma(SY, "dbg", lambda sy, t=t, vn=vn: sy.dma_start(dbg["y1"].ap()[t], vn[:, :]),
                      [f"s{s}.p4.y1b2.{t}"], f"dbg.y1.{t}")
            for ic in range(8):
                bank = PS[2][:, (ic % 2) * 512:(ic % 2) * 512 + 64].bitcast(BF16)
                wts = [f"s{s}.p4.y1bf.{t}"]
                if ic == 0 and t >= 1:
                    wts.append(f"s{s}.p4.ev.{t-1}.7")
                if ic >= 2:
                    wts.append(f"s{s}.p4.ev.{t}.{ic-2}")
                P.mm(lambda tns, i=i, ic=ic, bank=bank: tns.transpose(
                    bank, y1bf[i][:, ic * 128:(ic + 1) * 128], eyesb[:, :]),
                    wts, f"s{s}.p4.tr.{t}.{ic}")
                P.dve(lambda v, t=t, ic=ic, bank=bank: v.tensor_copy(
                    Y1T[ic][:, t * 128:(t + 1) * 128], bank),
                    [f"s{s}.p4.tr.{t}.{ic}"], f"s{s}.p4.ev.{t}.{ic}")

        stage_a_mm(0)
        stage_a_mm(1)
        stage_a(0)
        for t in range(1, NT):
            if t + 1 < NT:
                stage_a_mm(t + 1)
            stage_a(t)
            stage_b(t - 1)
        stage_b(NT - 1)
        P.marks[f"s{s}.y1all"] = ("y1st", P.cnt["y1st"])
        P.marks[f"s{s}.y1all2"] = ("y1s2", P.cnt["y1s2"])
        if debug_outputs and s == 0:
            P.dma(SY, "dbg", lambda sy: sy.dma_start(dbg["y1T0"].ap(), Y1T[0][:, :]),
                  [f"s{s}.p4.tr.{NT-1}.6", f"s{s}.p4.tr.{NT-1}.7"], "dbg.y1T0")
            P.dma(SY, "dbg", lambda sy: sy.dma_start(dbg["y1bf1"].ap(), y1bf[1][:, :]),
                  [f"s{s}.p4.tr.{NT-1}.6", f"s{s}.p4.tr.{NT-1}.7"], "dbg.y1bf1")

    def p5(s):
        base = s * S
        P.dma(GP, "w1", lambda g: g.dma_start(
            w1sb[6][:, :], w1_d.ap()[6 * 128:7 * 128, :]),
            [f"s{s}.p4.xpb.{NT-1}", f"s{s}.p4.y1b2.{NT-1}"], f"s{s}.w1.6")
        P.dma(GP, "w1", lambda g: g.dma_start(
            w1sb[7][:, :], w1_d.ap()[7 * 128:8 * 128, :]),
            [f"s{s}.p4.y1.{NT-1}"], f"s{s}.w1.7")
        w2gate = {0: [f"s{s}.p4.u.{NT-2}"], 2: [f"s{s}.p4.u.{NT-1}"],
                  4: [f"s{s}.p4.w.{NT-2}"], 6: [f"s{s}.p4.w.{NT-1}"],
                  8: [f"s{s}.p4.y1dma.{NT-1}"], 9: [f"s{s}.p4.y1dma.{NT-1}"],
                  10: [f"s{s}.p4.xpb.{NT-1}"], 11: [f"s{s}.p4.xpb.{NT-1}"],
                  12: [f"s{s}.p4.xpb.{NT-1}"], 13: [f"s{s}.p4.xpb.{NT-1}"],
                  14: [f"s{s}.p4.xpb.{NT-1}"], 15: [f"s{s}.p4.xpb.{NT-1}"],
                  16: [f"s{s}.p4.mm.{NT-1}"]}
        for i in list(range(16, 32)) + list(range(16)):
            P.dma(GP, "w2", lambda g, i=i: g.dma_start(
                w2sb[i][:, :], w2_d.ap()[i * 128:(i + 1) * 128, :]),
                w2gate.get(i, ()), f"s{s}.w2.{i}")
        for tq in range(NQ):
            for oc in range(32):
                pbank = oc % 2
                wts = []
                if tq > 0 or oc >= 2:
                    pt, po = (tq, oc - 2) if oc >= 2 else (tq - 1, 30 + oc)
                    wts.append(f"s{s}.f1ev.{pt}.{po}")
                if tq == 0 and oc == 0:
                    wts += [f"s{s}.p4.ev.{NT-1}.{ic2}" for ic2 in range(8)]
                    wts += [f"s{s}.p4.ev.{NT-2}.{ic2}" for ic2 in range(8)]
                for ic in range(8):
                    icw = list(wts) if ic == 0 else []
                    if tq == 0 and oc == 0:
                        icw.append(f"s{s}.w1.{ic}" if ic < 6 else f"s{s}.w1.7")
                    P.mm(lambda tns, tq=tq, oc=oc, ic=ic, pbank=pbank: tns.matmul(
                        PS[0][:, pbank * 512:(pbank + 1) * 512],
                        w1sb[ic][:, oc * 128:(oc + 1) * 128],
                        Y1T[ic][:, tq * 512:(tq + 1) * 512],
                        start=(ic == 0), stop=(ic == 7)),
                        icw,
                        f"s{s}.f1.{tq}.{oc}" if ic == 7 else None)
                if debug_outputs and s == 0 and tq == 0 and oc == 31:
                    P.dma(SY, "dbg", lambda sy: sy.dma_start(
                        dbg["y1T0b"].ap(), Y1T[0][:, 1920:2048]),
                        [f"s{s}.f1.{tq}.{oc}"], "dbg.y1T0b")
                wts = [f"s{s}.f1.{tq}.{oc}", "csts", "ms.z"]
                if tq == 0:
                    wts.append(f"s{s}.p4.y1dma.{NT-1}")
                if tq > 0:
                    wts.append(f"s{s}.f2.{tq * 4 - 1}")
                P.dve(lambda v, oc=oc, pbank=pbank: v.tensor_scalar(
                    hsb[oc][:, :], PS[0][:, pbank * 512:(pbank + 1) * 512],
                    b1col[:, oc:oc + 1], zerocol[:, 0:1], ALU.add, ALU.max),
                    wts, f"s{s}.f1ev.{tq}.{oc}")
            for tt in range(4):
                t = tq * 4 + tt
                ps2 = PS[3] if t % 2 == 0 else PS[2]
                wts = [f"s{s}.y1all", f"s{s}.y1all2"]
                if P.has(f"s{s}.p5.z.{t-1}"):
                    wts.append(f"s{s}.p5.z.{t-1}")
                if t == 0:
                    wts.append(f"s{s}.p4.n2.{NT-1}")
                P.dma(SC, "y1ld", lambda sc_, t=t: sc_.dma_start(y1b2t[:, :], y1_b.ap()[t]),
                      wts, f"s{s}.p5.y1.{t}")
                wts = [f"s{s}.f1ev.{tq}.31", f"s{s}.w2.31"]
                if P.has(f"s{s}.p5.z.{t-2}"):
                    wts.append(f"s{s}.p5.z.{t-2}")
                for wc in range(32):
                    for half in range(2):
                        P.mm(lambda tns, tt=tt, wc=wc, half=half, ps2=ps2: tns.matmul(
                            ps2[:, half * 512:(half + 1) * 512],
                            hsb[wc][:, tt * 128:(tt + 1) * 128],
                            w2sb[wc][:, half * 512:(half + 1) * 512],
                            start=(wc == 0), stop=(wc == 31)),
                            wts if (wc == 0 and half == 0) else (),
                            f"s{s}.f2.{t}" if (wc == 31 and half == 1) else None)
                P.dve(lambda v, ps2=ps2: v.tensor_tensor(out_st[1][:, :], ps2[:, :],
                                                         y1b2t[:, :], ALU.add),
                      [f"s{s}.f2.{t}", f"s{s}.p5.y1.{t}"], f"s{s}.p5.z.{t}")
                pref = f"s{s}.p5.c{t}"
                P.dve(lambda v: v.reduce_sum(statc[:, 0:1], out_st[1][:, :],
                                             mybir.AxisListType.X),
                      [f"s{s}.p5.z.{t}"], f"{pref}.s1")
                P.act(lambda sc_: sc_.activation(PS[1][:, :], out_st[1][:, :],
                                                 AF.Square, accum_out=statc[:, 2:3]),
                      [f"s{s}.p5.z.{t}"], f"{pref}.s2")
                P.dve(lambda v: v.tensor_scalar(statc[:, 5:6], statc[:, 0:1],
                                                1.0 / D, None, ALU.mult),
                      [f"{pref}.s1"], f"{pref}.m")
                P.dve(lambda v: v.tensor_tensor(statc[:, 1:2], statc[:, 0:1],
                                                statc[:, 0:1], ALU.mult),
                      [f"{pref}.s1"], f"{pref}.t1")
                P.dve(lambda v: v.scalar_tensor_tensor(
                    statc[:, 7:8], statc[:, 1:2], -1.0 / D, statc[:, 2:3],
                    ALU.mult, ALU.add), [f"{pref}.t1", f"{pref}.s2"], f"{pref}.var")
                P.act(lambda sc_: sc_.activation(statc[:, 9:10], statc[:, 7:8], AF.Ln,
                                                 bias=eps5col[:, :], scale=1.0 / D),
                      [f"{pref}.var"], f"{pref}.ln")
                P.act(lambda sc_: sc_.activation(statc[:, 8:9], statc[:, 9:10], AF.Exp,
                                                 scale=-0.5), [f"{pref}.ln"], f"{pref}.rstd")
                if debug_outputs and s == 0 and t == 15:
                    P.dma(SY, "dbg", lambda sy: sy.dma_start(dbg["z15"].ap(), out_st[1][:, :]),
                          [f"s{s}.p5.z.{t}"], "dbg.z15")
                    P.dma(SY, "dbg", lambda sy: sy.dma_start(dbg["yl15"].ap(), y1b2t[:, :]),
                          [f"s{s}.p5.z.{t}"], "dbg.yl15")
                    P.dma(SY, "dbg", lambda sy: sy.dma_start(dbg["h28"].ap(), hsb[28][:, :]),
                          [f"s{s}.p5.z.{t}"], "dbg.h28")
                wts = [f"{pref}.rstd"]
                if P.has(f"s{s}.out.{t-1}"):
                    wts.append(f"s{s}.out.{t-1}")
                P.dve(lambda v: v.tensor_scalar(out_st[0][:, :], out_st[1][:, :],
                                                statc[:, 5:6], statc[:, 8:9],
                                                ALU.subtract, ALU.mult),
                      wts, f"s{s}.p5.y2.{t}")
                P.dma(SY, "out", lambda sy, t=t: sy.dma_start(
                    out_d.ap()[base + t * 128:base + (t + 1) * 128, :], out_st[0][:, :]),
                    [f"s{s}.p5.y2.{t}"], f"s{s}.out.{t}")
        P.barrier_tagged(f"s{s}.end")

    p0()
    for s in range(SPC):
        p1(s)
        p23(s)
        p4(s)
        p5(s)

    with nc.Block() as block:
        def run(eng, lst, final_waits=()):
            for fn, waits, lane, amt in lst:
                for (wk, wv) in waits:
                    eng.wait_ge(sem[wk], wv)
                if fn is None:
                    continue
                instr = fn(eng)
                if amt:
                    instr.then_inc(sem[lane], amt)
            for (wk, wv) in final_waits:
                eng.wait_ge(sem[wk], wv)

        final = [(k, v) for k, v in P.cnt.items() if v > 0]

        @block.tensor
        def _(t):
            run(t, P.sched["tensor"])

        @block.scalar
        def _(s_):
            run(s_, P.sched["scalar"])

        @block.vector
        def _(v):
            run(v, P.sched["vector"])

        @block.sync
        def _(sy):
            run(sy, P.sched["sync"], final)

        @block.gpsimd
        def _(g):
            run(g, P.sched["gpsimd"])

    return nc


_cached = {}


def kernel(L0_attn_x, L0_attn_rby, L0_attn_Wq, L0_attn_bq, L0_attn_Wk,
           L0_attn_bk, L0_attn_Wv, L0_attn_bv, L0_attn_Wo, L0_attn_bo,
           L0_attn_ln_gamma, L0_attn_ln_beta, L0_ffn_w1, L0_ffn_b1,
           L0_ffn_w2, L0_ffn_b2):
    x = np.ascontiguousarray(np.asarray(L0_attn_x, dtype=np.float32))

    j = np.arange(512, dtype=np.float64)
    invd = np.repeat(np.power(10000.0, -j / 256.0), 2).astype(np.float32)[None, :]
    phase = np.zeros((1, D), np.float32)
    phase[0, 1::2] = np.float32(np.pi / 2)
    pos = (np.arange(128, dtype=np.float32)[:, None]
           + 128.0 * np.arange(NT, dtype=np.float32)[None, :])

    def col128(v, ncols):
        return np.ascontiguousarray(np.asarray(v, np.float32).reshape(ncols, 128).T)

    common = dict(
        rby=np.asarray(L0_attn_rby, np.float32).reshape(1, 3),
        wqT=np.ascontiguousarray(np.asarray(L0_attn_Wq, np.float32).T),
        wkT=np.ascontiguousarray(np.asarray(L0_attn_Wk, np.float32).T),
        wvT=np.ascontiguousarray(np.asarray(L0_attn_Wv, np.float32).T),
        woT=np.ascontiguousarray(np.asarray(L0_attn_Wo, np.float32).T),
        w1=np.ascontiguousarray(np.asarray(L0_ffn_w1, np.float32)),
        w2=np.ascontiguousarray(np.asarray(L0_ffn_w2, np.float32)),
        bv=np.asarray(L0_attn_bv, np.float32).reshape(1, D),
        bo=np.asarray(L0_attn_bo, np.float32).reshape(1, D),
        b2=np.asarray(L0_ffn_b2, np.float32).reshape(1, D),
        gamma=np.asarray(L0_attn_ln_gamma, np.float32).reshape(1, D),
        beta=np.asarray(L0_attn_ln_beta, np.float32).reshape(1, D),
        bqc=col128(L0_attn_bq, 8),
        bkc=col128(L0_attn_bk, 8),
        b1c=col128(L0_ffn_b1, 32),
        pos=np.ascontiguousarray(pos), invd=invd, phase=phase,
        eye=np.eye(128, dtype=np.float32).astype(ml_dtypes.bfloat16),
    )
    in_maps = []
    for c in range(NCORES):
        m = dict(common)
        m["x0"] = np.ascontiguousarray(x[c * SPC:(c + 1) * SPC].reshape(SPC * S, D))
        in_maps.append(m)

    if "nc" not in _cached:
        _cached["nc"] = build_program()
    nc = _cached["nc"]
    res = run_bass_kernel_spmd(nc, in_maps, core_ids=list(range(NCORES)))
    out = np.concatenate(
        [np.asarray(r["out"]).reshape(SPC, S, D) for r in res.results], axis=0)
    return out.astype(np.float32)


# version 18
# speedup vs baseline: 1.3540x; 1.0023x over previous
import numpy as np
import ml_dtypes
import concourse.bass as bass
import concourse.mybir as mybir
from concourse.bass_utils import run_bass_kernel_spmd

F32 = mybir.dt.float32
BF16 = mybir.dt.bfloat16
I32 = mybir.dt.int32
AF = mybir.ActivationFunctionType
ALU = mybir.AluOpType

B, S, D = 16, 2048, 1024
NCORES = 8
SPC = B // NCORES
NH, DK, NPAIR = 16, 64, 8
NT = S // 128
NQ = S // 512
NKT = S // 128
NPAIR_G = (S // 512) * (S // 128)
DFF = 4096
EXP_SHIFT = -14.0


def _codywaite():
    v = np.float64(2 * np.pi)

    def trunc12(x):
        i = np.frombuffer(np.float32(x).tobytes(), dtype=np.uint32)[0]
        i &= np.uint32(0xFFFFF000)
        return float(np.frombuffer(np.uint32(i).tobytes(), dtype=np.float32)[0])

    c1 = trunc12(v)
    c2 = trunc12(v - c1)
    c3 = float(np.float32(v - c1 - c2))
    return c1, c2, c3


CW1, CW2, CW3 = _codywaite()
INV2PI = float(np.float32(1.0 / (2 * np.pi)))


class Prog:
    ENGINES = ("tensor", "scalar", "vector", "sync", "gpsimd")
    COMPUTE_LANES = ("mm", "act", "dve", "gpc")
    DMA_LANES = ("x0a", "x0b", "x0c", "x0d", "pea", "peb", "pec", "ped",
                 "wv", "wq", "wk", "wo", "w1",
                 "w2", "tx0", "tx1", "tx2", "tx3", "ty0", "ty1", "ty2", "ty3",
                 "otb", "y1st", "y1s2", "y1ld", "out", "sta", "stb", "pest", "sc",
                 "cst", "bcv", "bcg", "bcbt", "bcb2", "dbg")

    def __init__(self):
        self.sched = {e: [] for e in self.ENGINES}
        self.cnt = {k: 0 for k in self.COMPUTE_LANES + self.DMA_LANES}
        self.marks = {}

    def at(self, tag):
        assert tag in self.marks, f"wait on undefined tag {tag}"
        return self.marks[tag]

    def has(self, tag):
        return tag in self.marks

    def emit(self, engine, lane, amt, fn, waits=(), tag=None):
        w = [self.at(t) if isinstance(t, str) else t for t in waits]
        byl = {}
        for (lk, lv) in w:
            byl[lk] = max(byl.get(lk, 0), lv)
        if tag is None and amt == 1:
            amt = 0
        self.sched[engine].append((fn, sorted(byl.items()), lane, amt))
        if amt:
            self.cnt[lane] += amt
            if tag:
                self.marks[tag] = (lane, self.cnt[lane])

    def mm(self, fn, waits=(), tag=None):
        self.emit("tensor", "mm", 1, fn, waits, tag)

    def act(self, fn, waits=(), tag=None):
        self.emit("scalar", "act", 1, fn, waits, tag)

    def dve(self, fn, waits=(), tag=None):
        self.emit("vector", "dve", 1, fn, waits, tag)

    def gp(self, fn, waits=(), tag=None):
        self.emit("gpsimd", "gpc", 1, fn, waits, tag)

    def dma(self, engine, lane, fn, waits=(), tag=None):
        self.emit(engine, lane, 16, fn, waits, tag)

    def barrier_tagged(self, prefix):
        for k, v in self.cnt.items():
            self.marks[f"{prefix}.{k}"] = (k, v)

    def gate(self, prefix):
        return [f"{prefix}.{k}" for k in self.cnt if self.cnt[k] > 0]

    def wait_only(self, engine, waits):
        w = [self.at(t) if isinstance(t, str) else t for t in waits]
        byl = {}
        for (lk, lv) in w:
            byl[lk] = max(byl.get(lk, 0), lv)
        self.sched[engine].append((None, sorted(byl.items()), None, 0))


def build_program(debug_outputs=False):
    nc = bass.Bass("TRN2", target_bir_lowering=False)

    x0_d = nc.dram_tensor("x0", [SPC * S, D], F32, kind="ExternalInput")
    rby_d = nc.dram_tensor("rby", [1, 3], F32, kind="ExternalInput")
    wqT_d = nc.dram_tensor("wqT", [D, D], F32, kind="ExternalInput")
    wkT_d = nc.dram_tensor("wkT", [D, D], F32, kind="ExternalInput")
    wvT_d = nc.dram_tensor("wvT", [D, D], F32, kind="ExternalInput")
    woT_d = nc.dram_tensor("woT", [D, D], F32, kind="ExternalInput")
    w1_d = nc.dram_tensor("w1", [D, DFF], F32, kind="ExternalInput")
    w2_d = nc.dram_tensor("w2", [DFF, D], F32, kind="ExternalInput")
    bv_d = nc.dram_tensor("bv", [1, D], F32, kind="ExternalInput")
    bo_d = nc.dram_tensor("bo", [1, D], F32, kind="ExternalInput")
    b2_d = nc.dram_tensor("b2", [1, D], F32, kind="ExternalInput")
    gamma_d = nc.dram_tensor("gamma", [1, D], F32, kind="ExternalInput")
    beta_d = nc.dram_tensor("beta", [1, D], F32, kind="ExternalInput")
    bqc_d = nc.dram_tensor("bqc", [128, 8], F32, kind="ExternalInput")
    bkc_d = nc.dram_tensor("bkc", [128, 8], F32, kind="ExternalInput")
    b1c_d = nc.dram_tensor("b1c", [128, 32], F32, kind="ExternalInput")
    pos_d = nc.dram_tensor("pos", [128, NT], F32, kind="ExternalInput")
    invd_d = nc.dram_tensor("invd", [1, D], F32, kind="ExternalInput")
    phase_d = nc.dram_tensor("phase", [1, D], F32, kind="ExternalInput")
    eye_d = nc.dram_tensor("eye", [128, 128], BF16, kind="ExternalInput")

    out_d = nc.dram_tensor("out", [SPC * S, D], F32, kind="ExternalOutput")

    pe_b = nc.dram_tensor("pe_b", [NT, 128, D], BF16, kind="Internal")
    peb_b = nc.dram_tensor("peb_b", [NT, 128, D], BF16, kind="Internal")
    y1_b = nc.dram_tensor("y1_b", [NT, 128, D], F32, kind="Internal")
    scl_hbm = nc.dram_tensor("scl_hbm", [1, 4], F32, kind="Internal")

    dbg = {}
    if debug_outputs:
        dbg["xpT"] = nc.dram_tensor("dbg_xpT", [8, 128, S], BF16, kind="ExternalOutput")
        dbg["v0"] = nc.dram_tensor("dbg_v0", [128, 1040], BF16, kind="ExternalOutput")
        dbg["ot"] = nc.dram_tensor("dbg_ot", [8, 128, S], BF16, kind="ExternalOutput")
        dbg["y1"] = nc.dram_tensor("dbg_y1", [NT, 128, D], F32, kind="ExternalOutput")
        dbg["pe"] = nc.dram_tensor("dbg_pe", [NT, 128, D], BF16, kind="ExternalOutput")
        dbg["z15"] = nc.dram_tensor("dbg_z15", [128, D], F32, kind="ExternalOutput")
        dbg["yl15"] = nc.dram_tensor("dbg_yl15", [128, D], F32, kind="ExternalOutput")
        dbg["h28"] = nc.dram_tensor("dbg_h28", [128, 512], BF16, kind="ExternalOutput")
        dbg["y1T0"] = nc.dram_tensor("dbg_y1T0", [128, S], BF16, kind="ExternalOutput")
        dbg["y1bf1"] = nc.dram_tensor("dbg_y1bf1", [128, D], BF16, kind="ExternalOutput")
        dbg["y1T0b"] = nc.dram_tensor("dbg_y1T0b", [128, 128], BF16, kind="ExternalOutput")

    def sb(name, shape, dtype, off):
        return nc.alloc_sbuf_tensor_at(name, shape, dtype, offset=off)

    BASE = 16512
    OFF_SMALL = BASE + 0
    OFF_H = BASE + 2048
    OFF_XPT = BASE + 15360
    OFF_W1R = BASE + 48128
    OFF_W2R = BASE + 113664
    OFF_HR = BASE + 179200

    ones64 = sb("ones64", [1, 64], BF16, OFF_SMALL + 0)
    expbias = sb("expbias", [128, 1], F32, OFF_SMALL + 128)
    zerocol = sb("zerocol", [128, 1], F32, OFF_SMALL + 160)
    eps5col = sb("eps5col", [128, 1], F32, OFF_SMALL + 192)
    bqcol = sb("bqcol", [128, 8], F32, OFF_SMALL + 224)
    bkcol = sb("bkcol", [128, 8], F32, OFF_SMALL + 256)
    b1col = sb("b1col", [128, 32], F32, OFF_SMALL + 288)
    poscol = sb("poscol", [128, NT], F32, OFF_SMALL + 416)
    sccol = sb("sccol", [128, 4], F32, OFF_SMALL + 480)
    rbysb = sb("rbysb", [1, 4], F32, OFF_SMALL + 512)
    scal = sb("scal", [1, 8], F32, OFF_SMALL + 544)
    statc = sb("statc", [128, 24], F32, OFF_SMALL + 576)
    eyesb = sb("eyesb", [128, 128], BF16, OFF_SMALL + 672)

    x0t = [sb(f"x0t{i}", [128, D], F32, OFF_H + i * 4096) for i in range(2)]
    xpbf = [sb(f"xpbf{i}", [128, D], BF16, OFF_H + 8192 + i * 2048) for i in range(2)]
    bv_bc = sb("bv_bc", [128, D], F32, OFF_H + 0)
    gamma_bc = sb("gamma_bc", [128, D], F32, OFF_H + 8192)
    out_st = [sb(f"out_st{i}", [128, D], F32, OFF_H + i * 4096) for i in range(2)]
    y1b2t = sb("y1b2t", [128, D], F32, OFF_H + 8192)
    lnrbuf = sb("lnrbuf", [1, 1024], F32, OFF_H + 4096)
    rinvbf = sb("rinvbf", [1, 1024], BF16, OFF_H + 11264)

    XPT = [sb(f"XPT{i}", [128, S], BF16, OFF_XPT + i * 4096) for i in range(8)]
    Y1T = [sb(f"Y1T{i}", [128, S], BF16, OFF_XPT + i * 4096) for i in range(8)]

    W1R = OFF_W1R
    Vt = [sb(f"Vt{i}", [128, 1040], BF16, W1R + i * 2080) for i in range(NT)]
    KTzp = [sb(f"KTzp{i}", [128, 2 * S], BF16, W1R + 33280 + i * 8192) for i in range(2)]
    QTp = [sb(f"QTp{i}", [128, S], BF16, W1R + 49664 + i * 4096) for i in range(2)]
    Ebuf = [sb(f"Ebuf{i}", [128, 1024], BF16, W1R + 57856 + i * 2048) for i in range(2)]
    Ebuf.append(sb("Ebuf2", [128, 1024], BF16, OFF_H + 0))
    OTBst = sb("OTBst", [64, 512], BF16, W1R + 61952)
    Rinv = sb("Rinv", [64, 512], F32, W1R + 62976)
    w1sb = [sb(f"w1sb{i}", [128, DFF], BF16, W1R + i * 8192) for i in range(8)]
    betab = sb("betab", [128, D], F32, W1R + 49152)
    b2b = sb("b2b", [128, D], F32, W1R + 53248)
    invd_bc = sb("invd_bc", [128, D], F32, W1R + 0)
    phase_bc = sb("phase_bc", [128, D], F32, W1R + 4096)
    ang = sb("ang", [128, D], F32, W1R + 8192)
    angk = sb("angk", [128, D], F32, W1R + 12288)
    angki = sb("angki", [128, D], I32, W1R + 16384)
    angr = sb("angr", [128, D], F32, W1R + 20480)
    pet = sb("pet", [128, D], BF16, W1R + 24576)
    pebt0 = sb("pebt0", [128, D], BF16, W1R + 26624)
    bo_bc = sb("bo_bc", [128, D], F32, W1R + 28672)

    W2R = OFF_W2R
    wvTsb = [sb(f"wvTsb{i}", [128, D], BF16, W2R + i * 2048) for i in range(8)]
    wqTsb = [sb(f"wqTsb{i}", [128, D], BF16, W2R + i * 2048) for i in range(8)]
    wkTsb = [sb(f"wkTsb{i}", [128, D], BF16, W2R + 16384 + i * 2048) for i in range(8)]
    xpe = [sb(f"xpe{i}", [128, D], F32, W2R + i * 4096) for i in range(2)]
    xpb = [sb(f"xpb{i}", [128, D], F32, W2R + 8192 + i * 4096) for i in range(2)]
    vnbuf2 = sb("vnbuf2", [128, D], F32, W2R + 16384)
    x0t2 = sb("x0t2", [128, D], F32, W2R + 20480)
    x0t3 = sb("x0t3", [128, D], F32, W2R + 24576)
    pebt2 = sb("pebt2", [128, D], BF16, W2R + 28672)
    pebt3 = sb("pebt3", [128, D], BF16, W2R + 30720)
    OT = [sb(f"OT{i}", [128, S], BF16, W2R + 32768 + i * 4096) for i in range(8)]
    w2sb = [sb(f"w2sb{i}", [128, D], BF16, W2R + i * 2048) for i in range(32)]

    HR = OFF_HR
    woTsb = [sb(f"woTsb{i}", [128, D], BF16, HR + i * 2048) for i in range(8)]
    pebt = [sb(f"pebt{i}", [128, D], BF16, HR + 16384 + i * 2048) for i in range(2)]
    ubuf = sb("ubuf", [128, D], F32, HR + 20480)
    ubuf2 = sb("ubuf2", [128, D], F32, W1R + 57856)
    vnbuf = sb("vnbuf", [128, D], F32, HR + 24576)
    y1bf = [sb(f"y1bf{i}", [128, D], BF16, HR + 28672 + i * 2048) for i in range(2)]
    hsb = [sb(f"hsb{i}", [128, 512], BF16, HR + i * 1024) for i in range(32)]

    PS = [nc.alloc_psum_tensor(f"PS{i}", [128, 1024], F32) for i in range(4)]

    P = Prog()
    sem = {}
    for name in P.COMPUTE_LANES + P.DMA_LANES:
        sem[name] = nc.semaphore(f"sem_{name}", None).__enter__()

    SY, SC, GP = "sync", "scalar", "gpsimd"

    def p0():
        P.dma(GP, "cst", lambda g: g.dma_start(rbysb[:, 0:3], rby_d.ap()))
        P.dma(GP, "cst", lambda g: g.dma_start(invd_bc[:, :], invd_d.ap().partition_broadcast(128)))
        P.dma(GP, "cst", lambda g: g.dma_start(phase_bc[:, :], phase_d.ap().partition_broadcast(128)))
        P.dma(GP, "cst", lambda g: g.dma_start(poscol[:, :], pos_d.ap()))
        P.dma(GP, "cst", lambda g: g.dma_start(bqcol[:, :], bqc_d.ap()))
        P.dma(GP, "cst", lambda g: g.dma_start(bkcol[:, :], bkc_d.ap()))
        P.dma(GP, "cst", lambda g: g.dma_start(b1col[:, :], b1c_d.ap()))
        P.dma(GP, "cst", lambda g: g.dma_start(bo_bc[:, :], bo_d.ap().partition_broadcast(128)))
        P.dma(GP, "cst", lambda g: g.dma_start(eyesb[:, :], eye_d.ap()), (), "csts")

        P.gp(lambda g: g.memset(ones64[:, :], 1.0), (), "ms.ones")
        P.gp(lambda g: g.memset(expbias[:, :], EXP_SHIFT), (), "ms.eb")
        P.gp(lambda g: g.memset(zerocol[:, :], 0.0), (), "ms.z")
        P.gp(lambda g: g.memset(eps5col[:, :], 1e-5), (), "ms.e5")

        r_, b_, y_ = rbysb[0:1, 0:1], rbysb[0:1, 1:2], rbysb[0:1, 2:3]

        def ts1(out, in_, s1, op0, s2=None, op1=None):
            if s2 is None:
                return lambda v: v.tensor_scalar(out, in_, s1, None, op0)
            return lambda v: v.tensor_scalar(out, in_, s1, s2, op0, op1)

        w = ["csts"]
        P.dve(ts1(scal[0:1, 3:4], r_, 0.0125, ALU.mult, 0.125, ALU.add), w, "sc.a")
        P.dve(ts1(scal[0:1, 4:5], b_, 0.00625, ALU.mult), w, "sc.b")
        P.dve(ts1(scal[0:1, 5:6], y_, 0.01875, ALU.mult), w, "sc.c")
        P.dve(lambda v: v.tensor_add(scal[0:1, 6:7], scal[0:1, 3:4], scal[0:1, 4:5]), ["sc.a", "sc.b"], "sc.d")
        P.dve(lambda v: v.tensor_add(scal[0:1, 0:1], scal[0:1, 6:7], scal[0:1, 5:6]), ["sc.c", "sc.d"], "sc.sp")
        P.dve(ts1(scal[0:1, 1:2], b_, 1e-6, ALU.mult, 1e-5, ALU.add), w, "sc.eps")
        P.dve(ts1(scal[0:1, 7:8], y_, 0.1, ALU.mult, 1.0, ALU.add), w, "sc.cy0")
        P.dve(lambda v: v.reciprocal(scal[0:1, 2:3], scal[0:1, 7:8]), ["sc.cy0"], "sc.cy")
        P.dve(ts1(scal[0:1, 4:5], r_, 0.1, ALU.mult, 1.0, ALU.add), ["sc.d"], "sc.g1")
        P.dve(ts1(scal[0:1, 5:6], y_, 0.05, ALU.mult), ["sc.sp"], "sc.g2")
        P.dve(lambda v: v.tensor_add(scal[0:1, 3:4], scal[0:1, 4:5], scal[0:1, 5:6]), ["sc.g1", "sc.g2"], "sc.gsc")
        P.dma(SY, "sc", lambda s: s.dma_start(scl_hbm.ap(), scal[0:1, 0:4]),
              ["sc.sp", "sc.eps", "sc.cy", "sc.gsc"], "sc.st")
        P.dma(SY, "sc", lambda s: s.dma_start(sccol[:, :], scl_hbm.ap().partition_broadcast(128)),
              ["sc.st"], "sc.bc")

        for t in range(NT):
            wd = ["csts", "sc.bc"]
            if t >= 1:
                wd.append(f"pe.st1.{t-1}")
            P.dve(lambda v, t=t: v.tensor_scalar(
                ang[:, :], invd_bc[:, :], sccol[:, 2:3], poscol[:, t:t + 1],
                ALU.mult, ALU.mult), wd, f"pe.ang.{t}")
            P.dve(lambda v: v.tensor_add(ang[:, :], ang[:, :], phase_bc[:, :]),
                  [f"pe.ang.{t}"], f"pe.ph.{t}")
            P.dve(lambda v: v.tensor_scalar(angk[:, :], ang[:, :], INV2PI, None,
                                            ALU.mult), [f"pe.ph.{t}"], f"pe.k0.{t}")
            P.dve(lambda v: v.tensor_copy(angki[:, :], angk[:, :]), [f"pe.k0.{t}"], f"pe.ki.{t}")
            P.dve(lambda v: v.tensor_copy(angk[:, :], angki[:, :]), [f"pe.ki.{t}"], f"pe.kf.{t}")
            P.dve(lambda v: v.scalar_tensor_tensor(
                angr[:, :], angk[:, :], -CW1, ang[:, :], ALU.mult, ALU.add),
                [f"pe.kf.{t}"], f"pe.cw1.{t}")
            P.dve(lambda v: v.scalar_tensor_tensor(
                ang[:, :], angk[:, :], -CW2, angr[:, :], ALU.mult, ALU.add),
                [f"pe.cw1.{t}"], f"pe.cw2.{t}")
            P.dve(lambda v: v.scalar_tensor_tensor(
                angr[:, :], angk[:, :], -CW3, ang[:, :], ALU.mult, ALU.add),
                [f"pe.cw2.{t}"], f"pe.cw.{t}")
            P.act(lambda sc_: sc_.activation(pet[:, :], angr[:, :], AF.Sin),
                  [f"pe.cw.{t}"], f"pe.sin.{t}")
            P.dve(lambda v: v.tensor_add(pebt0[:, :], pet[:, :], bo_bc[:, :]),
                  [f"pe.sin.{t}"], f"pe.peb.{t}")
            P.dma(SY, "pest", lambda s, t=t: s.dma_start(pe_b.ap()[t], pet[:, :]),
                  [f"pe.sin.{t}"], f"pe.st0.{t}")
            P.dma(SY, "pest", lambda s, t=t: s.dma_start(peb_b.ap()[t], pebt0[:, :]),
                  [f"pe.peb.{t}"], f"pe.st1.{t}")
            if debug_outputs:
                P.dma(SY, "dbg", lambda s, t=t: s.dma_start(dbg["pe"].ap()[t], pet[:, :]),
                      [f"pe.sin.{t}"], f"pe.dbg.{t}")
        P.marks["pes.done"] = ("pest", P.cnt["pest"])


    def p1(s):
        base = s * S
        if s > 0:
            for e in P.ENGINES:
                P.wait_only(e, P.gate(f"s{s-1}.end"))
        for t in range(NT):
            i = t % 2
            xl = "x0a" if i == 0 else "x0b"
            pl = "pea" if i == 0 else "peb"
            wts = []
            if t >= 2:
                wts.append(f"s{s}.p1.cast.{t-2}")
            P.dma(SY, xl, lambda sy, t=t, i=i: sy.dma_start(
                x0t[i][:, :], x0_d.ap()[base + t * 128:base + (t + 1) * 128, :]),
                wts, f"s{s}.p1.x0.{t}")
            wts = [f"s{s}.p1.x0.{t}", f"pe.st0.{t}"]
            if t >= 2:
                wts.append(f"s{s}.p1.tr.{t-2}.7")
            P.dma(GP, pl, lambda g, t=t, i=i: g.dma_start(xpbf[i][:, :], pe_b.ap()[t]),
                  wts, f"s{s}.p1.pe.{t}")
            P.dve(lambda v, i=i: v.tensor_tensor(xpbf[i][:, :], x0t[i][:, :],
                                                 xpbf[i][:, :], ALU.add),
                  [f"s{s}.p1.pe.{t}", f"s{s}.p1.x0.{t}"], f"s{s}.p1.cast.{t}")
            for ic in range(8):
                bank = PS[2][:, (ic % 2) * 512:(ic % 2) * 512 + 64].bitcast(BF16)
                wts = [f"s{s}.p1.cast.{t}"]
                if ic == 0 and t >= 1:
                    wts.append(f"s{s}.p1.ev.{t-1}.7")
                if ic >= 2:
                    wts.append(f"s{s}.p1.ev.{t}.{ic-2}")
                P.mm(lambda tns, i=i, ic=ic, bank=bank: tns.transpose(
                    bank, xpbf[i][:, ic * 128:(ic + 1) * 128], eyesb[:, :]),
                    wts, f"s{s}.p1.tr.{t}.{ic}")
                P.dve(lambda v, t=t, ic=ic, bank=bank: v.tensor_copy(
                    XPT[ic][:, t * 128:(t + 1) * 128], bank),
                    [f"s{s}.p1.tr.{t}.{ic}"], f"s{s}.p1.ev.{t}.{ic}")
        if debug_outputs and s == 0:
            for ic in range(8):
                P.dma(SY, "dbg", lambda sy, ic=ic: sy.dma_start(
                    dbg["xpT"].ap()[ic], XPT[ic][:, :]),
                    [f"s{s}.p1.ev.{NT-1}.{ic}"], f"dbg.xpT.{ic}")

    def proj_v(s):
        P.dma(GP, "bcv", lambda g: g.dma_start(bv_bc[:, :], bv_d.ap().partition_broadcast(128)),
              [f"s{s}.p1.cast.{NT-1}", f"s{s}.p1.cast.{NT-2}"], f"s{s}.bv")
        for i in range(8):
            P.dma(GP, "wv", lambda g, i=i: g.dma_start(
                wvTsb[i][:, :], wvT_d.ap()[i * 128:(i + 1) * 128, :]),
                (), f"s{s}.wvT.{i}")
        for t in range(NT):
            wts = [f"pe.peb.{NT-1}", f"pe.sin.{NT-1}", "pes.done"] if s == 0 else []
            P.gp(lambda g, t=t: g.memset(
                Vt[t][:, :].rearrange("p (h c) -> p h c", c=65)[:, :, 64:65], 1.0),
                wts, f"s{s}.vones.{t}")
        for t in range(NT):
            wts = [f"s{s}.p1.ev.{t}.{ic2}" for ic2 in range(8)]
            if t == 0:
                wts += [f"s{s}.wvT.7", f"s{s}.vones.{NT-1}"]
            if t >= 1:
                wts.append(f"s{s}.vev.{t-1}.1")
            for ic in range(8):
                for half in range(2):
                    P.mm(lambda tns, t=t, ic=ic, half=half: tns.matmul(
                        PS[3][:, half * 512:(half + 1) * 512],
                        XPT[ic][:, t * 128:(t + 1) * 128],
                        wvTsb[ic][:, half * 512:(half + 1) * 512],
                        start=(ic == 0), stop=(ic == 7)),
                        wts if (ic == 0 and half == 0) else (),
                        f"s{s}.vmm.{t}.{half}" if ic == 7 else None)
            for half in range(2):
                h0 = half * 8
                P.dve(lambda v, t=t, half=half, h0=h0: v.tensor_tensor(
                    Vt[t][:, :].rearrange("p (h c) -> p h c", c=65)[:, h0:h0 + 8, 0:64],
                    PS[3][:, half * 512:(half + 1) * 512].rearrange("p (h c) -> p h c", c=64),
                    bv_bc[:, half * 512:(half + 1) * 512].rearrange("p (h c) -> p h c", c=64),
                    ALU.add),
                    [f"s{s}.vmm.{t}.{half}", f"s{s}.bv"], f"s{s}.vev.{t}.{half}")
        if debug_outputs and s == 0:
            P.dma(SY, "dbg", lambda sy: sy.dma_start(dbg["v0"].ap(), Vt[0][:, :]),
                  [f"s{s}.vev.0.0", f"s{s}.vev.0.1"], "dbg.v0")

    def proj_qk_steps(s, c, last_proj):
        i = c % 2
        steps = []

        def zero_step():
            wz = []
            if c >= 2:
                wz.append(f"s{s}.attdone.{c-2}")
            P.dve(lambda v, i=i: v.memset(KTzp[i][64:128, 0:S], 0.0), wz, f"s{s}.kz0.{c}")
            P.dve(lambda v, i=i: v.memset(KTzp[i][0:64, S:2 * S], 0.0), wz, f"s{s}.kz1.{c}")

        def mk_mm(kind, tsl, ic):
            wsb = wqTsb if kind == "q" else wkTsb
            def em():
                wts = []
                if ic == 0:
                    wts.append(f"s{s}.wqT.7" if kind == "q" else f"s{s}.wkT.7")
                    if c >= 2:
                        wts.append(f"s{s}.attdone.{c-2}")
                    if last_proj[0] is not None:
                        wts.append(last_proj[0])
                    if last_proj[1] is not None:
                        wts.append(last_proj[1])
                for h2 in range(2):
                    P.mm(lambda tns, tsl=tsl, ic=ic, h2=h2, wsb=wsb: tns.matmul(
                        PS[3][:, h2 * 512:(h2 + 1) * 512],
                        wsb[ic][:, c * 128:(c + 1) * 128],
                        XPT[ic][:, (tsl + h2) * 512:(tsl + h2 + 1) * 512],
                        start=(ic == 0), stop=(ic == 7)),
                        wts if h2 == 0 else (),
                        f"s{s}.{kind}mm.{c}.{tsl + h2}" if ic == 7 else None)
                if ic == 7:
                    if kind == "q":
                        for h2 in range(2):
                            P.dve(lambda v, tsl=tsl, h2=h2: v.tensor_scalar(
                                QTp[i][:, (tsl + h2) * 512:(tsl + h2 + 1) * 512],
                                PS[3][:, h2 * 512:(h2 + 1) * 512],
                                bqcol[:, c:c + 1], sccol[:, 0:1], ALU.add, ALU.mult),
                                [f"s{s}.qmm.{c}.{tsl + h2}", "sc.bc"],
                                f"s{s}.qev.{c}.{tsl + h2}")
                        last_proj[0] = f"s{s}.qev.{c}.{tsl + 1}"
                    else:
                        for h2 in range(2):
                            P.dve(lambda v, tsl=tsl, h2=h2: v.tensor_scalar(
                                KTzp[i][0:64, (tsl + h2) * 512:(tsl + h2 + 1) * 512],
                                PS[3][0:64, h2 * 512:(h2 + 1) * 512],
                                bkcol[0:64, c:c + 1], None, ALU.add),
                                [f"s{s}.kmm.{c}.{tsl + h2}", f"s{s}.kz0.{c}",
                                 f"s{s}.kz1.{c}"], f"s{s}.kevA.{c}.{tsl + h2}")
                            P.dve(lambda v, tsl=tsl, h2=h2: v.tensor_scalar(
                                KTzp[i][64:128, S + (tsl + h2) * 512:S + (tsl + h2 + 1) * 512],
                                PS[3][64:128, h2 * 512:(h2 + 1) * 512],
                                bkcol[64:128, c:c + 1], None, ALU.add),
                                [f"s{s}.kevA.{c}.{tsl + h2}"], f"s{s}.kevB.{c}.{tsl + h2}")
                        last_proj[0] = f"s{s}.kevB.{c}.{tsl + 1}"
            return em

        first = True
        for kind, tsl in (("q", 0), ("q", 2), ("k", 0), ("k", 2)):
            for ic in range(8):
                em = mk_mm(kind, tsl, ic)
                if first:
                    def em0(em=em):
                        zero_step()
                        em()
                    steps.append(em0)
                    first = False
                else:
                    steps.append(em)
        return steps

    def attention_pair(s, c, interleave_steps, last_proj):
        i = c % 2

        def eb(g):
            return Ebuf[g % 3]

        def emit_sa_sb(q, kt):
            sp = kt % 2
            wts = []
            if kt == 0 and q == 0:
                wts += [f"s{s}.qev.{c}.{t}" for t in range(4)]
                wts += [f"s{s}.kevB.{c}.{t}" for t in range(4)]
            if kt >= 2:
                wts.append(f"s{s}.exp.{c}.{q}.{kt-2}")
            elif q > 0:
                wts.append(f"s{s}.exp.{c}.{q-1}.{NKT-2+kt}")
            elif c > 0:
                wts.append(f"s{s}.exp.{c-1}.{NQ-1}.{NKT-2+kt}")
            P.mm(lambda tns, i=i, q=q, kt=kt, sp=sp: tns.matmul(
                PS[sp][:, 0:512],
                KTzp[i][:, kt * 128:(kt + 1) * 128],
                QTp[i][:, q * 512:(q + 1) * 512],
                start=True, stop=True), wts, f"s{s}.sa.{c}.{q}.{kt}")
            P.mm(lambda tns, i=i, q=q, kt=kt, sp=sp: tns.matmul(
                PS[sp][:, 512:1024],
                KTzp[i][:, S + kt * 128:S + (kt + 1) * 128],
                QTp[i][:, q * 512:(q + 1) * 512],
                start=True, stop=True), (), f"s{s}.sb.{c}.{q}.{kt}")

        def emit_exp(q, kt):
            sp = kt % 2
            g = q * NKT + kt
            wts = [f"s{s}.sb.{c}.{q}.{kt}", "ms.eb"]
            gp = g - 3
            if gp >= 0:
                wts.append(f"s{s}.pvB.{c}.{gp // NKT}.{gp % NKT}")
            elif c > 0:
                gq = NPAIR_G + gp
                wts.append(f"s{s}.pvB.{c-1}.{gq // NKT}.{gq % NKT}")
            if c == 0 and g == 2:
                wts += [f"s{s}.vev.{NT-1}.1"]
            P.act(lambda sc_, sp=sp, g=g: sc_.activation(
                eb(g)[:, :], PS[sp][:, :], AF.Exp, bias=expbias[:, :]),
                wts, f"s{s}.exp.{c}.{q}.{kt}")

        def emit_pv(q, kt, extra=()):
            g = q * NKT + kt
            wts = [f"s{s}.exp.{c}.{q}.{kt}"] + list(extra)
            if c == 0 and q == 0:
                wts += [f"s{s}.vev.{kt}.0", f"s{s}.vev.{kt}.1"]
            P.mm(lambda tns, c=c, kt=kt, g=g: tns.matmul(
                PS[2][0:65, 0:512], Vt[kt][:, 130 * c:130 * c + 65],
                eb(g)[:, 0:512],
                start=(kt == 0), stop=(kt == NKT - 1)), wts,
                f"s{s}.pvA.{c}.{q}.{kt}")
            P.mm(lambda tns, c=c, kt=kt, g=g: tns.matmul(
                PS[2][0:65, 512:1024], Vt[kt][:, 130 * c + 65:130 * c + 130],
                eb(g)[:, 512:1024],
                start=(kt == 0), stop=(kt == NKT - 1)), (),
                f"s{s}.pvB.{c}.{q}.{kt}")

        def tail_act(q):
            wts = [f"s{s}.pvA.{c}.{q}.{NKT-1}", f"s{s}.pvB.{c}.{q}.{NKT-1}"]
            P.act(lambda sc_: sc_.activation(lnrbuf[:, :], PS[2][64:65, 0:1024],
                                             AF.Ln), wts, f"s{s}.lnr.{c}.{q}")
            wts = [f"s{s}.lnr.{c}.{q}"]
            if P.has(f"s{s}.Rmm.{c}.{q-1}.B"):
                wts.append(f"s{s}.Rmm.{c}.{q-1}.B")
            if c > 0 and q == 0:
                wts.append(f"s{s}.Rmm.{c-1}.{NQ-1}.B")
            P.act(lambda sc_: sc_.activation(rinvbf[:, :], lnrbuf[:, :], AF.Exp,
                                             scale=-1.0), wts, f"s{s}.rinv.{c}.{q}")

        def tail_pe(q):
            wts = [f"s{s}.rinv.{c}.{q}", "ms.ones"]
            if P.has(f"s{s}.rcpB.{c}.{q-1}"):
                wts.append(f"s{s}.rcpB.{c}.{q-1}")
            if c > 0 and q == 0:
                wts.append(f"s{s}.rcpB.{c-1}.{NQ-1}")
            if last_proj[0] is not None:
                wts.append(last_proj[0])
            P.mm(lambda tns: tns.matmul(PS[3][0:64, 0:512], ones64[:, :],
                                        rinvbf[0:1, 0:512], start=True, stop=True),
                 wts, f"s{s}.Rmm.{c}.{q}.A")
            P.mm(lambda tns: tns.matmul(PS[3][0:64, 512:1024], ones64[:, :],
                                        rinvbf[0:1, 512:1024], start=True, stop=True),
                 [f"s{s}.Rmm.{c}.{q}.A"], f"s{s}.Rmm.{c}.{q}.B")
            P.dve(lambda v: v.tensor_copy(Rinv[:, :], PS[3][0:64, 0:512]),
                  [f"s{s}.Rmm.{c}.{q}.A"], f"s{s}.rcpA.{c}.{q}")
            P.dve(lambda v, c=c, q=q: v.tensor_tensor(
                OT[c][0:64, q * 512:(q + 1) * 512], PS[2][0:64, 0:512],
                Rinv[:, :], ALU.mult), [f"s{s}.rcpA.{c}.{q}"], f"s{s}.otA.{c}.{q}")
            P.dve(lambda v: v.tensor_copy(Rinv[:, :], PS[3][0:64, 512:1024]),
                  [f"s{s}.Rmm.{c}.{q}.B", f"s{s}.otA.{c}.{q}"], f"s{s}.rcpB.{c}.{q}")
            wts = [f"s{s}.rcpB.{c}.{q}"]
            if P.has(f"s{s}.otbdma.{c}.{q-1}"):
                wts.append(f"s{s}.otbdma.{c}.{q-1}")
            elif c > 0:
                wts.append(f"s{s}.otbdma.{c-1}.{NQ-1}")
            P.dve(lambda v: v.tensor_tensor(OTBst[:, :], PS[2][0:64, 512:1024],
                                            Rinv[:, :], ALU.mult), wts, f"s{s}.otB.{c}.{q}")
            P.dma(SY, "otb", lambda sy, c=c, q=q: sy.dma_start(
                OT[c][64:128, q * 512:(q + 1) * 512], OTBst[:, :]),
                [f"s{s}.otB.{c}.{q}"], f"s{s}.otbdma.{c}.{q}")
            last_proj[1] = f"s{s}.rcpB.{c}.{q}"

        def pop():
            if interleave_steps:
                interleave_steps.pop(0)()

        for q in range(NQ):
            for kt in range(NKT):
                if kt < 3:
                    emit_sa_sb(q, kt)
                    emit_exp(q, kt)
                    if kt == 1 and q > 0:
                        tail_pe(q - 1)
                elif kt == 3:
                    emit_sa_sb(q, kt)
                    extra = []
                    if q > 0:
                        extra = [f"s{s}.otA.{c}.{q-1}", f"s{s}.otB.{c}.{q-1}"]
                    elif c > 0:
                        extra = [f"s{s}.otA.{c-1}.{NQ-1}", f"s{s}.otB.{c-1}.{NQ-1}"]
                    emit_pv(q, 0, extra=extra)
                    emit_pv(q, 1)
                    emit_pv(q, 2)
                    emit_exp(q, kt)
                else:
                    emit_sa_sb(q, kt)
                    emit_pv(q, kt - 1)
                    emit_exp(q, kt)
                    if 4 <= kt <= 11:
                        pop()
            emit_pv(q, NKT - 1)
            tail_act(q)
        tail_pe(NQ - 1)
        while interleave_steps:
            interleave_steps.pop(0)()
        P.marks[f"s{s}.attdone.{c}"] = ("dve", P.cnt["dve"])

    def p23(s):
        proj_v(s)
        for ic in range(8):
            P.dma(GP, "wq", lambda g, ic=ic: g.dma_start(
                wqTsb[ic][:, :], wqT_d.ap()[ic * 128:(ic + 1) * 128, :]),
                [f"s{s}.vmm.{NT-1}.1"] if ic == 0 else (), f"s{s}.wqT.{ic}")
        for ic in range(8):
            P.dma(GP, "wk", lambda g, ic=ic: g.dma_start(
                wkTsb[ic][:, :], wkT_d.ap()[ic * 128:(ic + 1) * 128, :]),
                (), f"s{s}.wkT.{ic}")
        for ic in range(8):
            P.dma(GP, "wo", lambda g, ic=ic: g.dma_start(
                woTsb[ic][:, :], woT_d.ap()[ic * 128:(ic + 1) * 128, :]),
                (), f"s{s}.woT.{ic}")
        last_proj = [None, None]
        for st in proj_qk_steps(s, 0, last_proj):
            st()
        for c in range(NPAIR):
            steps = proj_qk_steps(s, c + 1, last_proj) if c + 1 < NPAIR else []
            attention_pair(s, c, steps, last_proj)
        P.barrier_tagged(f"s{s}.attend")
        if debug_outputs and s == 0:
            for ic in range(8):
                P.dma(SY, "dbg", lambda sy, ic=ic: sy.dma_start(
                    dbg["ot"].ap()[ic], OT[ic][:, :]),
                    [f"s{s}.otA.{ic}.{NQ-1}", f"s{s}.otbdma.{ic}.{NQ-1}"],
                    f"dbg.ot.{ic}")

    def p4(s):
        base = s * S
        gate = [f"s{s}.attend.mm", f"s{s}.attend.dve", f"s{s}.attend.act",
                f"s{s}.attend.otb"]
        for e in P.ENGINES:
            P.wait_only(e, gate)
        P.dma(GP, "bcg", lambda g: g.dma_start(gamma_bc[:, :], gamma_d.ap().partition_broadcast(128)),
              (), f"s{s}.gbc0")
        P.dve(lambda v: v.tensor_scalar(gamma_bc[:, :], gamma_bc[:, :], sccol[:, 3:4],
                                        None, ALU.mult), [f"s{s}.gbc0", "sc.bc"], f"s{s}.gbc")
        P.dma(GP, "bcbt", lambda g: g.dma_start(betab[:, :], beta_d.ap().partition_broadcast(128)),
              (), f"s{s}.bbc")
        P.dma(GP, "bcb2", lambda g: g.dma_start(b2b[:, :], b2_d.ap().partition_broadcast(128)),
              (), f"s{s}.b2bc")

        def ln_stats(pref, srcb, wtag, so):
            P.act(lambda sc_, srcb=srcb, so=so: sc_.activation(
                PS[1][:, :], srcb[:, :], AF.Copy, accum_out=statc[:, so:so + 1]),
                [wtag], f"{pref}.s1")
            P.act(lambda sc_, srcb=srcb, so=so: sc_.activation(
                PS[1][:, :], srcb[:, :], AF.Square, accum_out=statc[:, so + 2:so + 3]),
                [wtag], f"{pref}.s2")
            P.dve(lambda v, so=so: v.tensor_scalar(
                statc[:, so + 5:so + 6], statc[:, so:so + 1], 1.0 / D, None, ALU.mult),
                [f"{pref}.s1"], f"{pref}.m")
            P.dve(lambda v, so=so: v.tensor_tensor(
                statc[:, so + 1:so + 2], statc[:, so:so + 1], statc[:, so:so + 1],
                ALU.mult), [f"{pref}.s1"], f"{pref}.t1")
            P.dve(lambda v, so=so: v.scalar_tensor_tensor(
                statc[:, so + 7:so + 8], statc[:, so + 1:so + 2], -1.0 / D,
                statc[:, so + 2:so + 3], ALU.mult, ALU.add),
                [f"{pref}.t1", f"{pref}.s2"], f"{pref}.var")

        UB = [ubuf, ubuf2]
        X04 = [x0t[0], x0t[1], x0t2, x0t3]
        PB4 = [pebt[0], pebt[1], pebt2, pebt3]
        XL4 = ["x0a", "x0b", "x0c", "x0d"]
        PL4 = ["pea", "peb", "pec", "ped"]

        def load_x0pe(t, xw, pw):
            j = t % 4
            P.dma(SY, XL4[j], lambda sy, t=t, j=j: sy.dma_start(
                X04[j][:, :], x0_d.ap()[base + t * 128:base + (t + 1) * 128, :]),
                xw, f"s{s}.p4.x0.{t}")
            P.dma(SC, PL4[j], lambda sc_, t=t, j=j: sc_.dma_start(
                PB4[j][:, :], peb_b.ap()[t]),
                pw + [f"pe.st1.{t}"], f"s{s}.p4.pe.{t}")

        for k in range(4):
            load_x0pe(k, [], [])
        for ic in (0, 1, 2, 3, 4, 5):
            P.dma(GP, "w1", lambda g, ic=ic: g.dma_start(
                w1sb[ic][:, :], w1_d.ap()[ic * 128:(ic + 1) * 128, :]),
                (), f"s{s}.w1.{ic}")

        def stage_a_mm(t):
            ps = PS[3] if t % 2 == 0 else PS[0]
            wts = [f"s{s}.woT.7"]
            if t >= 2:
                wts.append(f"s{s}.p4.u.{t-2}")
            for ic in range(8):
                for half in range(2):
                    P.mm(lambda tns, t=t, ic=ic, half=half, ps=ps: tns.matmul(
                        ps[:, half * 512:(half + 1) * 512],
                        OT[ic][:, t * 128:(t + 1) * 128],
                        woTsb[ic][:, half * 512:(half + 1) * 512],
                        start=(ic == 0), stop=(ic == 7)),
                        wts if (ic == 0 and half == 0) else (),
                        f"s{s}.p4.mm.{t}" if (ic == 7 and half == 1) else None)

        def stage_a(t):
            i = t % 2
            j = t % 4
            so = 12 * i
            ps = PS[3] if t % 2 == 0 else PS[0]
            ub = UB[i]
            wts = [f"s{s}.p4.x0.{t}", f"s{s}.p4.pe.{t}"]
            if t >= 2:
                wts.append(f"s{s}.p4.u.{t-2}")
            P.gp(lambda g, i=i, j=j: g.tensor_tensor(xpe[i][:, :], X04[j][:, :],
                                                     PB4[j][:, :], ALU.add),
                 wts, f"s{s}.p4.xpe.{t}")
            wts = [f"s{s}.p4.x0.{t}", f"s{s}.bbc"]
            if t >= 2:
                wts.append(f"s{s}.p4.w.{t-2}")
            P.gp(lambda g, i=i, j=j: g.tensor_tensor(xpb[i][:, :], X04[j][:, :],
                                                     betab[:, :], ALU.add),
                 wts, f"s{s}.p4.xpb.{t}")
            if t + 4 < NT:
                load_x0pe(t + 4, [f"s{s}.p4.xpe.{t}", f"s{s}.p4.xpb.{t}"],
                          [f"s{s}.p4.xpe.{t}"])
            P.dve(lambda v, ps=ps, i=i, ub=ub: v.tensor_tensor(
                ub[:, :], ps[:, :], xpe[i][:, :], ALU.add),
                  [f"s{s}.p4.mm.{t}", f"s{s}.p4.xpe.{t}"], f"s{s}.p4.u.{t}")
            ln_stats(f"s{s}.p4.a{t}", ub, f"s{s}.p4.u.{t}", so)
            P.act(lambda sc_, so=so: sc_.activation(statc[:, so + 9:so + 10],
                                                    statc[:, so + 7:so + 8], AF.Ln,
                                                    bias=sccol[:, 1:2], scale=1.0 / D),
                  [f"s{s}.p4.a{t}.var", "sc.bc"], f"s{s}.p4.ln.{t}")
            P.act(lambda sc_, so=so: sc_.activation(statc[:, so + 8:so + 9],
                                                    statc[:, so + 9:so + 10], AF.Exp,
                                                    scale=-0.5), [f"s{s}.p4.ln.{t}"],
                  f"s{s}.p4.rstd.{t}")

        VN = [vnbuf, vnbuf2]

        def stage_b(t):
            i = t % 2
            so = 12 * i
            ub = UB[i]
            vn = VN[i]
            wts = [f"s{s}.p4.rstd.{t}"]
            if t >= 2:
                wts.append(f"s{s}.p4.y1dma.{t-2}")
            P.dve(lambda v, ub=ub, so=so, vn=vn: v.tensor_scalar(
                vn[:, :], ub[:, :], statc[:, so + 5:so + 6],
                statc[:, so + 8:so + 9], ALU.subtract, ALU.mult),
                wts, f"s{s}.p4.n1.{t}")
            P.dve(lambda v, vn=vn: v.tensor_tensor(vn[:, :], vn[:, :], gamma_bc[:, :],
                                                   ALU.mult),
                  [f"s{s}.p4.n1.{t}", f"s{s}.gbc"], f"s{s}.p4.n2.{t}")
            P.dve(lambda v, i=i, ub=ub, vn=vn: v.tensor_tensor(
                ub[:, :], vn[:, :], xpb[i][:, :], ALU.add),
                  [f"s{s}.p4.n2.{t}", f"s{s}.p4.xpb.{t}"], f"s{s}.p4.w.{t}")
            ln_stats(f"s{s}.p4.b{t}", ub, f"s{s}.p4.w.{t}", so)
            P.act(lambda sc_, so=so: sc_.activation(statc[:, so + 9:so + 10],
                                                    statc[:, so + 7:so + 8], AF.Ln,
                                                    bias=eps5col[:, :], scale=1.0 / D),
                  [f"s{s}.p4.b{t}.var", "ms.e5"], f"s{s}.p4.lnb.{t}")
            P.act(lambda sc_, so=so: sc_.activation(statc[:, so + 8:so + 9],
                                                    statc[:, so + 9:so + 10], AF.Exp,
                                                    scale=-0.5), [f"s{s}.p4.lnb.{t}"],
                  f"s{s}.p4.rstdb.{t}")
            P.dve(lambda v, ub=ub, so=so, vn=vn: v.tensor_scalar(
                vn[:, :], ub[:, :], statc[:, so + 5:so + 6],
                statc[:, so + 8:so + 9], ALU.subtract, ALU.mult),
                [f"s{s}.p4.rstdb.{t}"], f"s{s}.p4.y1.{t}")
            wts = [f"s{s}.p4.y1.{t}"]
            if t >= 2:
                wts.append(f"s{s}.p4.tr.{t-2}.7")
            P.dve(lambda v, i=i, vn=vn: v.tensor_copy(y1bf[i][:, :], vn[:, :]), wts,
                  f"s{s}.p4.y1bf.{t}")
            P.gp(lambda g, vn=vn: g.tensor_tensor(vn[:, :], vn[:, :], b2b[:, :], ALU.add),
                 [f"s{s}.p4.y1bf.{t}", f"s{s}.b2bc"], f"s{s}.p4.y1b2.{t}")
            P.dma(SY, "y1st" if i == 0 else "y1s2",
                  lambda sy, t=t, vn=vn: sy.dma_start(y1_b.ap()[t], vn[:, :]),
                  [f"s{s}.p4.y1b2.{t}"], f"s{s}.p4.y1dma.{t}")
            if debug_outputs and s == 0:
                P.dma(SY, "dbg", lambda sy, t=t, vn=vn: sy.dma_start(dbg["y1"].ap()[t], vn[:, :]),
                      [f"s{s}.p4.y1b2.{t}"], f"dbg.y1.{t}")
            for ic in range(8):
                bank = PS[2][:, (ic % 2) * 512:(ic % 2) * 512 + 64].bitcast(BF16)
                wts = [f"s{s}.p4.y1bf.{t}"]
                if ic == 0 and t >= 1:
                    wts.append(f"s{s}.p4.ev.{t-1}.7")
                if ic >= 2:
                    wts.append(f"s{s}.p4.ev.{t}.{ic-2}")
                P.mm(lambda tns, i=i, ic=ic, bank=bank: tns.transpose(
                    bank, y1bf[i][:, ic * 128:(ic + 1) * 128], eyesb[:, :]),
                    wts, f"s{s}.p4.tr.{t}.{ic}")
                P.dve(lambda v, t=t, ic=ic, bank=bank: v.tensor_copy(
                    Y1T[ic][:, t * 128:(t + 1) * 128], bank),
                    [f"s{s}.p4.tr.{t}.{ic}"], f"s{s}.p4.ev.{t}.{ic}")

        stage_a_mm(0)
        stage_a_mm(1)
        stage_a(0)
        for t in range(1, NT):
            if t + 1 < NT:
                stage_a_mm(t + 1)
            stage_a(t)
            stage_b(t - 1)
        stage_b(NT - 1)
        P.marks[f"s{s}.y1all"] = ("y1st", P.cnt["y1st"])
        P.marks[f"s{s}.y1all2"] = ("y1s2", P.cnt["y1s2"])
        if debug_outputs and s == 0:
            P.dma(SY, "dbg", lambda sy: sy.dma_start(dbg["y1T0"].ap(), Y1T[0][:, :]),
                  [f"s{s}.p4.tr.{NT-1}.6", f"s{s}.p4.tr.{NT-1}.7"], "dbg.y1T0")
            P.dma(SY, "dbg", lambda sy: sy.dma_start(dbg["y1bf1"].ap(), y1bf[1][:, :]),
                  [f"s{s}.p4.tr.{NT-1}.6", f"s{s}.p4.tr.{NT-1}.7"], "dbg.y1bf1")

    def p5(s):
        base = s * S
        P.dma(GP, "w1", lambda g: g.dma_start(
            w1sb[6][:, :], w1_d.ap()[6 * 128:7 * 128, :]),
            [f"s{s}.p4.xpb.{NT-1}", f"s{s}.p4.y1b2.{NT-1}"], f"s{s}.w1.6")
        P.dma(GP, "w1", lambda g: g.dma_start(
            w1sb[7][:, :], w1_d.ap()[7 * 128:8 * 128, :]),
            [f"s{s}.p4.y1.{NT-1}"], f"s{s}.w1.7")
        w2gate = {0: [f"s{s}.p4.u.{NT-2}"], 2: [f"s{s}.p4.u.{NT-1}"],
                  4: [f"s{s}.p4.w.{NT-2}"], 6: [f"s{s}.p4.w.{NT-1}"],
                  8: [f"s{s}.p4.y1dma.{NT-1}"], 9: [f"s{s}.p4.y1dma.{NT-1}"],
                  10: [f"s{s}.p4.xpb.{NT-1}"], 11: [f"s{s}.p4.xpb.{NT-1}"],
                  12: [f"s{s}.p4.xpb.{NT-1}"], 13: [f"s{s}.p4.xpb.{NT-1}"],
                  14: [f"s{s}.p4.xpb.{NT-1}"], 15: [f"s{s}.p4.xpb.{NT-1}"],
                  16: [f"s{s}.p4.mm.{NT-1}"]}
        for i in list(range(16, 32)) + list(range(16)):
            P.dma(GP, "w2", lambda g, i=i: g.dma_start(
                w2sb[i][:, :], w2_d.ap()[i * 128:(i + 1) * 128, :]),
                w2gate.get(i, ()), f"s{s}.w2.{i}")
        for tq in range(NQ):
            for oc in range(32):
                pbank = oc % 2
                wts = []
                if tq > 0 or oc >= 2:
                    pt, po = (tq, oc - 2) if oc >= 2 else (tq - 1, 30 + oc)
                    wts.append(f"s{s}.f1ev.{pt}.{po}")
                if tq == 0 and oc == 0:
                    wts += [f"s{s}.p4.ev.{NT-1}.{ic2}" for ic2 in range(8)]
                    wts += [f"s{s}.p4.ev.{NT-2}.{ic2}" for ic2 in range(8)]
                for ic in range(8):
                    icw = list(wts) if ic == 0 else []
                    if tq == 0 and oc == 0:
                        icw.append(f"s{s}.w1.{ic}" if ic < 6 else f"s{s}.w1.7")
                    P.mm(lambda tns, tq=tq, oc=oc, ic=ic, pbank=pbank: tns.matmul(
                        PS[0][:, pbank * 512:(pbank + 1) * 512],
                        w1sb[ic][:, oc * 128:(oc + 1) * 128],
                        Y1T[ic][:, tq * 512:(tq + 1) * 512],
                        start=(ic == 0), stop=(ic == 7)),
                        icw,
                        f"s{s}.f1.{tq}.{oc}" if ic == 7 else None)
                if debug_outputs and s == 0 and tq == 0 and oc == 31:
                    P.dma(SY, "dbg", lambda sy: sy.dma_start(
                        dbg["y1T0b"].ap(), Y1T[0][:, 1920:2048]),
                        [f"s{s}.f1.{tq}.{oc}"], "dbg.y1T0b")
                wts = [f"s{s}.f1.{tq}.{oc}", "csts", "ms.z"]
                if tq == 0:
                    wts.append(f"s{s}.p4.y1dma.{NT-1}")
                if tq > 0:
                    wts.append(f"s{s}.f2.{tq * 4 - 1}")
                P.dve(lambda v, oc=oc, pbank=pbank: v.tensor_scalar(
                    hsb[oc][:, :], PS[0][:, pbank * 512:(pbank + 1) * 512],
                    b1col[:, oc:oc + 1], zerocol[:, 0:1], ALU.add, ALU.max),
                    wts, f"s{s}.f1ev.{tq}.{oc}")
            for tt in range(4):
                t = tq * 4 + tt
                ps2 = PS[3] if t % 2 == 0 else PS[2]
                wts = [f"s{s}.y1all", f"s{s}.y1all2"]
                if P.has(f"s{s}.p5.z.{t-1}"):
                    wts.append(f"s{s}.p5.z.{t-1}")
                if t == 0:
                    wts.append(f"s{s}.p4.n2.{NT-1}")
                P.dma(SC, "y1ld", lambda sc_, t=t: sc_.dma_start(y1b2t[:, :], y1_b.ap()[t]),
                      wts, f"s{s}.p5.y1.{t}")
                wts = [f"s{s}.f1ev.{tq}.31", f"s{s}.w2.31"]
                if P.has(f"s{s}.p5.z.{t-2}"):
                    wts.append(f"s{s}.p5.z.{t-2}")
                for wc in range(32):
                    for half in range(2):
                        P.mm(lambda tns, tt=tt, wc=wc, half=half, ps2=ps2: tns.matmul(
                            ps2[:, half * 512:(half + 1) * 512],
                            hsb[wc][:, tt * 128:(tt + 1) * 128],
                            w2sb[wc][:, half * 512:(half + 1) * 512],
                            start=(wc == 0), stop=(wc == 31)),
                            wts if (wc == 0 and half == 0) else (),
                            f"s{s}.f2.{t}" if (wc == 31 and half == 1) else None)
                P.dve(lambda v, ps2=ps2: v.tensor_tensor(out_st[1][:, :], ps2[:, :],
                                                         y1b2t[:, :], ALU.add),
                      [f"s{s}.f2.{t}", f"s{s}.p5.y1.{t}"], f"s{s}.p5.z.{t}")
                pref = f"s{s}.p5.c{t}"
                P.dve(lambda v: v.reduce_sum(statc[:, 0:1], out_st[1][:, :],
                                             mybir.AxisListType.X),
                      [f"s{s}.p5.z.{t}"], f"{pref}.s1")
                P.act(lambda sc_: sc_.activation(PS[1][:, :], out_st[1][:, :],
                                                 AF.Square, accum_out=statc[:, 2:3]),
                      [f"s{s}.p5.z.{t}"], f"{pref}.s2")
                P.dve(lambda v: v.tensor_scalar(statc[:, 5:6], statc[:, 0:1],
                                                1.0 / D, None, ALU.mult),
                      [f"{pref}.s1"], f"{pref}.m")
                P.dve(lambda v: v.tensor_tensor(statc[:, 1:2], statc[:, 0:1],
                                                statc[:, 0:1], ALU.mult),
                      [f"{pref}.s1"], f"{pref}.t1")
                P.dve(lambda v: v.scalar_tensor_tensor(
                    statc[:, 7:8], statc[:, 1:2], -1.0 / D, statc[:, 2:3],
                    ALU.mult, ALU.add), [f"{pref}.t1", f"{pref}.s2"], f"{pref}.var")
                P.act(lambda sc_: sc_.activation(statc[:, 9:10], statc[:, 7:8], AF.Ln,
                                                 bias=eps5col[:, :], scale=1.0 / D),
                      [f"{pref}.var"], f"{pref}.ln")
                P.act(lambda sc_: sc_.activation(statc[:, 8:9], statc[:, 9:10], AF.Exp,
                                                 scale=-0.5), [f"{pref}.ln"], f"{pref}.rstd")
                if debug_outputs and s == 0 and t == 15:
                    P.dma(SY, "dbg", lambda sy: sy.dma_start(dbg["z15"].ap(), out_st[1][:, :]),
                          [f"s{s}.p5.z.{t}"], "dbg.z15")
                    P.dma(SY, "dbg", lambda sy: sy.dma_start(dbg["yl15"].ap(), y1b2t[:, :]),
                          [f"s{s}.p5.z.{t}"], "dbg.yl15")
                    P.dma(SY, "dbg", lambda sy: sy.dma_start(dbg["h28"].ap(), hsb[28][:, :]),
                          [f"s{s}.p5.z.{t}"], "dbg.h28")
                wts = [f"{pref}.rstd"]
                if P.has(f"s{s}.out.{t-1}"):
                    wts.append(f"s{s}.out.{t-1}")
                P.dve(lambda v: v.tensor_scalar(out_st[0][:, :], out_st[1][:, :],
                                                statc[:, 5:6], statc[:, 8:9],
                                                ALU.subtract, ALU.mult),
                      wts, f"s{s}.p5.y2.{t}")
                P.dma(SY, "out", lambda sy, t=t: sy.dma_start(
                    out_d.ap()[base + t * 128:base + (t + 1) * 128, :], out_st[0][:, :]),
                    [f"s{s}.p5.y2.{t}"], f"s{s}.out.{t}")
        P.barrier_tagged(f"s{s}.end")

    p0()
    for s in range(SPC):
        p1(s)
        p23(s)
        p4(s)
        p5(s)

    with nc.Block() as block:
        def run(eng, lst, final_waits=()):
            for fn, waits, lane, amt in lst:
                for (wk, wv) in waits:
                    eng.wait_ge(sem[wk], wv)
                if fn is None:
                    continue
                instr = fn(eng)
                if amt:
                    instr.then_inc(sem[lane], amt)
            for (wk, wv) in final_waits:
                eng.wait_ge(sem[wk], wv)

        final = [(k, v) for k, v in P.cnt.items() if v > 0]

        @block.tensor
        def _(t):
            run(t, P.sched["tensor"])

        @block.scalar
        def _(s_):
            run(s_, P.sched["scalar"])

        @block.vector
        def _(v):
            run(v, P.sched["vector"])

        @block.sync
        def _(sy):
            run(sy, P.sched["sync"], final)

        @block.gpsimd
        def _(g):
            run(g, P.sched["gpsimd"])

    return nc


_cached = {}


def kernel(L0_attn_x, L0_attn_rby, L0_attn_Wq, L0_attn_bq, L0_attn_Wk,
           L0_attn_bk, L0_attn_Wv, L0_attn_bv, L0_attn_Wo, L0_attn_bo,
           L0_attn_ln_gamma, L0_attn_ln_beta, L0_ffn_w1, L0_ffn_b1,
           L0_ffn_w2, L0_ffn_b2):
    x = np.ascontiguousarray(np.asarray(L0_attn_x, dtype=np.float32))

    j = np.arange(512, dtype=np.float64)
    invd = np.repeat(np.power(10000.0, -j / 256.0), 2).astype(np.float32)[None, :]
    phase = np.zeros((1, D), np.float32)
    phase[0, 1::2] = np.float32(np.pi / 2)
    pos = (np.arange(128, dtype=np.float32)[:, None]
           + 128.0 * np.arange(NT, dtype=np.float32)[None, :])

    def col128(v, ncols):
        return np.ascontiguousarray(np.asarray(v, np.float32).reshape(ncols, 128).T)

    common = dict(
        rby=np.asarray(L0_attn_rby, np.float32).reshape(1, 3),
        wqT=np.ascontiguousarray(np.asarray(L0_attn_Wq, np.float32).T),
        wkT=np.ascontiguousarray(np.asarray(L0_attn_Wk, np.float32).T),
        wvT=np.ascontiguousarray(np.asarray(L0_attn_Wv, np.float32).T),
        woT=np.ascontiguousarray(np.asarray(L0_attn_Wo, np.float32).T),
        w1=np.ascontiguousarray(np.asarray(L0_ffn_w1, np.float32)),
        w2=np.ascontiguousarray(np.asarray(L0_ffn_w2, np.float32)),
        bv=np.asarray(L0_attn_bv, np.float32).reshape(1, D),
        bo=np.asarray(L0_attn_bo, np.float32).reshape(1, D),
        b2=np.asarray(L0_ffn_b2, np.float32).reshape(1, D),
        gamma=np.asarray(L0_attn_ln_gamma, np.float32).reshape(1, D),
        beta=np.asarray(L0_attn_ln_beta, np.float32).reshape(1, D),
        bqc=col128(L0_attn_bq, 8),
        bkc=col128(L0_attn_bk, 8),
        b1c=col128(L0_ffn_b1, 32),
        pos=np.ascontiguousarray(pos), invd=invd, phase=phase,
        eye=np.eye(128, dtype=np.float32).astype(ml_dtypes.bfloat16),
    )
    in_maps = []
    for c in range(NCORES):
        m = dict(common)
        m["x0"] = np.ascontiguousarray(x[c * SPC:(c + 1) * SPC].reshape(SPC * S, D))
        in_maps.append(m)

    if "nc" not in _cached:
        _cached["nc"] = build_program()
    nc = _cached["nc"]
    res = run_bass_kernel_spmd(nc, in_maps, core_ids=list(range(NCORES)))
    out = np.concatenate(
        [np.asarray(r["out"]).reshape(SPC, S, D) for r in res.results], axis=0)
    return out.astype(np.float32)
